# version 13
# baseline (speedup 1.0000x reference)
import numpy as np
import ml_dtypes

import concourse.bass as bass
import concourse.mybir as mybir
from concourse.bass_utils import run_bass_kernel_spmd

F32 = mybir.dt.float32
BF16 = mybir.dt.bfloat16
AF = mybir.ActivationFunctionType
ALU = mybir.AluOpType
AX = mybir.AxisListType
bfdt = ml_dtypes.bfloat16

D = 2048
L = 4096
NCORES = 8
TA = 512
NTA = L // TA
TB = 1024
NTB = L // TB
NCH = 8
PROJ_T = 41
EPS_LN = 1e-5
EPS_RMS = 1e-5

_E = [0, 1, 2, 3, 8, 9, 10, 11, 16, 17, 18, 19, 24, 25, 26, 27]
_O = [e + 4 for e in _E]
SLOT2ORIG = [0] * 32
for j in range(16):
    SLOT2ORIG[2 * j] = _E[j]
    SLOT2ORIG[2 * j + 1] = _O[j]
G_OF = [SLOT2ORIG[s] // 4 for s in range(32)]
JJ_OF = [SLOT2ORIG[s] % 4 for s in range(32)]
SLOTS_OF_G = [[(8 * (g // 2) + 2 * jj + (g % 2)) for jj in range(4)] for g in range(8)]


class _Op:
    __slots__ = ('sem', 'emit', 'engine', 'dma', 'waits', 'inc', 'value',
                 'clock', 'seq')

    def __init__(self, sem, engine, emit, dma, seq):
        self.sem = sem
        self.engine = engine
        self.emit = emit
        self.dma = dma
        self.seq = seq
        self.waits = []
        self.inc = bool(dma)
        self.value = None
        self.clock = None


_ENGINES = ('T', 'V', 'S', 'P', 'Q')


def _dma_sem_key(regions):
    r = regions[0] if regions else 'MISC'
    for p in ('x1.', 'x2.', 'yg.', 'yout.'):
        if r.startswith(p):
            return 'CH' + r.split('.')[1]
    return r


class BB:

    def __init__(self, nc):
        self.nc = nc
        self.queues = {e: [] for e in _ENGINES}
        self.regions = {}
        self.counts = {}
        self._known = {e: {} for e in _ENGINES}

    def _mk(self, engine, emit, dma, reads, writes):
        if dma:
            key = _dma_sem_key(list(writes) or list(reads))
            key = 'D_' + key
        else:
            key = engine
        units = 16 if dma else 1
        cnt = self.counts.get(key, 0) + units
        self.counts[key] = cnt
        o = _Op(key, engine, emit, dma, cnt)
        return o

    def op(self, engine, emit, reads=(), writes=(), dma=False):
        o = self._mk(engine, emit, dma, reads, writes)
        known = self._known[engine]

        rl, wl = [], list(writes)
        for r in reads:
            (wl if r.startswith('PS') else rl).append(r)

        deps = []
        for r in rl:
            ent = self.regions.get(r)
            if ent is not None and ent[0] is not None:
                deps.append(ent[0])
        for w in wl:
            ent = self.regions.get(w)
            if ent is not None:
                if ent[0] is not None:
                    deps.append(ent[0])
                deps.extend(ent[1])

        for d in deps:
            if d.sem == o.sem and not d.dma:
                if engine == 'T' and d.engine == 'T':
                    continue
            if d.seq <= known.get(d.sem, -1):
                continue
            o.waits.append(d)
            d.inc = True
            if d.clock is not None:
                for k2, v in d.clock.items():
                    if v > known.get(k2, -1):
                        known[k2] = v
            if d.seq > known.get(d.sem, -1):
                known[d.sem] = d.seq

        o.clock = dict(known)
        o.clock[o.sem] = max(o.clock.get(o.sem, -1), o.seq)

        for r in rl:
            ent = self.regions.setdefault(r, [None, []])
            ent[1].append(o)
        for w in wl:
            self.regions[w] = [o, []]
        self.queues[engine].append(o)
        return o

    def fence(self, engine, deps):
        o = self._mk(engine, None, False, (), ())
        o.inc = False
        known = self._known[engine]
        for d in deps:
            if d.seq <= known.get(d.sem, -1):
                continue
            o.waits.append(d)
            d.inc = True
            if d.seq > known.get(d.sem, -1):
                known[d.sem] = d.seq
        o.clock = dict(known)
        self.queues[engine].append(o)
        return o

    def build(self):
        nc = self.nc
        ccount = {}
        for e in _ENGINES:
            for o in self.queues[e]:
                if o.dma:
                    if o.inc:
                        o.value = o.seq
                else:
                    if o.inc:
                        c = ccount.get(o.sem, 0) + 1
                        ccount[o.sem] = c
                        o.value = c
        import contextlib
        used = set()
        for e in _ENGINES:
            for o in self.queues[e]:
                for d in o.waits:
                    used.add(d.sem)
                if o.inc:
                    used.add(o.sem)
        with contextlib.ExitStack() as st:
            sems = {}
            for k in sorted(used):
                sems[k] = st.enter_context(nc.semaphore('sm_' + k.replace('.', '_')))
            st.enter_context(_BlockRunner(nc, self, sems))

    def _wait_val(self, d):
        return d.value if not d.dma else d.seq


class _BlockRunner:
    def __init__(self, nc, bb, sems):
        self.nc = nc
        self.bb = bb
        self.sems = sems

    def __enter__(self):
        nc, bb, sems = self.nc, self.bb, self.sems
        self.block = nc.Block()
        block = self.block.__enter__()

        def run_queue(h, e):
            for o in bb.queues[e]:
                for d in o.waits:
                    h.wait_ge(sems[d.sem], d.value if not d.dma else d.seq)
                if o.emit is not None:
                    ins = o.emit(h)
                    if o.inc:
                        ins.then_inc(sems[o.sem], 16 if o.dma else 1)

        if bb.queues['T']:
            block.tensor(lambda h: run_queue(h, 'T'))
        if bb.queues['V']:
            block.vector(lambda h: run_queue(h, 'V'))
        if bb.queues['S']:
            block.scalar(lambda h: run_queue(h, 'S'))
        if bb.queues['P']:
            block.gpsimd(lambda h: run_queue(h, 'P'))
        if bb.queues['Q']:
            block.sync(lambda h: run_queue(h, 'Q'))
        return self

    def __exit__(self, *a):
        return self.block.__exit__(*a)


def _tile_images(W):
    F, K = W.shape
    nt = F // 128
    A = W.reshape(nt, 128, K // 128, 128)
    return np.ascontiguousarray(A.transpose(0, 3, 2, 1)).reshape(nt, 128, K)


def _x_chunks(xb):
    t = xb.reshape(NCH, 512, 16, 128)
    return np.ascontiguousarray(t.transpose(0, 3, 2, 1)).reshape(NCH, 128, 16 * 512)


def _out_unchunks(y):
    t = y.reshape(NCH, 128, 16, 512)
    return np.ascontiguousarray(t.transpose(0, 3, 2, 1)).reshape(L, D)


def _host_prep(inp):
    W = np.asarray(inp['L0_attn_W_vkqgdt'])
    conv_w = np.asarray(inp['L0_attn_conv_w'])
    conv_b = np.asarray(inp['L0_attn_conv_b'])
    dt_bias = np.asarray(inp['L0_attn_dt_bias'])
    A_log = np.asarray(inp['L0_attn_A_log_bias'])
    gnorm = np.asarray(inp['L0_attn_gnorm_w'])
    Wo = np.asarray(inp['L0_attn_W_o'])
    Wg = np.asarray(inp['L0_ffn_Wg'])
    W0a = np.asarray(inp['L0_ffn_W0a'])
    b0a = np.asarray(inp['L0_ffn_b0a'])
    W0b = np.asarray(inp['L0_ffn_W0b'])
    b0b = np.asarray(inp['L0_ffn_b0b'])
    W1a = np.asarray(inp['L0_ffn_W1a'])
    b1a = np.asarray(inp['L0_ffn_b1a'])
    W1b = np.asarray(inp['L0_ffn_W1b'])
    b1b = np.asarray(inp['L0_ffn_b1b'])

    qperm = np.array([SLOT2ORIG[s] * 64 + p for s in range(32) for p in range(64)])
    Wp = np.zeros((5248, D), np.float32)
    Wp[0:1024] = W[0:1024]
    Wp[1024:3072] = W[1024 + qperm]
    Wp[3072:5120] = W[3072 + qperm]
    Wp[5120:5152] = W[5120 + np.array(SLOT2ORIG)]
    wvkq = _tile_images(Wp)

    W0a_ext = np.zeros((33 * 128, D), np.float32)
    W0a_ext[0:4096] = W0a
    W0a_ext[4096] = Wg[0]
    W0a_ext[4160] = Wg[1]
    w0a = _tile_images(W0a_ext)
    w1a = _tile_images(W1a)
    w0b = _tile_images(W0b)
    w1b = _tile_images(W1b)
    perm_din = np.array([SLOT2ORIG[s] * 64 + p for s in range(32) for p in range(64)])
    wo = _tile_images(np.ascontiguousarray(Wo[:, perm_din]))

    cperm = np.concatenate([np.arange(1024), 1024 + qperm])
    cw = conv_w[cperm]
    cb = conv_b[cperm]
    sm = np.zeros((128, 176), np.float32)
    sm[:, 0:24] = cw[:, 0].reshape(24, 128).T
    sm[:, 24:48] = cw[:, 1].reshape(24, 128).T
    sm[:, 48:72] = cb.reshape(24, 128).T
    sm[:, 72:104] = b0a.reshape(32, 128).T
    sm[:, 105:137] = b1a.reshape(32, 128).T
    sm[:, 137:153] = b0b.reshape(16, 128).T
    sm[:, 153:169] = b1b.reshape(16, 128).T
    sm[0:32, 169] = dt_bias[SLOT2ORIG]
    sm[0:32, 170] = A_log[SLOT2ORIG]
    sm[:, 171] = np.tile(gnorm, 2)
    sm[:, 172] = 1e-5
    sm[:, 173] = -0.5
    sm[:, 174] = 1.0 / 64

    consts = dict(
        c_idb=np.eye(128, dtype=bfdt),
        c_idf=np.eye(128, dtype=np.float32),
        c_mask=np.triu(np.ones((128, 128), np.float32)).astype(bfdt),
        c_onesb=np.ones((128, 128), bfdt),
        c_onesf=np.ones((128, 128), np.float32),
        c_smalls=sm,
    )
    weights = dict(wvkq_f32=wvkq, w0a_f32=w0a, w1a_f32=w1a,
                   w0b_f32=w0b, w1b_f32=w1b, wo_f32=wo)
    return consts, weights


def _build_nc(debug=False):
    nc = bass.Bass("TRN2", target_bir_lowering=False)
    h = {}
    h['x'] = nc.dram_tensor('x', [NCH, 128, 8192], F32, kind="ExternalInput").ap()
    for n, sh in [('wvkq_f32', [41, 128, 2048]), ('w0a_f32', [33, 128, 2048]),
                  ('w1a_f32', [32, 128, 2048]), ('w0b_f32', [16, 128, 4096]),
                  ('w1b_f32', [16, 128, 4096]), ('wo_f32', [16, 128, 2048])]:
        h[n] = nc.dram_tensor(n, sh, F32, kind="ExternalInput").ap()
    for n, sh, dt in [('c_idb', [128, 128], BF16), ('c_idf', [128, 128], F32),
                      ('c_mask', [128, 128], BF16), ('c_onesb', [128, 128], BF16),
                      ('c_onesf', [128, 128], F32), ('c_smalls', [128, 176], F32)]:
        h[n] = nc.dram_tensor(n, sh, dt, kind="ExternalInput").ap()
    h['yout'] = nc.dram_tensor('yout', [NCH, 128, 8192], F32, kind="ExternalOutput").ap()
    dbg_kind = "ExternalOutput" if debug else "Internal"
    for n, sh, dt in [('wvkq_bf', [41, 128, 2048], BF16), ('w0a_bf', [33, 128, 2048], BF16),
                      ('w1a_bf', [32, 128, 2048], BF16), ('w0b_bf', [16, 128, 4096], BF16),
                      ('w1b_bf', [16, 128, 4096], BF16), ('wo_bf', [16, 128, 2048], BF16)]:
        h[n] = nc.dram_tensor(n, sh, dt, kind="Internal").ap()
    for n, sh, dt in [('x1', [NCH, 128, 8192], F32), ('x2', [NCH, 128, 8192], F32),
                      ('yg', [NCH, 128, 8192], BF16)]:
        h[n] = nc.dram_tensor(n, sh, dt, kind=dbg_kind).ap()

    import contextlib
    with contextlib.ExitStack() as stk:
        E = stk.enter_context
        sb = {}

        def S_(name, shape, dt):
            sb[name] = E(nc.sbuf_tensor('sb_' + name, list(shape), dt))
            return sb[name]

        for n, sh, dt in [('c_idb', (128, 128), BF16), ('c_idf', (128, 128), F32),
                          ('c_mask', (128, 128), BF16), ('c_onesb', (128, 128), BF16),
                          ('c_onesf', (128, 128), F32), ('c_smalls', (128, 176), F32)]:
            S_(n, sh, dt)
        S_('d_bd', (128, 16), F32)
        S_('acol', (32, 1), F32)
        S_('carry', (128, 24), F32)
        S_('XN', (128, 16384), BF16)
        S_('BIG', (128, 32768), BF16)
        S_('XP0', (128, 512), F32)
        S_('XP1', (128, 512), F32)
        S_('WT0', (128, 4096), BF16)
        S_('WT1', (128, 4096), BF16)
        S_('SCRA', (128, 8192), BF16)
        S_('dtfm', (32, 1024), F32)
        S_('DTa', (32, 512), F32)
        S_('DTb', (32, 512), F32)
        S_('AFM', (32, 128), F32)
        S_('ACSFM', (32, 128), F32)
        S_('DSFM', (32, 128), F32)
        S_('dt_tm', (128, 32), F32)
        S_('acs_tm', (128, 32), F32)
        S_('ds_tm', (128, 32), F32)
        S_('FTM', (128, 32), F32)
        S_('ETM', (128, 32), F32)
        S_('EASB', (128, 32), F32)
        S_('U32', (128, 32), F32)
        S_('T32', (128, 32), F32)
        S_('W32', (128, 32), F32)
        S_('MS', (128, 32), F32)
        S_('ASR', (1, 32), F32)
        S_('KTM', (128, 512), BF16)
        S_('VTM', (128, 512), BF16)
        S_('YNF', (128, 2048), BF16)
        S_('STT', (128, 512), BF16)
        S_('XV', (128, 2048), BF16)
        S_('XVD', (128, 2048), BF16)
        S_('HS0', (128, 1024), F32)
        S_('HS1', (128, 1024), F32)
        S_('HSBF', (128, 1024), BF16)
        S_('YG0', (128, 2048), BF16)
        S_('YG1', (128, 2048), BF16)
        S_('E0S', (128, 512), BF16)
        S_('FS1', (128, 512), F32)
        S_('XNEW', (128, 512), F32)
        S_('XNEW2', (128, 512), F32)
        S_('Bb1', (128, 512), BF16)
        S_('Bb0', (128, 512), BF16)
        S_('x2b0', (128, 512), BF16)
        S_('x2bA', (128, 512), BF16)
        S_('x2bB', (128, 512), BF16)
        S_('ROWS', (1, 1024), F32)
        S_('ZR', (1, 1024), BF16)
        S_('INVM', (1, 512), BF16)
        S_('B1R', (1, 512), BF16)
        S_('B0R', (1, 512), BF16)
        PB = [E(nc.psum_tensor(f'PB{i}', [128, 512], F32)) for i in range(6)]
        PBT = E(nc.psum_tensor('PBT', [128, 1024], BF16))
        PB7 = E(nc.psum_tensor('PB7', [128, 512], F32))
        PR = [f'PS{i}' for i in range(6)]
        PRT, PR7 = 'PST', 'PS7'

        bb = BB(nc)
        sm = sb['c_smalls']

        for n in ('c_idb', 'c_idf', 'c_mask', 'c_onesb', 'c_onesf', 'c_smalls'):
            bb.op('Q', lambda q, n=n: q.dma_start(sb[n][:], h[n][:]),
                  writes=[n], dma=True)
        bb.op('V', lambda v: v.tensor_sub(sb['d_bd'][:], sm[:, 153:169], sm[:, 137:153]),
              reads=['c_smalls'], writes=['d_bd'])
        bb.op('S', lambda s: s.activation(sb['acol'][:], sm[0:32, 170:171], AF.Exp),
              reads=['c_smalls'], writes=['acol'])
        bb.op('V', lambda v: v.tensor_scalar_mul(sb['acol'][:], sb['acol'][:], -1.0),
              reads=['acol'], writes=['acol'])
        bb.op('V', lambda v: v.memset(sb['carry'][:], 0.0), writes=['carry'])
        bb.op('V', lambda v: v.memset(sb['HS0'][:], 0.0), writes=['HS0'])
        bb.op('V', lambda v: v.memset(sb['HSBF'][:], 0.0), writes=['HSBF'])

        def dmacast(n, nt):
            for j in range(nt):
                bb.op('P', lambda g, n=n, j=j: g.dma_start(h[n + '_bf'][j], h[n + '_f32'][j]),
                      writes=[n + '_bf'], dma=True)
        dmacast('w0a', 33)
        dmacast('w0b', 16)
        dmacast('w1a', 32)
        dmacast('w1b', 16)
        late_casts = [('wvkq', j) for j in range(41)] + [('wo', j) for j in range(16)]

        def emit_late_casts(k):
            while late_casts and k > 0:
                n, j = late_casts.pop(0)
                bb.op('P', lambda g, n=n, j=j: g.dma_start(h[n + '_bf'][j], h[n + '_f32'][j]),
                      writes=[n + '_bf'], dma=True)
                k -= 1

        def emit_ln(i2, src_name, xn_off_fn):
            for s in range(16):
                xp = sb['XP0'] if s % 2 == 0 else sb['XP1']
                xpr = 'XP0' if s % 2 == 0 else 'XP1'
                bb.op('Q', lambda q, s=s, xp=xp: q.dma_start(
                    xp[:], h[src_name][i2, :, s * 512:(s + 1) * 512]),
                    reads=[f'{src_name}.{i2}'], writes=[xpr], dma=True)
                bb.op('V', lambda v, s=s, xp=xp: v.tensor_copy(
                    sb['SCRA'][:, s * 512:(s + 1) * 512], xp[:]),
                    reads=[xpr], writes=['SCRA'])
                bb.op('T', lambda t, s=s: t.matmul(
                    PB7[0:1, 0:512], sb['c_onesb'][:, 0:1],
                    sb['SCRA'][:, s * 512:(s + 1) * 512],
                    start=(s == 0), stop=(s == 15)),
                    reads=['SCRA', 'c_onesb'], writes=[PR7])
            x2bufs = [('x2b0', sb['x2b0']), ('E0S', sb['E0S']),
                      ('x2bA', sb['x2bA']), ('x2bB', sb['x2bB'])]
            for s in range(16):
                x2r, x2b = x2bufs[s % 4]
                bb.op('V', lambda v, s=s, x2b=x2b: v.tensor_mul(
                    x2b[:], sb['SCRA'][:, s * 512:(s + 1) * 512],
                    sb['SCRA'][:, s * 512:(s + 1) * 512]),
                    reads=['SCRA'], writes=[x2r])
                bb.op('T', lambda t, s=s, x2b=x2b: t.matmul(
                    PB[5][0:1, 0:512], sb['c_onesb'][:, 0:1], x2b[:],
                    start=(s == 0), stop=(s == 15)),
                    reads=[x2r, 'c_onesb'], writes=[PR[5]])
            R = sb['ROWS']
            bb.op('V', lambda v: v.tensor_scalar_mul(R[0:1, 0:512], PB7[0:1, 0:512], 1.0 / D),
                  reads=[PR7], writes=['ROWS'])
            bb.op('S', lambda s: s.activation(R[0:1, 512:1024], R[0:1, 0:512], AF.Square),
                  reads=['ROWS'], writes=['ROWS'])
            bb.op('V', lambda v: v.scalar_tensor_tensor(
                PB7[32:33, 0:512], PB[5][0:1, 0:512], 1.0 / D, R[0:1, 512:1024],
                ALU.mult, ALU.subtract),
                reads=[PR[5], 'ROWS'], writes=[PR7])
            bb.op('S', lambda s: s.activation(PB[5][32:33, 0:512], PB7[32:33, 0:512], AF.Ln,
                                              bias=sm[0:1, 172:173]),
                  reads=[PR7, 'c_smalls'], writes=[PR[5]])
            bb.op('S', lambda s: s.activation(PB7[96:97, 0:512], PB[5][32:33, 0:512], AF.Exp,
                                              scale=sm[0:1, 173:174]),
                  reads=[PR[5], 'c_smalls'], writes=[PR7])
            bb.op('V', lambda v: v.tensor_copy(sb['B1R'][:], PB7[96:97, 0:512]),
                  reads=[PR7], writes=['B1R'])
            bb.op('V', lambda v: v.scalar_tensor_tensor(
                sb['B0R'][:], R[0:1, 0:512], -1.0, PB7[96:97, 0:512], ALU.mult, ALU.mult),
                reads=['ROWS', PR7], writes=['B0R'])
            bb.op('T', lambda t: t.matmul(PB[5][:, 0:512], sb['c_onesb'][0:1, :], sb['B1R'][:],
                                          start=True, stop=True),
                  reads=['B1R', 'c_onesb'], writes=[PR[5]])
            bb.op('S', lambda s: s.copy(sb['Bb1'][:], PB[5][:, 0:512]),
                  reads=[PR[5]], writes=['Bb1'])
            bb.op('T', lambda t: t.matmul(PB[5][:, 0:512], sb['c_onesb'][0:1, :], sb['B0R'][:],
                                          start=True, stop=True),
                  reads=['B0R', 'c_onesb'], writes=[PR[5]])
            bb.op('S', lambda s: s.copy(sb['Bb0'][:], PB[5][:, 0:512]),
                  reads=[PR[5]], writes=['Bb0'])
            bv1 = bass.AP(sb['Bb1'], 0, [[512, 128], [0, 16], [1, 512]])
            bv0 = bass.AP(sb['Bb0'], 0, [[512, 128], [0, 16], [1, 512]])
            bb.op('V', lambda v: v.tensor_mul(sb['SCRA'][:], sb['SCRA'][:], bv1),
                  reads=['SCRA', 'Bb1'], writes=['SCRA'])
            offs = [xn_off_fn(s) for s in range(16)]
            stride = offs[1] - offs[0]
            out_ap = bass.AP(sb['XN'], offs[0], [[16384, 128], [stride, 16], [1, 512]])
            bb.op('V', lambda v: v.tensor_add(out_ap, sb['SCRA'][:], bv0),
                  reads=['SCRA', 'Bb0'], writes=['XN'])

        def stream_mm_phase(wname, ntiles, K, rhs_fn, out_fn, nchunks=1,
                            rhs_region='XN'):
            ks = K // 128
            for j in range(min(2, ntiles)):
                wt = 'WT0' if j % 2 == 0 else 'WT1'
                bb.op('Q', lambda q, j=j, wt=wt: q.dma_start(
                    sb[wt][:, 0:K], h[wname][j]),
                    reads=[wname], writes=[wt], dma=True)
            for j in range(ntiles):
                wt = 'WT0' if j % 2 == 0 else 'WT1'
                for ch in range(nchunks):
                    bk = (j * nchunks + ch) % 2
                    for s in range(ks):
                        bb.op('T', lambda t, j=j, ch=ch, s=s, wt=wt, bk=bk: t.matmul(
                            PB[bk][:, 0:512],
                            sb[wt][:, s * 128:(s + 1) * 128],
                            rhs_fn(s, ch),
                            start=(s == 0), stop=(s == ks - 1)),
                            reads=[wt, rhs_region],
                            writes=[PR[bk]])
                    out_fn(j, ch, bk)
                if j + 2 < ntiles:
                    wt2 = 'WT0' if j % 2 == 0 else 'WT1'
                    bb.op('Q', lambda q, j=j, wt2=wt2: q.dma_start(
                        sb[wt2][:, 0:K], h[wname][j + 2]),
                        reads=[wname], writes=[wt2], dma=True)

        def ffn_phase(src_name, dst_name, is_out):
            for ia in range(NTA):
                emit_ln(ia, src_name, lambda s: s * 512)
                def h0_out(j, ch, bk):
                    if j < 32:
                        bb.op('S', lambda s, j=j, bk=bk: s.activation(
                            sb['BIG'][:, j * 512:(j + 1) * 512], PB[bk][:, 0:512],
                            AF.Gelu, bias=sm[:, 72 + j:73 + j]),
                            reads=[PR[bk], 'c_smalls'], writes=['BIG'])
                    else:
                        bb.op('S', lambda s, bk=bk: s.copy(sb['ZR'][0:1, 0:512], PB[bk][0:1, 0:512]),
                              reads=[PR[bk]], writes=['ZR'])
                        bb.op('S', lambda s, bk=bk: s.copy(sb['ZR'][0:1, 512:1024], PB[bk][64:65, 0:512]),
                              reads=[PR[bk]], writes=['ZR'])
                        bb.op('V', lambda v: v.tensor_sub(PB7[32:33, 0:512], sb['ZR'][0:1, 0:512],
                                                          sb['ZR'][0:1, 512:1024]),
                              reads=['ZR'], writes=[PR7])
                        bb.op('V', lambda v: v.tensor_single_scalar(sb['INVM'][:], PB7[32:33, 0:512],
                                                                    0.0, ALU.is_lt),
                              reads=[PR7], writes=['INVM'])
                        bb.op('T', lambda t: t.matmul(PB7[:, 0:512], sb['c_onesb'][0:1, :],
                                                      sb['INVM'][:], start=True, stop=True),
                              reads=['INVM', 'c_onesb'], writes=[PR7])
                stream_mm_phase('w0a_bf', 33, 2048,
                                lambda s, ch: sb['XN'][:, s * 512:(s + 1) * 512], h0_out)
                def out0_out(j, ch, bk):
                    bb.op('S', lambda s, j=j, bk=bk: s.activation(
                        sb['SCRA'][:, j * 512:(j + 1) * 512], PB[bk][:, 0:512],
                        AF.Identity, bias=sm[:, 137 + j:138 + j]),
                        reads=[PR[bk], 'c_smalls'], writes=['SCRA'])
                stream_mm_phase('w0b_bf', 16, 4096,
                                lambda s, ch: sb['BIG'][:, s * 512:(s + 1) * 512], out0_out,
                                rhs_region='BIG')
                def h1_out(j, ch, bk):
                    bb.op('S', lambda s, j=j, bk=bk: s.activation(
                        sb['BIG'][:, (32 + j) * 512:(33 + j) * 512], PB[bk][:, 0:512],
                        AF.Relu, bias=sm[:, 105 + j:106 + j]),
                        reads=[PR[bk], 'c_smalls'], writes=['BIG'])
                stream_mm_phase('w1a_bf', 32, 2048,
                                lambda s, ch: sb['XN'][:, s * 512:(s + 1) * 512], h1_out)
                store_ops = []

                def out1_out(j, ch, bk):
                    e0 = sb['SCRA'][:, j * 512:(j + 1) * 512]
                    bb.op('V', lambda v, j=j, bk=bk, e0=e0: v.scalar_tensor_tensor(
                        sb['FS1'][:], PB[bk][:, 0:512], sb['d_bd'][:, j:j + 1], e0,
                        ALU.add, ALU.subtract),
                        reads=[PR[bk], 'd_bd', 'SCRA'], writes=['FS1'])
                    bb.op('V', lambda v: v.tensor_mul(sb['FS1'][:], sb['FS1'][:], PB7[:, 0:512]),
                          reads=['FS1', PR7], writes=['FS1'])
                    bb.op('P', lambda p, e0=e0: p.tensor_add(sb['FS1'][:], sb['FS1'][:], e0),
                          reads=['FS1', 'SCRA'], writes=['FS1'])
                    xp = sb['XP0'] if j % 2 == 0 else sb['XP1']
                    xpr = 'XP0' if j % 2 == 0 else 'XP1'
                    bb.op('Q', lambda q, j=j, xp=xp, ia=ia: q.dma_start(
                        xp[:], h[src_name][ia, :, j * 512:(j + 1) * 512]),
                        reads=[f'{src_name}.{ia}'], writes=[xpr], dma=True)
                    xn2 = sb['XNEW'] if j % 2 == 0 else sb['XNEW2']
                    xr = 'XNEW' if j % 2 == 0 else 'XNEW2'
                    bb.op('V', lambda v, xn2=xn2, xp=xp: v.scalar_tensor_tensor(
                        xn2[:], sb['FS1'][:], 0.5, xp[:], ALU.mult, ALU.add),
                        reads=['FS1', xpr], writes=[xr])
                    store_ops.append(bb.op('S', lambda s, j=j, xn2=xn2, ia=ia: s.dma_start(
                        h[dst_name][ia, :, j * 512:(j + 1) * 512], xn2[:]),
                        reads=[xr], writes=[f'{dst_name}.{ia}'], dma=True))
                stream_mm_phase('w1b_bf', 16, 4096,
                                lambda s, ch: sb['BIG'][:, (32 + s) * 512:(33 + s) * 512], out1_out,
                                rhs_region='BIG')
                if dst_name == 'x1':
                    emit_late_casts(8)
            return store_ops

        def b1_phase():
            gchunk = [0]
            for tb in range(NTB):
                for half in range(2):
                    emit_ln(tb * 2 + half, 'x1', lambda s, half=half: s * 1024 + half * 512)
                def proj_out(j, ch, bk):
                    col = j * 1024 + ch * 512
                    if j < 24:
                        ct = sb['E0S']
                        bb.op('S', lambda s, j=j, bk=bk, ct=ct: s.activation(
                            ct[:], PB[bk][:, 0:512], AF.Identity,
                            bias=sm[:, 48 + j:49 + j], scale=sm[:, 24 + j:25 + j]),
                            reads=[PR[bk], 'c_smalls'], writes=['E0S'])
                        bb.op('V', lambda v, j=j, bk=bk, ct=ct: v.scalar_tensor_tensor(
                            ct[:, 1:512], PB[bk][:, 0:511], sm[:, j:j + 1], ct[:, 1:512],
                            ALU.mult, ALU.add),
                            reads=[PR[bk], 'c_smalls', 'E0S'], writes=['E0S'])
                        bb.op('V', lambda v, j=j, ct=ct: v.scalar_tensor_tensor(
                            ct[:, 0:1], sb['carry'][:, j:j + 1], sm[:, j:j + 1], ct[:, 0:1],
                            ALU.mult, ALU.add),
                            reads=['carry', 'c_smalls', 'E0S'], writes=['E0S'])
                        bb.op('S', lambda s, j=j, bk=bk: s.copy(
                            sb['carry'][:, j:j + 1], PB[bk][:, 511:512]),
                            reads=[PR[bk]], writes=['carry'])
                        bb.op('S', lambda s, j=j, ch=ch, ct=ct, col=col: s.activation(
                            sb['BIG'][:, col:col + 512], ct[:], AF.Silu),
                            reads=['E0S'], writes=[f'BG{ch}'])
                    elif j < 40:
                        gi = j - 24
                        if gi < 8:
                            dst = sb['BIG'][:, 24576 + gi * 1024 + ch * 512: 24576 + gi * 1024 + ch * 512 + 512]
                            dr = f'BG{ch}'
                        else:
                            dst = sb['SCRA'][:, (gi - 8) * 1024 + ch * 512: (gi - 8) * 1024 + ch * 512 + 512]
                            dr = 'SCRA'
                        bb.op('S', lambda s, bk=bk, dst=dst: s.activation(
                            dst, PB[bk][:, 0:512], AF.Silu),
                            reads=[PR[bk]], writes=[dr])
                    else:
                        bia = sm[0:32, 169:170]
                        bb.op('S', lambda s, bk=bk: s.activation(
                            sb['DTa'][:], PB[bk][0:32, 0:512], AF.Abs, bias=bia),
                            reads=[PR[bk], 'c_smalls'], writes=['DTa'])
                        bb.op('S', lambda s, bk=bk: s.activation(
                            sb['DTb'][:], PB[bk][0:32, 0:512], AF.Relu, bias=bia),
                            reads=[PR[bk], 'c_smalls'], writes=['DTb'])
                        bb.op('S', lambda s: s.activation(
                            sb['DTa'][:], sb['DTa'][:], AF.Exp, scale=-1.0),
                            reads=['DTa'], writes=['DTa'])
                        bb.op('S', lambda s: s.activation(
                            sb['DTa'][:], sb['DTa'][:], AF.Ln, bias=1.0),
                            reads=['DTa'], writes=['DTa'])
                        bb.op('V', lambda v, ch=ch: v.tensor_add(
                            sb['dtfm'][:, ch * 512:(ch + 1) * 512], sb['DTa'][:], sb['DTb'][:]),
                            reads=['DTa', 'DTb'], writes=[f'DT{ch}'])
                for half in range(2):
                    stream_mm_phase('wvkq_bf', 41, 2048,
                                    lambda s, ch, half=half: sb['XN'][:, s * 1024 + half * 512: s * 1024 + half * 512 + 512],
                                    lambda j, ch, bk, half=half: proj_out(j, half, bk),
                                    nchunks=1)
                    for cc in range(4 * half, 4 * half + 4):
                        _ssd_chunk(tb, cc, gchunk[0])
                        gchunk[0] += 1

        def _ssd_chunk(tb, cc, gc):
            col0 = cc * 128
            BGR = f'BG{cc // 4}'
            DTR = f'DT{cc // 4}'
            hs_old = sb['HS0'] if gc % 2 == 0 else sb['HS1']
            hs_new = sb['HS1'] if gc % 2 == 0 else sb['HS0']
            hro = 'HS0' if gc % 2 == 0 else 'HS1'
            hrn = 'HS1' if gc % 2 == 0 else 'HS0'
            bb.op('T', lambda t: t.transpose(PB7[:, 0:32], sb['dtfm'][0:32, col0:col0 + 128],
                                             sb['c_idf'][0:32, 0:32]),
                  reads=[DTR, 'c_idf'], writes=[PR7])
            bb.op('S', lambda s: s.copy(sb['dt_tm'][:], PB7[:, 0:32]),
                  reads=[PR7], writes=['dt_tm'])
            bb.op('V', lambda v: v.tensor_scalar_mul(sb['AFM'][:], sb['dtfm'][0:32, col0:col0 + 128],
                                                     sb['acol'][:]),
                  reads=[DTR, 'acol'], writes=['AFM'])
            bb.op('V', lambda v: v.tensor_tensor_scan(sb['ACSFM'][:], sb['AFM'][:], sb['AFM'][:],
                                                      0.0, ALU.add, ALU.bypass),
                  reads=['AFM'], writes=['ACSFM'])
            bb.op('S', lambda s: s.activation(sb['DSFM'][:], sb['ACSFM'][:], AF.Exp,
                                              scale=-1.0, bias=sb['ACSFM'][:, 127:128]),
                  reads=['ACSFM'], writes=['DSFM'])
            bb.op('T', lambda t: t.transpose(PB7[:, 32:64], sb['ACSFM'][:], sb['c_idf'][0:32, 0:32]),
                  reads=['ACSFM', 'c_idf'], writes=[PR7])
            bb.op('S', lambda s: s.copy(sb['acs_tm'][:], PB7[:, 32:64]),
                  reads=[PR7], writes=['acs_tm'])
            bb.op('T', lambda t: t.transpose(PB7[:, 64:96], sb['DSFM'][:], sb['c_idf'][0:32, 0:32]),
                  reads=['DSFM', 'c_idf'], writes=[PR7])
            bb.op('S', lambda s: s.copy(sb['ds_tm'][:], PB7[:, 64:96]),
                  reads=[PR7], writes=['ds_tm'])
            bb.op('T', lambda t: t.transpose(PB7[0:1, 96:128], sb['ACSFM'][:, 127:128],
                                             sb['c_idf'][0:32, 0:32]),
                  reads=['ACSFM', 'c_idf'], writes=[PR7])
            bb.op('S', lambda s: s.activation(sb['ASR'][:], PB7[0:1, 96:128], AF.Exp),
                  reads=[PR7], writes=['ASR'])
            bb.op('S', lambda s: s.activation(sb['FTM'][:], sb['acs_tm'][:], AF.Exp, scale=-1.0),
                  reads=['acs_tm'], writes=['FTM'])
            bb.op('S', lambda s: s.activation(sb['ETM'][:], sb['acs_tm'][:], AF.Exp),
                  reads=['acs_tm'], writes=['ETM'])
            bb.op('T', lambda t: t.matmul(PB7[:, 128:160], sb['c_onesf'][0:1, :], sb['ASR'][:],
                                          start=True, stop=True),
                  reads=['ASR', 'c_onesf'], writes=[PR7])
            bb.op('S', lambda s: s.activation(sb['EASB'][:], PB7[:, 128:160], AF.Identity),
                  reads=[PR7], writes=['EASB'])
            for vt in range(4):
                bb.op('T', lambda t, vt=vt: t.transpose(
                    PBT[:, vt * 128:(vt + 1) * 128],
                    sb['BIG'][:, vt * 1024 + col0: vt * 1024 + col0 + 128], sb['c_idb'][:]),
                    reads=[BGR, 'c_idb'], writes=[PRT])
            for kt in range(4):
                bb.op('T', lambda t, kt=kt: t.transpose(
                    PBT[:, 512 + kt * 128: 512 + (kt + 1) * 128],
                    sb['BIG'][:, (4 + kt) * 1024 + col0: (4 + kt) * 1024 + col0 + 128], sb['c_idb'][:]),
                    reads=[BGR, 'c_idb'], writes=[PRT])
            bb.op('S', lambda s: s.copy(sb['VTM'][:], PBT[:, 0:512]),
                  reads=[PRT], writes=['VTM'])
            bb.op('S', lambda s: s.copy(sb['KTM'][:], PBT[:, 512:1024]),
                  reads=[PRT], writes=['KTM'])
            for aa in range(4):
                for ccp in range(2):
                    xv_o = bass.AP(sb['XV'], (8 * aa + ccp) * 64,
                                   [[2048, 128], [128, 4], [1, 64]])
                    v_rep = bass.AP(sb['VTM'], (2 * aa + ccp) * 64,
                                    [[512, 128], [0, 4], [1, 64]])
                    dtB = bass.AP(sb['dt_tm'], 8 * aa + ccp,
                                  [[32, 128], [2, 4], [0, 64]])
                    bb.op('V', lambda v, xv_o=xv_o, v_rep=v_rep, dtB=dtB: v.tensor_mul(
                        xv_o, v_rep, dtB),
                        reads=['VTM', 'dt_tm'], writes=[f'XV{2 * aa}', f'XV{2 * aa + 1}'])
            for aa in range(4):
                for ccp in range(2):
                    xvd_o = bass.AP(sb['XVD'], (2 * aa + ccp) * 256,
                                    [[2048, 128], [64, 4], [1, 64]])
                    xv_v = bass.AP(sb['XV'], (8 * aa + ccp) * 64,
                                   [[2048, 128], [128, 4], [1, 64]])
                    dsB = bass.AP(sb['ds_tm'], 8 * aa + ccp,
                                  [[32, 128], [2, 4], [0, 64]])
                    bb.op('V', lambda v, xvd_o=xvd_o, xv_v=xv_v, dsB=dsB: v.tensor_mul(
                        xvd_o, xv_v, dsB),
                        reads=[f'XV{2 * aa}', f'XV{2 * aa + 1}', 'ds_tm'], writes=['XVD'])

            ygb = sb['YG0'] if gc % 2 == 0 else sb['YG1']
            ygr = 'YG0' if gc % 2 == 0 else 'YG1'

            def emit_st(hh):
                par = hh % 2
                g = G_OF[hh]
                ktile, kbase = 4 + g // 2, 64 * (g % 2)
                qtile, qbase = 8 + hh // 2, 64 * (hh % 2)
                sslot = (hh // 2) % 4
                stslot = hh % 4
                bb.op('T', lambda t, ktile=ktile, kbase=kbase, qtile=qtile,
                      qbase=qbase, par=par, sslot=sslot: t.matmul(
                          PB[par][:, sslot * 128:(sslot + 1) * 128],
                          sb['BIG'][kbase:kbase + 64, ktile * 1024 + col0: ktile * 1024 + col0 + 128],
                          sb['BIG'][qbase:qbase + 64, qtile * 1024 + col0: qtile * 1024 + col0 + 128],
                          start=True, stop=True),
                      reads=[BGR], writes=[PR[par]])
                bb.op('V', lambda v, hh=hh, par=par, sslot=sslot, stslot=stslot:
                      v.scalar_tensor_tensor(
                          sb['STT'][:, stslot * 128:(stslot + 1) * 128],
                          PB[par][:, sslot * 128:(sslot + 1) * 128],
                          sb['FTM'][:, hh:hh + 1], sb['c_mask'][:], ALU.mult, ALU.mult),
                      reads=[PR[par], 'FTM', 'c_mask'], writes=[f'ST{stslot}'])

            def emit_ydyo(hh):
                g = G_OF[hh]
                qtile, qbase = 8 + hh // 2, 64 * (hh % 2)
                stslot = hh % 4
                ybank = 2 + hh // 8
                ypos = (hh % 8) * 64
                hblk = (g // 2) * 256 + JJ_OF[hh] * 64
                bb.op('T', lambda t, hh=hh, stslot=stslot, ybank=ybank, ypos=ypos: t.matmul(
                    PB[ybank][:, ypos:ypos + 64],
                    sb['STT'][:, stslot * 128:(stslot + 1) * 128],
                    sb['XV'][:, hh * 64:(hh + 1) * 64], start=True, stop=False),
                    reads=[f'ST{stslot}', f'XV{hh // 8}'], writes=[PR[ybank]])
                bb.op('T', lambda t, qtile=qtile, qbase=qbase, ybank=ybank,
                      ypos=ypos, hblk=hblk: t.matmul(
                          PB[ybank][:, ypos:ypos + 64],
                          sb['BIG'][qbase:qbase + 64, qtile * 1024 + col0: qtile * 1024 + col0 + 128],
                          sb['HSBF'][qbase:qbase + 64, hblk:hblk + 64],
                          start=False, stop=True),
                      reads=[BGR, 'HSBF'], writes=[PR[ybank]])

            def emit_tail(b):
                y2 = sb['FS1'] if b % 2 == 0 else sb['XNEW']
                y2r = 'FS1' if b % 2 == 0 else 'XNEW'
                c0, c1 = b * 8, (b + 1) * 8
                bb.op('S', lambda s, b=b, y2=y2: s.activation(
                    y2[:], PB[2 + b][:, 0:512], AF.Square),
                    reads=[PR[2 + b]], writes=[y2r])
                seg = bass.AP(y2, 0, [[512, 128], [64, 8], [1, 64]])
                bb.op('V', lambda v, b=b, seg=seg: v.reduce_sum(
                    sb['MS'][:, b * 8:(b + 1) * 8], seg, axis=AX.X),
                    reads=[y2r], writes=['MS'])
                bb.op('S', lambda s, c0=c0, c1=c1: s.activation(
                    sb['U32'][:, c0:c1], sb['ETM'][:, c0:c1], AF.Square),
                    reads=['ETM'], writes=['U32'])
                bb.op('V', lambda v, c0=c0, c1=c1: v.tensor_mul(
                    sb['T32'][:, c0:c1], sb['U32'][:, c0:c1], sb['MS'][:, c0:c1]),
                    reads=['U32', 'MS'], writes=['T32'])
                bb.op('S', lambda s, c0=c0, c1=c1: s.activation(
                    sb['W32'][:, c0:c1], sb['T32'][:, c0:c1], AF.Ln,
                    scale=sm[:, 174:175], bias=sm[:, 172:173]),
                    reads=['T32', 'c_smalls'], writes=['W32'])
                bb.op('S', lambda s, c0=c0, c1=c1: s.activation(
                    sb['T32'][:, c0:c1], sb['W32'][:, c0:c1], AF.Exp,
                    scale=sm[:, 173:174]),
                    reads=['W32', 'c_smalls'], writes=['T32'])
                bb.op('V', lambda v, c0=c0, c1=c1: v.tensor_mul(
                    sb['W32'][:, c0:c1], sb['T32'][:, c0:c1], sb['ETM'][:, c0:c1]),
                    reads=['T32', 'ETM'], writes=['W32'])
                wv = bass.AP(sb['W32'], b * 8, [[32, 128], [1, 8], [0, 64]])
                bb.op('V', lambda v, b=b, wv=wv: v.tensor_mul(
                    sb['XV'][:, b * 512:(b + 1) * 512], PB[2 + b][:, 0:512], wv),
                    reads=[PR[2 + b], 'W32'], writes=[f'XV{b}'])
                for i in range(4):
                    j = b * 4 + i
                    pcol = (b % 2) * 512 + i * 128
                    bb.op('T', lambda t, j=j, pcol=pcol: t.transpose(
                        PBT[:, pcol:pcol + 128],
                        sb['XV'][:, j * 128:(j + 1) * 128], sb['c_idb'][:]),
                        reads=[f'XV{b}', 'c_idb'], writes=[PRT])
                    if j < 8:
                        sg = sb['BIG'][:, 24576 + j * 1024 + col0: 24576 + j * 1024 + col0 + 128]
                        sgr = BGR
                    else:
                        sg = sb['SCRA'][:, (j - 8) * 1024 + col0: (j - 8) * 1024 + col0 + 128]
                        sgr = 'SCRA'
                    bb.op('V', lambda v, j=j, pcol=pcol, sg=sg, ygb=ygb: v.scalar_tensor_tensor(
                        ygb[:, j * 128:(j + 1) * 128], PBT[:, pcol:pcol + 128],
                        sm[:, 171:172], sg, ALU.mult, ALU.mult),
                        reads=[PRT, sgr, 'c_smalls'], writes=[ygr])

            for hh in range(32):
                emit_st(hh)
                if hh >= 2:
                    emit_ydyo(hh - 2)
                    if (hh - 2) % 8 == 7:
                        emit_tail((hh - 2) // 8)
            emit_ydyo(30)
            emit_ydyo(31)
            emit_tail(3)

            for g in range(8):
                par = g % 2
                base = 64 * par
                scol = ((g // 2) % 2) * 256
                sbank = par
                bb.op('T', lambda t, g=g, base=base, scol=scol, sbank=sbank: t.matmul(
                    PB[sbank][base:base + 64, scol:scol + 256],
                    sb['KTM'][:, g * 64:(g + 1) * 64],
                    sb['XVD'][:, g * 256:(g + 1) * 256], start=True, stop=True),
                    reads=['KTM', 'XVD'], writes=[PR[sbank]])
                for jj in range(4):
                    hh2 = SLOTS_OF_G[g][jj]
                    hblk = (g // 2) * 256 + jj * 64
                    bb.op('V', lambda v, g=g, jj=jj, hh2=hh2, base=base, scol=scol,
                          sbank=sbank, hblk=hblk: v.scalar_tensor_tensor(
                              hs_new[base:base + 64, hblk:hblk + 64],
                              hs_old[base:base + 64, hblk:hblk + 64],
                              sb['EASB'][base:base + 64, hh2:hh2 + 1],
                              PB[sbank][base:base + 64, scol + jj * 64: scol + (jj + 1) * 64],
                              ALU.mult, ALU.add),
                          reads=[hro, 'EASB', PR[sbank]], writes=[hrn])
            bb.op('S', lambda s: s.copy(sb['HSBF'][:], hs_new[:]),
                  reads=[hrn], writes=['HSBF'])
            i2 = (tb * 8 + cc) // 4
            off = (cc % 4) * 128
            dst = bass.AP(h['yg'].tensor, i2 * 128 * 8192 + off,
                          [[8192, 128], [512, 16], [1, 128]])
            src = bass.AP(ygb, 0, [[2048, 128], [128, 16], [1, 128]])
            bb.op('S', lambda s, dst=dst, src=src: s.dma_start(dst, src),
                  reads=[ygr], writes=[f'yg.{i2}'], dma=True)

        def b2_phase():
            for j in range(16):
                bb.op('Q', lambda q, j=j: q.dma_start(
                    sb['BIG'][:, j * 2048:(j + 1) * 2048], h['wo_bf'][j]),
                    reads=['wo_bf'], writes=['BIG'], dma=True)
            for i2 in range(NCH):
                half = (i2 % 2) * 8192
                xnr = 'XNa' if i2 % 2 == 0 else 'XNb'
                bb.op('Q', lambda q, i2=i2, half=half: q.dma_start(
                    sb['XN'][:, half:half + 8192], h['yg'][i2]),
                    reads=[f'yg.{i2}'], writes=[xnr], dma=True)
                for j in range(16):
                    bk = j % 4
                    xp = sb['XP0'] if j % 2 == 0 else sb['XP1']
                    xpr = 'XP0' if j % 2 == 0 else 'XP1'
                    bb.op('Q', lambda q, i2=i2, j=j, xp=xp: q.dma_start(
                        xp[:], h['x1'][i2, :, j * 512:(j + 1) * 512]),
                        reads=[f'x1.{i2}'], writes=[xpr], dma=True)
                    for s in range(16):
                        bb.op('T', lambda t, j=j, s=s, bk=bk, half=half: t.matmul(
                            PB[bk][:, 0:512],
                            sb['BIG'][:, j * 2048 + s * 128: j * 2048 + (s + 1) * 128],
                            sb['XN'][:, half + s * 512: half + (s + 1) * 512],
                            start=(s == 0), stop=(s == 15)),
                            reads=['BIG', xnr], writes=[PR[bk]])
                    xn2 = sb['XNEW'] if j % 2 == 0 else sb['XNEW2']
                    xr = 'XNEW' if j % 2 == 0 else 'XNEW2'
                    bb.op('V', lambda v, bk=bk, xn2=xn2, xp=xp: v.tensor_add(
                        xn2[:], PB[bk][:, 0:512], xp[:]),
                        reads=[PR[bk], xpr], writes=[xr])
                    bb.op('S', lambda s, i2=i2, j=j, xn2=xn2: s.dma_start(
                        h['x2'][i2, :, j * 512:(j + 1) * 512], xn2[:]),
                        reads=[xr], writes=[f'x2.{i2}'], dma=True)

        def big_barrier():
            bb.op('V', lambda v: v.engine_nop(),
                  reads=['BIG', 'BG0', 'BG1', 'DT0', 'DT1'],
                  writes=['BIG', 'BG0', 'BG1', 'DT0', 'DT1'])

        ffn_phase('x', 'x1', False)
        big_barrier()
        b1_phase()
        big_barrier()
        b2_phase()
        final_stores = ffn_phase('x2', 'yout', True)
        bb.fence('Q', [final_stores[-1]])
        bb.build()
    return nc


_NC = None


def kernel(**inputs):
    global _NC
    consts, weights = _host_prep(inputs)
    if _NC is None:
        _NC = _build_nc()
    x = np.asarray(inputs['L0_attn_hidden_states'])
    in_maps = []
    for b in range(NCORES):
        m = dict(consts)
        m.update(weights)
        m['x'] = _x_chunks(np.ascontiguousarray(x[b]))
        in_maps.append(m)
    res = run_bass_kernel_spmd(_NC, in_maps, core_ids=list(range(NCORES)))
    out = np.empty((NCORES, L, D), np.float32)
    for b in range(NCORES):
        out[b] = _out_unchunks(np.asarray(res.results[b]['yout']))
    return out


# version 14
# speedup vs baseline: 1.0074x; 1.0074x over previous
import numpy as np
import ml_dtypes

import concourse.bass as bass
import concourse.mybir as mybir
from concourse.bass_utils import run_bass_kernel_spmd

F32 = mybir.dt.float32
BF16 = mybir.dt.bfloat16
AF = mybir.ActivationFunctionType
ALU = mybir.AluOpType
AX = mybir.AxisListType
bfdt = ml_dtypes.bfloat16

D = 2048
L = 4096
NCORES = 8
TA = 512
NTA = L // TA
TB = 1024
NTB = L // TB
NCH = 8
PROJ_T = 41
EPS_LN = 1e-5
EPS_RMS = 1e-5

_E = [0, 1, 2, 3, 8, 9, 10, 11, 16, 17, 18, 19, 24, 25, 26, 27]
_O = [e + 4 for e in _E]
SLOT2ORIG = [0] * 32
for j in range(16):
    SLOT2ORIG[2 * j] = _E[j]
    SLOT2ORIG[2 * j + 1] = _O[j]
G_OF = [SLOT2ORIG[s] // 4 for s in range(32)]
JJ_OF = [SLOT2ORIG[s] % 4 for s in range(32)]
SLOTS_OF_G = [[(8 * (g // 2) + 2 * jj + (g % 2)) for jj in range(4)] for g in range(8)]


class _Op:
    __slots__ = ('sem', 'emit', 'engine', 'dma', 'waits', 'inc', 'value',
                 'clock', 'seq')

    def __init__(self, sem, engine, emit, dma, seq):
        self.sem = sem
        self.engine = engine
        self.emit = emit
        self.dma = dma
        self.seq = seq
        self.waits = []
        self.inc = bool(dma)
        self.value = None
        self.clock = None


_ENGINES = ('T', 'V', 'S', 'P', 'Q')


def _dma_sem_key(regions):
    r = regions[0] if regions else 'MISC'
    for p in ('x1.', 'x2.', 'yg.', 'yout.'):
        if r.startswith(p):
            return 'CH' + r.split('.')[1]
    return r


class BB:

    def __init__(self, nc):
        self.nc = nc
        self.queues = {e: [] for e in _ENGINES}
        self.regions = {}
        self.counts = {}
        self._known = {e: {} for e in _ENGINES}

    def _mk(self, engine, emit, dma, reads, writes):
        if dma:
            key = _dma_sem_key(list(writes) or list(reads))
            key = 'D_' + key
        else:
            key = engine
        units = 16 if dma else 1
        cnt = self.counts.get(key, 0) + units
        self.counts[key] = cnt
        o = _Op(key, engine, emit, dma, cnt)
        return o

    def op(self, engine, emit, reads=(), writes=(), dma=False):
        o = self._mk(engine, emit, dma, reads, writes)
        known = self._known[engine]

        rl, wl = [], list(writes)
        for r in reads:
            (wl if r.startswith('PS') else rl).append(r)

        deps = []
        for r in rl:
            ent = self.regions.get(r)
            if ent is not None and ent[0] is not None:
                deps.append(ent[0])
        for w in wl:
            ent = self.regions.get(w)
            if ent is not None:
                if ent[0] is not None:
                    deps.append(ent[0])
                deps.extend(ent[1])

        for d in deps:
            if d.sem == o.sem and not d.dma:
                if engine == 'T' and d.engine == 'T':
                    continue
            if d.seq <= known.get(d.sem, -1):
                continue
            o.waits.append(d)
            d.inc = True
            if d.clock is not None:
                for k2, v in d.clock.items():
                    if v > known.get(k2, -1):
                        known[k2] = v
            if d.seq > known.get(d.sem, -1):
                known[d.sem] = d.seq

        o.clock = dict(known)
        o.clock[o.sem] = max(o.clock.get(o.sem, -1), o.seq)

        for r in rl:
            ent = self.regions.setdefault(r, [None, []])
            ent[1].append(o)
        for w in wl:
            self.regions[w] = [o, []]
        self.queues[engine].append(o)
        return o

    def fence(self, engine, deps):
        o = self._mk(engine, None, False, (), ())
        o.inc = False
        known = self._known[engine]
        for d in deps:
            if d.seq <= known.get(d.sem, -1):
                continue
            o.waits.append(d)
            d.inc = True
            if d.seq > known.get(d.sem, -1):
                known[d.sem] = d.seq
        o.clock = dict(known)
        self.queues[engine].append(o)
        return o

    def build(self):
        nc = self.nc
        ccount = {}
        for e in _ENGINES:
            for o in self.queues[e]:
                if o.dma:
                    if o.inc:
                        o.value = o.seq
                else:
                    if o.inc:
                        c = ccount.get(o.sem, 0) + 1
                        ccount[o.sem] = c
                        o.value = c
        import contextlib
        used = set()
        for e in _ENGINES:
            for o in self.queues[e]:
                for d in o.waits:
                    used.add(d.sem)
                if o.inc:
                    used.add(o.sem)
        with contextlib.ExitStack() as st:
            sems = {}
            for k in sorted(used):
                sems[k] = st.enter_context(nc.semaphore('sm_' + k.replace('.', '_')))
            st.enter_context(_BlockRunner(nc, self, sems))

    def _wait_val(self, d):
        return d.value if not d.dma else d.seq


class _BlockRunner:
    def __init__(self, nc, bb, sems):
        self.nc = nc
        self.bb = bb
        self.sems = sems

    def __enter__(self):
        nc, bb, sems = self.nc, self.bb, self.sems
        self.block = nc.Block()
        block = self.block.__enter__()

        def run_queue(h, e):
            for o in bb.queues[e]:
                for d in o.waits:
                    h.wait_ge(sems[d.sem], d.value if not d.dma else d.seq)
                if o.emit is not None:
                    ins = o.emit(h)
                    if o.inc:
                        ins.then_inc(sems[o.sem], 16 if o.dma else 1)

        if bb.queues['T']:
            block.tensor(lambda h: run_queue(h, 'T'))
        if bb.queues['V']:
            block.vector(lambda h: run_queue(h, 'V'))
        if bb.queues['S']:
            block.scalar(lambda h: run_queue(h, 'S'))
        if bb.queues['P']:
            block.gpsimd(lambda h: run_queue(h, 'P'))
        if bb.queues['Q']:
            block.sync(lambda h: run_queue(h, 'Q'))
        return self

    def __exit__(self, *a):
        return self.block.__exit__(*a)


def _tile_images(W):
    F, K = W.shape
    nt = F // 128
    A = W.reshape(nt, 128, K // 128, 128)
    return np.ascontiguousarray(A.transpose(0, 3, 2, 1)).reshape(nt, 128, K)


def _x_chunks(xb):
    t = xb.reshape(NCH, 512, 16, 128)
    return np.ascontiguousarray(t.transpose(0, 3, 2, 1)).reshape(NCH, 128, 16 * 512)


def _out_unchunks(y):
    t = y.reshape(NCH, 128, 16, 512)
    return np.ascontiguousarray(t.transpose(0, 3, 2, 1)).reshape(L, D)


def _host_prep(inp):
    W = np.asarray(inp['L0_attn_W_vkqgdt'])
    conv_w = np.asarray(inp['L0_attn_conv_w'])
    conv_b = np.asarray(inp['L0_attn_conv_b'])
    dt_bias = np.asarray(inp['L0_attn_dt_bias'])
    A_log = np.asarray(inp['L0_attn_A_log_bias'])
    gnorm = np.asarray(inp['L0_attn_gnorm_w'])
    Wo = np.asarray(inp['L0_attn_W_o'])
    Wg = np.asarray(inp['L0_ffn_Wg'])
    W0a = np.asarray(inp['L0_ffn_W0a'])
    b0a = np.asarray(inp['L0_ffn_b0a'])
    W0b = np.asarray(inp['L0_ffn_W0b'])
    b0b = np.asarray(inp['L0_ffn_b0b'])
    W1a = np.asarray(inp['L0_ffn_W1a'])
    b1a = np.asarray(inp['L0_ffn_b1a'])
    W1b = np.asarray(inp['L0_ffn_W1b'])
    b1b = np.asarray(inp['L0_ffn_b1b'])

    qperm = np.array([SLOT2ORIG[s] * 64 + p for s in range(32) for p in range(64)])
    Wp = np.zeros((5248, D), np.float32)
    Wp[0:1024] = W[0:1024]
    Wp[1024:3072] = W[1024 + qperm]
    Wp[3072:5120] = W[3072 + qperm]
    Wp[5120:5152] = W[5120 + np.array(SLOT2ORIG)]
    wvkq = _tile_images(Wp)

    W0a_ext = np.zeros((33 * 128, D), np.float32)
    W0a_ext[0:4096] = W0a
    W0a_ext[4096] = Wg[0]
    W0a_ext[4160] = Wg[1]
    w0a = _tile_images(W0a_ext)
    w1a = _tile_images(W1a)
    w0b = _tile_images(W0b)
    w1b = _tile_images(W1b)
    perm_din = np.array([SLOT2ORIG[s] * 64 + p for s in range(32) for p in range(64)])
    wo = _tile_images(np.ascontiguousarray(Wo[:, perm_din]))

    cperm = np.concatenate([np.arange(1024), 1024 + qperm])
    cw = conv_w[cperm]
    cb = conv_b[cperm]
    sm = np.zeros((128, 176), np.float32)
    sm[:, 0:24] = cw[:, 0].reshape(24, 128).T
    sm[:, 24:48] = cw[:, 1].reshape(24, 128).T
    sm[:, 48:72] = cb.reshape(24, 128).T
    sm[:, 72:104] = b0a.reshape(32, 128).T
    sm[:, 105:137] = b1a.reshape(32, 128).T
    sm[:, 137:153] = b0b.reshape(16, 128).T
    sm[:, 153:169] = b1b.reshape(16, 128).T
    sm[0:32, 169] = dt_bias[SLOT2ORIG]
    sm[0:32, 170] = A_log[SLOT2ORIG]
    sm[:, 171] = np.tile(gnorm, 2)
    sm[:, 172] = 1e-5
    sm[:, 173] = -0.5
    sm[:, 174] = 1.0 / 64

    consts = dict(
        c_idb=np.eye(128, dtype=bfdt),
        c_idf=np.eye(128, dtype=np.float32),
        c_mask=np.triu(np.ones((128, 128), np.float32)).astype(bfdt),
        c_onesb=np.ones((128, 128), bfdt),
        c_onesf=np.ones((128, 128), np.float32),
        c_smalls=sm,
    )
    weights = dict(wvkq_f32=wvkq, w0a_f32=w0a, w1a_f32=w1a,
                   w0b_f32=w0b, w1b_f32=w1b, wo_f32=wo)
    return consts, weights


def _build_nc(debug=False):
    nc = bass.Bass("TRN2", target_bir_lowering=False)
    h = {}
    h['x'] = nc.dram_tensor('x', [NCH, 128, 8192], F32, kind="ExternalInput").ap()
    for n, sh in [('wvkq_f32', [41, 128, 2048]), ('w0a_f32', [33, 128, 2048]),
                  ('w1a_f32', [32, 128, 2048]), ('w0b_f32', [16, 128, 4096]),
                  ('w1b_f32', [16, 128, 4096]), ('wo_f32', [16, 128, 2048])]:
        h[n] = nc.dram_tensor(n, sh, F32, kind="ExternalInput").ap()
    for n, sh, dt in [('c_idb', [128, 128], BF16), ('c_idf', [128, 128], F32),
                      ('c_mask', [128, 128], BF16), ('c_onesb', [128, 128], BF16),
                      ('c_onesf', [128, 128], F32), ('c_smalls', [128, 176], F32)]:
        h[n] = nc.dram_tensor(n, sh, dt, kind="ExternalInput").ap()
    h['yout'] = nc.dram_tensor('yout', [NCH, 128, 8192], F32, kind="ExternalOutput").ap()
    dbg_kind = "ExternalOutput" if debug else "Internal"
    for n, sh, dt in [('wvkq_bf', [41, 128, 2048], BF16), ('w0a_bf', [33, 128, 2048], BF16),
                      ('w1a_bf', [32, 128, 2048], BF16), ('w0b_bf', [16, 128, 4096], BF16),
                      ('w1b_bf', [16, 128, 4096], BF16), ('wo_bf', [16, 128, 2048], BF16)]:
        h[n] = nc.dram_tensor(n, sh, dt, kind="Internal").ap()
    for n, sh, dt in [('x1', [NCH, 128, 8192], F32), ('x2', [NCH, 128, 8192], F32),
                      ('yg', [NCH, 128, 8192], BF16)]:
        h[n] = nc.dram_tensor(n, sh, dt, kind=dbg_kind).ap()

    import contextlib
    with contextlib.ExitStack() as stk:
        E = stk.enter_context
        sb = {}

        def S_(name, shape, dt):
            sb[name] = E(nc.sbuf_tensor('sb_' + name, list(shape), dt))
            return sb[name]

        for n, sh, dt in [('c_idb', (128, 128), BF16), ('c_idf', (128, 128), F32),
                          ('c_mask', (128, 128), BF16), ('c_onesb', (128, 128), BF16),
                          ('c_onesf', (128, 128), F32), ('c_smalls', (128, 176), F32)]:
            S_(n, sh, dt)
        S_('d_bd', (128, 16), F32)
        S_('acol', (32, 1), F32)
        S_('carry', (128, 24), F32)
        S_('XN', (128, 16384), BF16)
        S_('BIG', (128, 32768), BF16)
        S_('XP0', (128, 512), F32)
        S_('XP1', (128, 512), F32)
        S_('WT0', (128, 4096), BF16)
        S_('WT1', (128, 4096), BF16)
        S_('SCRA', (128, 8192), BF16)
        S_('dtfm', (32, 1024), F32)
        S_('DTa', (32, 512), F32)
        S_('DTb', (32, 512), F32)
        S_('AFM', (32, 128), F32)
        S_('ACSFM', (32, 128), F32)
        S_('DSFM', (32, 128), F32)
        S_('dt_tm', (128, 32), F32)
        S_('acs_tm', (128, 32), F32)
        S_('ds_tm', (128, 32), F32)
        S_('FTM', (128, 32), F32)
        S_('ETM', (128, 32), F32)
        S_('EASB', (128, 32), F32)
        S_('U32', (128, 32), F32)
        S_('T32', (128, 32), F32)
        S_('W32', (128, 32), F32)
        S_('MS', (128, 32), F32)
        S_('ASR', (1, 32), F32)
        S_('KTM', (128, 512), BF16)
        S_('VTM', (128, 512), BF16)
        S_('YNF', (128, 2048), BF16)
        S_('STT', (128, 512), BF16)
        S_('XV', (128, 2048), BF16)
        S_('XVD', (128, 2048), BF16)
        S_('HS0', (128, 1024), F32)
        S_('HS1', (128, 1024), F32)
        S_('HSBF', (128, 1024), BF16)
        S_('YG0', (128, 2048), BF16)
        S_('YG1', (128, 2048), BF16)
        S_('E0S', (128, 512), BF16)
        S_('FS1', (128, 512), F32)
        S_('XNEW', (128, 512), F32)
        S_('XNEW2', (128, 512), F32)
        S_('Bb1', (128, 512), BF16)
        S_('Bb0', (128, 512), BF16)
        S_('x2b0', (128, 512), BF16)
        S_('x2bA', (128, 512), BF16)
        S_('x2bB', (128, 512), BF16)
        S_('ROWS', (1, 1024), F32)
        S_('ZR', (1, 1024), BF16)
        S_('INVM', (1, 512), BF16)
        S_('B1R', (1, 512), BF16)
        S_('B0R', (1, 512), BF16)
        PB = [E(nc.psum_tensor(f'PB{i}', [128, 512], F32)) for i in range(6)]
        PBT = E(nc.psum_tensor('PBT', [128, 1024], BF16))
        PB7 = E(nc.psum_tensor('PB7', [128, 512], F32))
        PR = [f'PS{i}' for i in range(6)]
        PRT, PR7 = 'PST', 'PS7'

        bb = BB(nc)
        sm = sb['c_smalls']

        for n in ('c_idb', 'c_idf', 'c_mask', 'c_onesb', 'c_onesf', 'c_smalls'):
            bb.op('Q', lambda q, n=n: q.dma_start(sb[n][:], h[n][:]),
                  writes=[n], dma=True)
        bb.op('V', lambda v: v.tensor_sub(sb['d_bd'][:], sm[:, 153:169], sm[:, 137:153]),
              reads=['c_smalls'], writes=['d_bd'])
        bb.op('S', lambda s: s.activation(sb['acol'][:], sm[0:32, 170:171], AF.Exp),
              reads=['c_smalls'], writes=['acol'])
        bb.op('V', lambda v: v.tensor_scalar_mul(sb['acol'][:], sb['acol'][:], -1.0),
              reads=['acol'], writes=['acol'])
        bb.op('V', lambda v: v.memset(sb['carry'][:], 0.0), writes=['carry'])
        bb.op('V', lambda v: v.memset(sb['HS0'][:], 0.0), writes=['HS0'])
        bb.op('V', lambda v: v.memset(sb['HSBF'][:], 0.0), writes=['HSBF'])

        def dmacast(n, nt):
            for j in range(nt):
                bb.op('P', lambda g, n=n, j=j: g.dma_start(h[n + '_bf'][j], h[n + '_f32'][j]),
                      writes=[n + '_bf'], dma=True)
        dmacast('w0a', 33)
        dmacast('w0b', 16)
        dmacast('w1a', 32)
        dmacast('w1b', 16)
        late_casts = [('wvkq', j) for j in range(41)] + [('wo', j) for j in range(16)]

        def emit_late_casts(k):
            while late_casts and k > 0:
                n, j = late_casts.pop(0)
                bb.op('P', lambda g, n=n, j=j: g.dma_start(h[n + '_bf'][j], h[n + '_f32'][j]),
                      writes=[n + '_bf'], dma=True)
                k -= 1

        def emit_ln(i2, src_name, xn_off_fn):
            for s in range(16):
                xp = sb['XP0'] if s % 2 == 0 else sb['XP1']
                xpr = 'XP0' if s % 2 == 0 else 'XP1'
                bb.op('Q', lambda q, s=s, xp=xp: q.dma_start(
                    xp[:], h[src_name][i2, :, s * 512:(s + 1) * 512]),
                    reads=[f'{src_name}.{i2}'], writes=[xpr], dma=True)
                bb.op('V', lambda v, s=s, xp=xp: v.tensor_copy(
                    sb['SCRA'][:, s * 512:(s + 1) * 512], xp[:]),
                    reads=[xpr], writes=['SCRA'])
                bb.op('T', lambda t, s=s: t.matmul(
                    PB7[0:1, 0:512], sb['c_onesb'][:, 0:1],
                    sb['SCRA'][:, s * 512:(s + 1) * 512],
                    start=(s == 0), stop=(s == 15)),
                    reads=['SCRA', 'c_onesb'], writes=[PR7])
            x2bufs = [('x2b0', sb['x2b0']), ('E0S', sb['E0S']),
                      ('x2bA', sb['x2bA']), ('x2bB', sb['x2bB'])]
            for s in range(16):
                x2r, x2b = x2bufs[s % 4]
                bb.op('V', lambda v, s=s, x2b=x2b: v.tensor_mul(
                    x2b[:], sb['SCRA'][:, s * 512:(s + 1) * 512],
                    sb['SCRA'][:, s * 512:(s + 1) * 512]),
                    reads=['SCRA'], writes=[x2r])
                bb.op('T', lambda t, s=s, x2b=x2b: t.matmul(
                    PB[5][0:1, 0:512], sb['c_onesb'][:, 0:1], x2b[:],
                    start=(s == 0), stop=(s == 15)),
                    reads=[x2r, 'c_onesb'], writes=[PR[5]])
            R = sb['ROWS']
            bb.op('V', lambda v: v.tensor_scalar_mul(R[0:1, 0:512], PB7[0:1, 0:512], 1.0 / D),
                  reads=[PR7], writes=['ROWS'])
            bb.op('S', lambda s: s.activation(R[0:1, 512:1024], R[0:1, 0:512], AF.Square),
                  reads=['ROWS'], writes=['ROWS'])
            bb.op('V', lambda v: v.scalar_tensor_tensor(
                PB7[32:33, 0:512], PB[5][0:1, 0:512], 1.0 / D, R[0:1, 512:1024],
                ALU.mult, ALU.subtract),
                reads=[PR[5], 'ROWS'], writes=[PR7])
            bb.op('S', lambda s: s.activation(PB[5][32:33, 0:512], PB7[32:33, 0:512], AF.Ln,
                                              bias=sm[0:1, 172:173]),
                  reads=[PR7, 'c_smalls'], writes=[PR[5]])
            bb.op('S', lambda s: s.activation(PB7[96:97, 0:512], PB[5][32:33, 0:512], AF.Exp,
                                              scale=sm[0:1, 173:174]),
                  reads=[PR[5], 'c_smalls'], writes=[PR7])
            bb.op('V', lambda v: v.tensor_copy(sb['B1R'][:], PB7[96:97, 0:512]),
                  reads=[PR7], writes=['B1R'])
            bb.op('V', lambda v: v.scalar_tensor_tensor(
                sb['B0R'][:], R[0:1, 0:512], -1.0, PB7[96:97, 0:512], ALU.mult, ALU.mult),
                reads=['ROWS', PR7], writes=['B0R'])
            bb.op('T', lambda t: t.matmul(PB[5][:, 0:512], sb['c_onesb'][0:1, :], sb['B1R'][:],
                                          start=True, stop=True),
                  reads=['B1R', 'c_onesb'], writes=[PR[5]])
            bb.op('S', lambda s: s.copy(sb['Bb1'][:], PB[5][:, 0:512]),
                  reads=[PR[5]], writes=['Bb1'])
            bb.op('T', lambda t: t.matmul(PB[5][:, 0:512], sb['c_onesb'][0:1, :], sb['B0R'][:],
                                          start=True, stop=True),
                  reads=['B0R', 'c_onesb'], writes=[PR[5]])
            bb.op('S', lambda s: s.copy(sb['Bb0'][:], PB[5][:, 0:512]),
                  reads=[PR[5]], writes=['Bb0'])
            bv1 = bass.AP(sb['Bb1'], 0, [[512, 128], [0, 16], [1, 512]])
            bv0 = bass.AP(sb['Bb0'], 0, [[512, 128], [0, 16], [1, 512]])
            bb.op('V', lambda v: v.tensor_mul(sb['SCRA'][:], sb['SCRA'][:], bv1),
                  reads=['SCRA', 'Bb1'], writes=['SCRA'])
            offs = [xn_off_fn(s) for s in range(16)]
            stride = offs[1] - offs[0]
            out_ap = bass.AP(sb['XN'], offs[0], [[16384, 128], [stride, 16], [1, 512]])
            bb.op('V', lambda v: v.tensor_add(out_ap, sb['SCRA'][:], bv0),
                  reads=['SCRA', 'Bb0'], writes=['XN'])

        def stream_mm_phase(wname, ntiles, K, rhs_fn, out_fn, nchunks=1,
                            rhs_region='XN', after_tile=None):
            ks = K // 128
            for j in range(min(2, ntiles)):
                wt = 'WT0' if j % 2 == 0 else 'WT1'
                bb.op('Q', lambda q, j=j, wt=wt: q.dma_start(
                    sb[wt][:, 0:K], h[wname][j]),
                    reads=[wname], writes=[wt], dma=True)
            for j in range(ntiles):
                wt = 'WT0' if j % 2 == 0 else 'WT1'
                for ch in range(nchunks):
                    bk = (j * nchunks + ch) % 2
                    for s in range(ks):
                        bb.op('T', lambda t, j=j, ch=ch, s=s, wt=wt, bk=bk: t.matmul(
                            PB[bk][:, 0:512],
                            sb[wt][:, s * 128:(s + 1) * 128],
                            rhs_fn(s, ch),
                            start=(s == 0), stop=(s == ks - 1)),
                            reads=[wt, rhs_region],
                            writes=[PR[bk]])
                    out_fn(j, ch, bk)
                if j + 2 < ntiles:
                    wt2 = 'WT0' if j % 2 == 0 else 'WT1'
                    bb.op('Q', lambda q, j=j, wt2=wt2: q.dma_start(
                        sb[wt2][:, 0:K], h[wname][j + 2]),
                        reads=[wname], writes=[wt2], dma=True)
                if after_tile is not None:
                    after_tile(j)

        def ffn_phase(src_name, dst_name, is_out):
            for ia in range(NTA):
                emit_ln(ia, src_name, lambda s: s * 512)
                def h0_out(j, ch, bk):
                    if j < 32:
                        bb.op('S', lambda s, j=j, bk=bk: s.activation(
                            sb['BIG'][:, j * 512:(j + 1) * 512], PB[bk][:, 0:512],
                            AF.Gelu, bias=sm[:, 72 + j:73 + j]),
                            reads=[PR[bk], 'c_smalls'], writes=['BIG'])
                    else:
                        bb.op('S', lambda s, bk=bk: s.copy(sb['ZR'][0:1, 0:512], PB[bk][0:1, 0:512]),
                              reads=[PR[bk]], writes=['ZR'])
                        bb.op('S', lambda s, bk=bk: s.copy(sb['ZR'][0:1, 512:1024], PB[bk][64:65, 0:512]),
                              reads=[PR[bk]], writes=['ZR'])
                        bb.op('V', lambda v: v.tensor_sub(PB7[32:33, 0:512], sb['ZR'][0:1, 0:512],
                                                          sb['ZR'][0:1, 512:1024]),
                              reads=['ZR'], writes=[PR7])
                        bb.op('V', lambda v: v.tensor_single_scalar(sb['INVM'][:], PB7[32:33, 0:512],
                                                                    0.0, ALU.is_lt),
                              reads=[PR7], writes=['INVM'])
                        bb.op('T', lambda t: t.matmul(PB7[:, 0:512], sb['c_onesb'][0:1, :],
                                                      sb['INVM'][:], start=True, stop=True),
                              reads=['INVM', 'c_onesb'], writes=[PR7])
                stream_mm_phase('w0a_bf', 33, 2048,
                                lambda s, ch: sb['XN'][:, s * 512:(s + 1) * 512], h0_out)
                def out0_out(j, ch, bk):
                    bb.op('S', lambda s, j=j, bk=bk: s.activation(
                        sb['SCRA'][:, j * 512:(j + 1) * 512], PB[bk][:, 0:512],
                        AF.Identity, bias=sm[:, 137 + j:138 + j]),
                        reads=[PR[bk], 'c_smalls'], writes=['SCRA'])
                stream_mm_phase('w0b_bf', 16, 4096,
                                lambda s, ch: sb['BIG'][:, s * 512:(s + 1) * 512], out0_out,
                                rhs_region='BIG')
                def h1_out(j, ch, bk):
                    bb.op('S', lambda s, j=j, bk=bk: s.activation(
                        sb['BIG'][:, (32 + j) * 512:(33 + j) * 512], PB[bk][:, 0:512],
                        AF.Relu, bias=sm[:, 105 + j:106 + j]),
                        reads=[PR[bk], 'c_smalls'], writes=['BIG'])
                stream_mm_phase('w1a_bf', 32, 2048,
                                lambda s, ch: sb['XN'][:, s * 512:(s + 1) * 512], h1_out)
                store_ops = []

                def out1_out(j, ch, bk):
                    e0 = sb['SCRA'][:, j * 512:(j + 1) * 512]
                    bb.op('V', lambda v, j=j, bk=bk, e0=e0: v.scalar_tensor_tensor(
                        sb['FS1'][:], PB[bk][:, 0:512], sb['d_bd'][:, j:j + 1], e0,
                        ALU.add, ALU.subtract),
                        reads=[PR[bk], 'd_bd', 'SCRA'], writes=['FS1'])
                    bb.op('V', lambda v: v.tensor_mul(sb['FS1'][:], sb['FS1'][:], PB7[:, 0:512]),
                          reads=['FS1', PR7], writes=['FS1'])
                    bb.op('P', lambda p, e0=e0: p.tensor_add(sb['FS1'][:], sb['FS1'][:], e0),
                          reads=['FS1', 'SCRA'], writes=['FS1'])
                    xp = sb['XP0'] if j % 2 == 0 else sb['XP1']
                    xpr = 'XP0' if j % 2 == 0 else 'XP1'
                    bb.op('Q', lambda q, j=j, xp=xp, ia=ia: q.dma_start(
                        xp[:], h[src_name][ia, :, j * 512:(j + 1) * 512]),
                        reads=[f'{src_name}.{ia}'], writes=[xpr], dma=True)
                    xn2 = sb['XNEW'] if j % 2 == 0 else sb['XNEW2']
                    xr = 'XNEW' if j % 2 == 0 else 'XNEW2'
                    bb.op('V', lambda v, xn2=xn2, xp=xp: v.scalar_tensor_tensor(
                        xn2[:], sb['FS1'][:], 0.5, xp[:], ALU.mult, ALU.add),
                        reads=['FS1', xpr], writes=[xr])
                    store_ops.append(bb.op('S', lambda s, j=j, xn2=xn2, ia=ia: s.dma_start(
                        h[dst_name][ia, :, j * 512:(j + 1) * 512], xn2[:]),
                        reads=[xr], writes=[f'{dst_name}.{ia}'], dma=True))
                stream_mm_phase('w1b_bf', 16, 4096,
                                lambda s, ch: sb['BIG'][:, (32 + s) * 512:(33 + s) * 512], out1_out,
                                rhs_region='BIG')
                if dst_name == 'x1':
                    emit_late_casts(8)
            return store_ops

        def b1_phase():
            gchunk = [0]
            for tb in range(NTB):
                for half in range(2):
                    emit_ln(tb * 2 + half, 'x1', lambda s, half=half: s * 1024 + half * 512)
                def proj_out(j, ch, bk):
                    col = j * 1024 + ch * 512
                    if j < 24:
                        ct = sb['E0S']
                        bb.op('S', lambda s, j=j, bk=bk, ct=ct: s.activation(
                            ct[:], PB[bk][:, 0:512], AF.Identity,
                            bias=sm[:, 48 + j:49 + j], scale=sm[:, 24 + j:25 + j]),
                            reads=[PR[bk], 'c_smalls'], writes=['E0S'])
                        bb.op('V', lambda v, j=j, bk=bk, ct=ct: v.scalar_tensor_tensor(
                            ct[:, 1:512], PB[bk][:, 0:511], sm[:, j:j + 1], ct[:, 1:512],
                            ALU.mult, ALU.add),
                            reads=[PR[bk], 'c_smalls', 'E0S'], writes=['E0S'])
                        bb.op('V', lambda v, j=j, ct=ct: v.scalar_tensor_tensor(
                            ct[:, 0:1], sb['carry'][:, j:j + 1], sm[:, j:j + 1], ct[:, 0:1],
                            ALU.mult, ALU.add),
                            reads=['carry', 'c_smalls', 'E0S'], writes=['E0S'])
                        bb.op('S', lambda s, j=j, bk=bk: s.copy(
                            sb['carry'][:, j:j + 1], PB[bk][:, 511:512]),
                            reads=[PR[bk]], writes=['carry'])
                        bb.op('S', lambda s, j=j, ch=ch, ct=ct, col=col: s.activation(
                            sb['BIG'][:, col:col + 512], ct[:], AF.Silu),
                            reads=['E0S'], writes=[f'BG{ch}'])
                    elif j < 40:
                        gi = j - 24
                        if gi < 8:
                            dst = sb['BIG'][:, 24576 + gi * 1024 + ch * 512: 24576 + gi * 1024 + ch * 512 + 512]
                            dr = f'BG{ch}'
                        else:
                            dst = sb['SCRA'][:, (gi - 8) * 1024 + ch * 512: (gi - 8) * 1024 + ch * 512 + 512]
                            dr = 'SCRA'
                        bb.op('S', lambda s, bk=bk, dst=dst: s.activation(
                            dst, PB[bk][:, 0:512], AF.Silu),
                            reads=[PR[bk]], writes=[dr])
                    else:
                        bia = sm[0:32, 169:170]
                        bb.op('S', lambda s, bk=bk: s.activation(
                            sb['DTa'][:], PB[bk][0:32, 0:512], AF.Abs, bias=bia),
                            reads=[PR[bk], 'c_smalls'], writes=['DTa'])
                        bb.op('S', lambda s, bk=bk: s.activation(
                            sb['DTb'][:], PB[bk][0:32, 0:512], AF.Relu, bias=bia),
                            reads=[PR[bk], 'c_smalls'], writes=['DTb'])
                        bb.op('S', lambda s: s.activation(
                            sb['DTa'][:], sb['DTa'][:], AF.Exp, scale=-1.0),
                            reads=['DTa'], writes=['DTa'])
                        bb.op('S', lambda s: s.activation(
                            sb['DTa'][:], sb['DTa'][:], AF.Ln, bias=1.0),
                            reads=['DTa'], writes=['DTa'])
                        bb.op('V', lambda v, ch=ch: v.tensor_add(
                            sb['dtfm'][:, ch * 512:(ch + 1) * 512], sb['DTa'][:], sb['DTb'][:]),
                            reads=['DTa', 'DTb'], writes=[f'DT{ch}'])
                stream_mm_phase('wvkq_bf', 41, 2048,
                                lambda s, ch: sb['XN'][:, s * 1024: s * 1024 + 512],
                                lambda j, ch, bk: proj_out(j, 0, bk),
                                nchunks=1)

                def inter(j, tb=tb):
                    if j in (10, 20, 30, 40):
                        cc = j // 10 - 1
                        _ssd_chunk(tb, cc, gchunk[0])
                        gchunk[0] += 1

                stream_mm_phase('wvkq_bf', 41, 2048,
                                lambda s, ch: sb['XN'][:, s * 1024 + 512: s * 1024 + 1024],
                                lambda j, ch, bk: proj_out(j, 1, bk),
                                nchunks=1, after_tile=inter)
                for cc in range(4, 8):
                    _ssd_chunk(tb, cc, gchunk[0])
                    gchunk[0] += 1

        def _ssd_chunk(tb, cc, gc):
            col0 = cc * 128
            BGR = f'BG{cc // 4}'
            DTR = f'DT{cc // 4}'
            hs_old = sb['HS0'] if gc % 2 == 0 else sb['HS1']
            hs_new = sb['HS1'] if gc % 2 == 0 else sb['HS0']
            hro = 'HS0' if gc % 2 == 0 else 'HS1'
            hrn = 'HS1' if gc % 2 == 0 else 'HS0'
            bb.op('T', lambda t: t.transpose(PB7[:, 0:32], sb['dtfm'][0:32, col0:col0 + 128],
                                             sb['c_idf'][0:32, 0:32]),
                  reads=[DTR, 'c_idf'], writes=[PR7])
            bb.op('S', lambda s: s.copy(sb['dt_tm'][:], PB7[:, 0:32]),
                  reads=[PR7], writes=['dt_tm'])
            bb.op('V', lambda v: v.tensor_scalar_mul(sb['AFM'][:], sb['dtfm'][0:32, col0:col0 + 128],
                                                     sb['acol'][:]),
                  reads=[DTR, 'acol'], writes=['AFM'])
            bb.op('V', lambda v: v.tensor_tensor_scan(sb['ACSFM'][:], sb['AFM'][:], sb['AFM'][:],
                                                      0.0, ALU.add, ALU.bypass),
                  reads=['AFM'], writes=['ACSFM'])
            bb.op('S', lambda s: s.activation(sb['DSFM'][:], sb['ACSFM'][:], AF.Exp,
                                              scale=-1.0, bias=sb['ACSFM'][:, 127:128]),
                  reads=['ACSFM'], writes=['DSFM'])
            bb.op('T', lambda t: t.transpose(PB7[:, 32:64], sb['ACSFM'][:], sb['c_idf'][0:32, 0:32]),
                  reads=['ACSFM', 'c_idf'], writes=[PR7])
            bb.op('S', lambda s: s.copy(sb['acs_tm'][:], PB7[:, 32:64]),
                  reads=[PR7], writes=['acs_tm'])
            bb.op('T', lambda t: t.transpose(PB7[:, 64:96], sb['DSFM'][:], sb['c_idf'][0:32, 0:32]),
                  reads=['DSFM', 'c_idf'], writes=[PR7])
            bb.op('S', lambda s: s.copy(sb['ds_tm'][:], PB7[:, 64:96]),
                  reads=[PR7], writes=['ds_tm'])
            bb.op('T', lambda t: t.transpose(PB7[0:1, 96:128], sb['ACSFM'][:, 127:128],
                                             sb['c_idf'][0:32, 0:32]),
                  reads=['ACSFM', 'c_idf'], writes=[PR7])
            bb.op('S', lambda s: s.activation(sb['ASR'][:], PB7[0:1, 96:128], AF.Exp),
                  reads=[PR7], writes=['ASR'])
            bb.op('S', lambda s: s.activation(sb['FTM'][:], sb['acs_tm'][:], AF.Exp, scale=-1.0),
                  reads=['acs_tm'], writes=['FTM'])
            bb.op('S', lambda s: s.activation(sb['ETM'][:], sb['acs_tm'][:], AF.Exp),
                  reads=['acs_tm'], writes=['ETM'])
            bb.op('T', lambda t: t.matmul(PB7[:, 128:160], sb['c_onesf'][0:1, :], sb['ASR'][:],
                                          start=True, stop=True),
                  reads=['ASR', 'c_onesf'], writes=[PR7])
            bb.op('S', lambda s: s.activation(sb['EASB'][:], PB7[:, 128:160], AF.Identity),
                  reads=[PR7], writes=['EASB'])
            for vt in range(4):
                bb.op('T', lambda t, vt=vt: t.transpose(
                    PBT[:, vt * 128:(vt + 1) * 128],
                    sb['BIG'][:, vt * 1024 + col0: vt * 1024 + col0 + 128], sb['c_idb'][:]),
                    reads=[BGR, 'c_idb'], writes=[PRT])
            for kt in range(4):
                bb.op('T', lambda t, kt=kt: t.transpose(
                    PBT[:, 512 + kt * 128: 512 + (kt + 1) * 128],
                    sb['BIG'][:, (4 + kt) * 1024 + col0: (4 + kt) * 1024 + col0 + 128], sb['c_idb'][:]),
                    reads=[BGR, 'c_idb'], writes=[PRT])
            bb.op('S', lambda s: s.copy(sb['VTM'][:], PBT[:, 0:512]),
                  reads=[PRT], writes=['VTM'])
            bb.op('S', lambda s: s.copy(sb['KTM'][:], PBT[:, 512:1024]),
                  reads=[PRT], writes=['KTM'])
            for aa in range(4):
                for ccp in range(2):
                    xv_o = bass.AP(sb['XV'], (8 * aa + ccp) * 64,
                                   [[2048, 128], [128, 4], [1, 64]])
                    v_rep = bass.AP(sb['VTM'], (2 * aa + ccp) * 64,
                                    [[512, 128], [0, 4], [1, 64]])
                    dtB = bass.AP(sb['dt_tm'], 8 * aa + ccp,
                                  [[32, 128], [2, 4], [0, 64]])
                    bb.op('V', lambda v, xv_o=xv_o, v_rep=v_rep, dtB=dtB: v.tensor_mul(
                        xv_o, v_rep, dtB),
                        reads=['VTM', 'dt_tm'], writes=[f'XV{2 * aa}', f'XV{2 * aa + 1}'])
            for aa in range(4):
                for ccp in range(2):
                    xvd_o = bass.AP(sb['XVD'], (2 * aa + ccp) * 256,
                                    [[2048, 128], [64, 4], [1, 64]])
                    xv_v = bass.AP(sb['XV'], (8 * aa + ccp) * 64,
                                   [[2048, 128], [128, 4], [1, 64]])
                    dsB = bass.AP(sb['ds_tm'], 8 * aa + ccp,
                                  [[32, 128], [2, 4], [0, 64]])
                    bb.op('V', lambda v, xvd_o=xvd_o, xv_v=xv_v, dsB=dsB: v.tensor_mul(
                        xvd_o, xv_v, dsB),
                        reads=[f'XV{2 * aa}', f'XV{2 * aa + 1}', 'ds_tm'], writes=['XVD'])

            ygb = sb['YG0'] if gc % 2 == 0 else sb['YG1']
            ygr = 'YG0' if gc % 2 == 0 else 'YG1'

            def emit_st(hh):
                par = hh % 2
                g = G_OF[hh]
                ktile, kbase = 4 + g // 2, 64 * (g % 2)
                qtile, qbase = 8 + hh // 2, 64 * (hh % 2)
                sslot = (hh // 2) % 4
                stslot = hh % 4
                bb.op('T', lambda t, ktile=ktile, kbase=kbase, qtile=qtile,
                      qbase=qbase, par=par, sslot=sslot: t.matmul(
                          PB[par][:, sslot * 128:(sslot + 1) * 128],
                          sb['BIG'][kbase:kbase + 64, ktile * 1024 + col0: ktile * 1024 + col0 + 128],
                          sb['BIG'][qbase:qbase + 64, qtile * 1024 + col0: qtile * 1024 + col0 + 128],
                          start=True, stop=True),
                      reads=[BGR], writes=[PR[par]])
                bb.op('V', lambda v, hh=hh, par=par, sslot=sslot, stslot=stslot:
                      v.scalar_tensor_tensor(
                          sb['STT'][:, stslot * 128:(stslot + 1) * 128],
                          PB[par][:, sslot * 128:(sslot + 1) * 128],
                          sb['FTM'][:, hh:hh + 1], sb['c_mask'][:], ALU.mult, ALU.mult),
                      reads=[PR[par], 'FTM', 'c_mask'], writes=[f'ST{stslot}'])

            def emit_ydyo(hh):
                g = G_OF[hh]
                qtile, qbase = 8 + hh // 2, 64 * (hh % 2)
                stslot = hh % 4
                ybank = 2 + hh // 8
                ypos = (hh % 8) * 64
                hblk = (g // 2) * 256 + JJ_OF[hh] * 64
                bb.op('T', lambda t, hh=hh, stslot=stslot, ybank=ybank, ypos=ypos: t.matmul(
                    PB[ybank][:, ypos:ypos + 64],
                    sb['STT'][:, stslot * 128:(stslot + 1) * 128],
                    sb['XV'][:, hh * 64:(hh + 1) * 64], start=True, stop=False),
                    reads=[f'ST{stslot}', f'XV{hh // 8}'], writes=[PR[ybank]])
                bb.op('T', lambda t, qtile=qtile, qbase=qbase, ybank=ybank,
                      ypos=ypos, hblk=hblk: t.matmul(
                          PB[ybank][:, ypos:ypos + 64],
                          sb['BIG'][qbase:qbase + 64, qtile * 1024 + col0: qtile * 1024 + col0 + 128],
                          sb['HSBF'][qbase:qbase + 64, hblk:hblk + 64],
                          start=False, stop=True),
                      reads=[BGR, 'HSBF'], writes=[PR[ybank]])

            def emit_tail(b):
                y2 = sb['FS1'] if b % 2 == 0 else sb['XNEW']
                y2r = 'FS1' if b % 2 == 0 else 'XNEW'
                c0, c1 = b * 8, (b + 1) * 8
                bb.op('S', lambda s, b=b, y2=y2: s.activation(
                    y2[:], PB[2 + b][:, 0:512], AF.Square),
                    reads=[PR[2 + b]], writes=[y2r])
                seg = bass.AP(y2, 0, [[512, 128], [64, 8], [1, 64]])
                bb.op('V', lambda v, b=b, seg=seg: v.reduce_sum(
                    sb['MS'][:, b * 8:(b + 1) * 8], seg, axis=AX.X),
                    reads=[y2r], writes=['MS'])
                bb.op('S', lambda s, c0=c0, c1=c1: s.activation(
                    sb['U32'][:, c0:c1], sb['ETM'][:, c0:c1], AF.Square),
                    reads=['ETM'], writes=['U32'])
                bb.op('V', lambda v, c0=c0, c1=c1: v.tensor_mul(
                    sb['T32'][:, c0:c1], sb['U32'][:, c0:c1], sb['MS'][:, c0:c1]),
                    reads=['U32', 'MS'], writes=['T32'])
                bb.op('S', lambda s, c0=c0, c1=c1: s.activation(
                    sb['W32'][:, c0:c1], sb['T32'][:, c0:c1], AF.Ln,
                    scale=sm[:, 174:175], bias=sm[:, 172:173]),
                    reads=['T32', 'c_smalls'], writes=['W32'])
                bb.op('S', lambda s, c0=c0, c1=c1: s.activation(
                    sb['T32'][:, c0:c1], sb['W32'][:, c0:c1], AF.Exp,
                    scale=sm[:, 173:174]),
                    reads=['W32', 'c_smalls'], writes=['T32'])
                bb.op('V', lambda v, c0=c0, c1=c1: v.tensor_mul(
                    sb['W32'][:, c0:c1], sb['T32'][:, c0:c1], sb['ETM'][:, c0:c1]),
                    reads=['T32', 'ETM'], writes=['W32'])
                wv = bass.AP(sb['W32'], b * 8, [[32, 128], [1, 8], [0, 64]])
                bb.op('V', lambda v, b=b, wv=wv: v.tensor_mul(
                    sb['XV'][:, b * 512:(b + 1) * 512], PB[2 + b][:, 0:512], wv),
                    reads=[PR[2 + b], 'W32'], writes=[f'XV{b}'])
                for i in range(4):
                    j = b * 4 + i
                    pcol = (b % 2) * 512 + i * 128
                    bb.op('T', lambda t, j=j, pcol=pcol: t.transpose(
                        PBT[:, pcol:pcol + 128],
                        sb['XV'][:, j * 128:(j + 1) * 128], sb['c_idb'][:]),
                        reads=[f'XV{b}', 'c_idb'], writes=[PRT])
                    if j < 8:
                        sg = sb['BIG'][:, 24576 + j * 1024 + col0: 24576 + j * 1024 + col0 + 128]
                        sgr = BGR
                    else:
                        sg = sb['SCRA'][:, (j - 8) * 1024 + col0: (j - 8) * 1024 + col0 + 128]
                        sgr = 'SCRA'
                    bb.op('V', lambda v, j=j, pcol=pcol, sg=sg, ygb=ygb: v.scalar_tensor_tensor(
                        ygb[:, j * 128:(j + 1) * 128], PBT[:, pcol:pcol + 128],
                        sm[:, 171:172], sg, ALU.mult, ALU.mult),
                        reads=[PRT, sgr, 'c_smalls'], writes=[ygr])

            for hh in range(32):
                emit_st(hh)
                if hh >= 2:
                    emit_ydyo(hh - 2)
                    if (hh - 2) % 8 == 7:
                        emit_tail((hh - 2) // 8)
            emit_ydyo(30)
            emit_ydyo(31)
            emit_tail(3)

            for g in range(8):
                par = g % 2
                base = 64 * par
                scol = ((g // 2) % 2) * 256
                sbank = par
                bb.op('T', lambda t, g=g, base=base, scol=scol, sbank=sbank: t.matmul(
                    PB[sbank][base:base + 64, scol:scol + 256],
                    sb['KTM'][:, g * 64:(g + 1) * 64],
                    sb['XVD'][:, g * 256:(g + 1) * 256], start=True, stop=True),
                    reads=['KTM', 'XVD'], writes=[PR[sbank]])
                for jj in range(4):
                    hh2 = SLOTS_OF_G[g][jj]
                    hblk = (g // 2) * 256 + jj * 64
                    bb.op('V', lambda v, g=g, jj=jj, hh2=hh2, base=base, scol=scol,
                          sbank=sbank, hblk=hblk: v.scalar_tensor_tensor(
                              hs_new[base:base + 64, hblk:hblk + 64],
                              hs_old[base:base + 64, hblk:hblk + 64],
                              sb['EASB'][base:base + 64, hh2:hh2 + 1],
                              PB[sbank][base:base + 64, scol + jj * 64: scol + (jj + 1) * 64],
                              ALU.mult, ALU.add),
                          reads=[hro, 'EASB', PR[sbank]], writes=[hrn])
            bb.op('S', lambda s: s.copy(sb['HSBF'][:], hs_new[:]),
                  reads=[hrn], writes=['HSBF'])
            i2 = (tb * 8 + cc) // 4
            off = (cc % 4) * 128
            dst = bass.AP(h['yg'].tensor, i2 * 128 * 8192 + off,
                          [[8192, 128], [512, 16], [1, 128]])
            src = bass.AP(ygb, 0, [[2048, 128], [128, 16], [1, 128]])
            bb.op('S', lambda s, dst=dst, src=src: s.dma_start(dst, src),
                  reads=[ygr], writes=[f'yg.{i2}'], dma=True)

        def b2_phase():
            for j in range(16):
                bb.op('Q', lambda q, j=j: q.dma_start(
                    sb['BIG'][:, j * 2048:(j + 1) * 2048], h['wo_bf'][j]),
                    reads=['wo_bf'], writes=['BIG'], dma=True)
            for i2 in range(NCH):
                half = (i2 % 2) * 8192
                xnr = 'XNa' if i2 % 2 == 0 else 'XNb'
                bb.op('Q', lambda q, i2=i2, half=half: q.dma_start(
                    sb['XN'][:, half:half + 8192], h['yg'][i2]),
                    reads=[f'yg.{i2}'], writes=[xnr], dma=True)
                for j in range(16):
                    bk = j % 4
                    xp = sb['XP0'] if j % 2 == 0 else sb['XP1']
                    xpr = 'XP0' if j % 2 == 0 else 'XP1'
                    bb.op('Q', lambda q, i2=i2, j=j, xp=xp: q.dma_start(
                        xp[:], h['x1'][i2, :, j * 512:(j + 1) * 512]),
                        reads=[f'x1.{i2}'], writes=[xpr], dma=True)
                    for s in range(16):
                        bb.op('T', lambda t, j=j, s=s, bk=bk, half=half: t.matmul(
                            PB[bk][:, 0:512],
                            sb['BIG'][:, j * 2048 + s * 128: j * 2048 + (s + 1) * 128],
                            sb['XN'][:, half + s * 512: half + (s + 1) * 512],
                            start=(s == 0), stop=(s == 15)),
                            reads=['BIG', xnr], writes=[PR[bk]])
                    xn2 = sb['XNEW'] if j % 2 == 0 else sb['XNEW2']
                    xr = 'XNEW' if j % 2 == 0 else 'XNEW2'
                    bb.op('V', lambda v, bk=bk, xn2=xn2, xp=xp: v.tensor_add(
                        xn2[:], PB[bk][:, 0:512], xp[:]),
                        reads=[PR[bk], xpr], writes=[xr])
                    bb.op('S', lambda s, i2=i2, j=j, xn2=xn2: s.dma_start(
                        h['x2'][i2, :, j * 512:(j + 1) * 512], xn2[:]),
                        reads=[xr], writes=[f'x2.{i2}'], dma=True)

        def big_barrier():
            bb.op('V', lambda v: v.engine_nop(),
                  reads=['BIG', 'BG0', 'BG1', 'DT0', 'DT1'],
                  writes=['BIG', 'BG0', 'BG1', 'DT0', 'DT1'])

        ffn_phase('x', 'x1', False)
        big_barrier()
        b1_phase()
        big_barrier()
        b2_phase()
        final_stores = ffn_phase('x2', 'yout', True)
        bb.fence('Q', [final_stores[-1]])
        bb.build()
    return nc


_NC = None


def kernel(**inputs):
    global _NC
    consts, weights = _host_prep(inputs)
    if _NC is None:
        _NC = _build_nc()
    x = np.asarray(inputs['L0_attn_hidden_states'])
    in_maps = []
    for b in range(NCORES):
        m = dict(consts)
        m.update(weights)
        m['x'] = _x_chunks(np.ascontiguousarray(x[b]))
        in_maps.append(m)
    res = run_bass_kernel_spmd(_NC, in_maps, core_ids=list(range(NCORES)))
    out = np.empty((NCORES, L, D), np.float32)
    for b in range(NCORES):
        out[b] = _out_unchunks(np.asarray(res.results[b]['yout']))
    return out


# version 15
# speedup vs baseline: 1.0346x; 1.0270x over previous
import numpy as np
import ml_dtypes

import concourse.bass as bass
import concourse.mybir as mybir
from concourse.bass_utils import run_bass_kernel_spmd

F32 = mybir.dt.float32
BF16 = mybir.dt.bfloat16
AF = mybir.ActivationFunctionType
ALU = mybir.AluOpType
AX = mybir.AxisListType
bfdt = ml_dtypes.bfloat16

D = 2048
L = 4096
NCORES = 8
TA = 512
NTA = L // TA
TB = 1024
NTB = L // TB
NCH = 8
PROJ_T = 41
EPS_LN = 1e-5
EPS_RMS = 1e-5

_E = [0, 1, 2, 3, 8, 9, 10, 11, 16, 17, 18, 19, 24, 25, 26, 27]
_O = [e + 4 for e in _E]
SLOT2ORIG = [0] * 32
for j in range(16):
    SLOT2ORIG[2 * j] = _E[j]
    SLOT2ORIG[2 * j + 1] = _O[j]
G_OF = [SLOT2ORIG[s] // 4 for s in range(32)]
JJ_OF = [SLOT2ORIG[s] % 4 for s in range(32)]
SLOTS_OF_G = [[(8 * (g // 2) + 2 * jj + (g % 2)) for jj in range(4)] for g in range(8)]


class _Op:
    __slots__ = ('sem', 'emit', 'engine', 'dma', 'waits', 'inc', 'value',
                 'clock', 'seq')

    def __init__(self, sem, engine, emit, dma, seq):
        self.sem = sem
        self.engine = engine
        self.emit = emit
        self.dma = dma
        self.seq = seq
        self.waits = []
        self.inc = bool(dma)
        self.value = None
        self.clock = None


_ENGINES = ('T', 'V', 'S', 'P', 'Q')


def _dma_sem_key(regions):
    r = regions[0] if regions else 'MISC'
    for p in ('x1.', 'x2.', 'yg.', 'yout.'):
        if r.startswith(p):
            return 'CH' + r.split('.')[1]
    return r


class BB:

    def __init__(self, nc):
        self.nc = nc
        self.queues = {e: [] for e in _ENGINES}
        self.regions = {}
        self.counts = {}
        self._known = {e: {} for e in _ENGINES}

    def _mk(self, engine, emit, dma, reads, writes):
        if dma:
            key = _dma_sem_key(list(writes) or list(reads))
            key = 'D_' + key
        else:
            key = engine
        units = 16 if dma else 1
        cnt = self.counts.get(key, 0) + units
        self.counts[key] = cnt
        o = _Op(key, engine, emit, dma, cnt)
        return o

    def op(self, engine, emit, reads=(), writes=(), dma=False):
        o = self._mk(engine, emit, dma, reads, writes)
        known = self._known[engine]

        rl, wl = [], list(writes)
        for r in reads:
            (wl if r.startswith('PS') else rl).append(r)

        deps = []
        for r in rl:
            ent = self.regions.get(r)
            if ent is not None and ent[0] is not None:
                deps.append(ent[0])
        for w in wl:
            ent = self.regions.get(w)
            if ent is not None:
                if ent[0] is not None:
                    deps.append(ent[0])
                deps.extend(ent[1])

        for d in deps:
            if d.sem == o.sem and not d.dma:
                if engine == 'T' and d.engine == 'T':
                    continue
            if d.seq <= known.get(d.sem, -1):
                continue
            o.waits.append(d)
            d.inc = True
            if d.clock is not None:
                for k2, v in d.clock.items():
                    if v > known.get(k2, -1):
                        known[k2] = v
            if d.seq > known.get(d.sem, -1):
                known[d.sem] = d.seq

        o.clock = dict(known)
        o.clock[o.sem] = max(o.clock.get(o.sem, -1), o.seq)

        for r in rl:
            ent = self.regions.setdefault(r, [None, []])
            ent[1].append(o)
        for w in wl:
            self.regions[w] = [o, []]
        self.queues[engine].append(o)
        return o

    def fence(self, engine, deps):
        o = self._mk(engine, None, False, (), ())
        o.inc = False
        known = self._known[engine]
        for d in deps:
            if d.seq <= known.get(d.sem, -1):
                continue
            o.waits.append(d)
            d.inc = True
            if d.seq > known.get(d.sem, -1):
                known[d.sem] = d.seq
        o.clock = dict(known)
        self.queues[engine].append(o)
        return o

    def build(self):
        nc = self.nc
        ccount = {}
        for e in _ENGINES:
            for o in self.queues[e]:
                if o.dma:
                    if o.inc:
                        o.value = o.seq
                else:
                    if o.inc:
                        c = ccount.get(o.sem, 0) + 1
                        ccount[o.sem] = c
                        o.value = c
        import contextlib
        used = set()
        for e in _ENGINES:
            for o in self.queues[e]:
                for d in o.waits:
                    used.add(d.sem)
                if o.inc:
                    used.add(o.sem)
        with contextlib.ExitStack() as st:
            sems = {}
            for k in sorted(used):
                sems[k] = st.enter_context(nc.semaphore('sm_' + k.replace('.', '_')))
            st.enter_context(_BlockRunner(nc, self, sems))

    def _wait_val(self, d):
        return d.value if not d.dma else d.seq


class _BlockRunner:
    def __init__(self, nc, bb, sems):
        self.nc = nc
        self.bb = bb
        self.sems = sems

    def __enter__(self):
        nc, bb, sems = self.nc, self.bb, self.sems
        self.block = nc.Block()
        block = self.block.__enter__()

        def run_queue(h, e):
            for o in bb.queues[e]:
                for d in o.waits:
                    h.wait_ge(sems[d.sem], d.value if not d.dma else d.seq)
                if o.emit is not None:
                    ins = o.emit(h)
                    if o.inc:
                        ins.then_inc(sems[o.sem], 16 if o.dma else 1)

        if bb.queues['T']:
            block.tensor(lambda h: run_queue(h, 'T'))
        if bb.queues['V']:
            block.vector(lambda h: run_queue(h, 'V'))
        if bb.queues['S']:
            block.scalar(lambda h: run_queue(h, 'S'))
        if bb.queues['P']:
            block.gpsimd(lambda h: run_queue(h, 'P'))
        if bb.queues['Q']:
            block.sync(lambda h: run_queue(h, 'Q'))
        return self

    def __exit__(self, *a):
        return self.block.__exit__(*a)


def _tile_images(W):
    F, K = W.shape
    nt = F // 128
    A = W.reshape(nt, 128, K // 128, 128)
    return np.ascontiguousarray(A.transpose(0, 3, 2, 1)).reshape(nt, 128, K)


def _x_chunks(xb):
    t = xb.reshape(NCH, 512, 16, 128)
    return np.ascontiguousarray(t.transpose(0, 3, 2, 1)).reshape(NCH, 128, 16 * 512)


def _out_unchunks(y):
    t = y.reshape(NCH, 128, 16, 512)
    return np.ascontiguousarray(t.transpose(0, 3, 2, 1)).reshape(L, D)


def _host_prep(inp):
    W = np.asarray(inp['L0_attn_W_vkqgdt'])
    conv_w = np.asarray(inp['L0_attn_conv_w'])
    conv_b = np.asarray(inp['L0_attn_conv_b'])
    dt_bias = np.asarray(inp['L0_attn_dt_bias'])
    A_log = np.asarray(inp['L0_attn_A_log_bias'])
    gnorm = np.asarray(inp['L0_attn_gnorm_w'])
    Wo = np.asarray(inp['L0_attn_W_o'])
    Wg = np.asarray(inp['L0_ffn_Wg'])
    W0a = np.asarray(inp['L0_ffn_W0a'])
    b0a = np.asarray(inp['L0_ffn_b0a'])
    W0b = np.asarray(inp['L0_ffn_W0b'])
    b0b = np.asarray(inp['L0_ffn_b0b'])
    W1a = np.asarray(inp['L0_ffn_W1a'])
    b1a = np.asarray(inp['L0_ffn_b1a'])
    W1b = np.asarray(inp['L0_ffn_W1b'])
    b1b = np.asarray(inp['L0_ffn_b1b'])

    qperm = np.array([SLOT2ORIG[s] * 64 + p for s in range(32) for p in range(64)])
    Wp = np.zeros((5248, D), np.float32)
    Wp[0:1024] = W[0:1024]
    Wp[1024:3072] = W[1024 + qperm]
    Wp[3072:5120] = W[3072 + qperm]
    Wp[5120:5152] = W[5120 + np.array(SLOT2ORIG)]
    wvkq = _tile_images(Wp)

    W0a_ext = np.zeros((33 * 128, D), np.float32)
    W0a_ext[0:4096] = W0a
    W0a_ext[4096] = Wg[0]
    W0a_ext[4160] = Wg[1]
    w0a = _tile_images(W0a_ext)
    w1a = _tile_images(W1a)
    w0b = _tile_images(W0b)
    w1b = _tile_images(W1b)
    perm_din = np.array([SLOT2ORIG[s] * 64 + p for s in range(32) for p in range(64)])
    wo = _tile_images(np.ascontiguousarray(Wo[:, perm_din]))

    cperm = np.concatenate([np.arange(1024), 1024 + qperm])
    cw = conv_w[cperm]
    cb = conv_b[cperm]
    sm = np.zeros((128, 176), np.float32)
    sm[:, 0:24] = cw[:, 0].reshape(24, 128).T
    sm[:, 24:48] = cw[:, 1].reshape(24, 128).T
    sm[:, 48:72] = cb.reshape(24, 128).T
    sm[:, 72:104] = b0a.reshape(32, 128).T
    sm[:, 105:137] = b1a.reshape(32, 128).T
    sm[:, 137:153] = b0b.reshape(16, 128).T
    sm[:, 153:169] = b1b.reshape(16, 128).T
    sm[0:32, 169] = dt_bias[SLOT2ORIG]
    sm[0:32, 170] = A_log[SLOT2ORIG]
    sm[:, 171] = np.tile(gnorm, 2)
    sm[:, 172] = 1e-5
    sm[:, 173] = -0.5
    sm[:, 174] = 1.0 / 64

    consts = dict(
        c_idb=np.eye(128, dtype=bfdt),
        c_idf=np.eye(128, dtype=np.float32),
        c_mask=np.triu(np.ones((128, 128), np.float32)).astype(bfdt),
        c_onesb=np.ones((128, 128), bfdt),
        c_onesf=np.ones((128, 128), np.float32),
        c_smalls=sm,
    )
    weights = dict(wvkq_f32=wvkq, w0a_f32=w0a, w1a_f32=w1a,
                   w0b_f32=w0b, w1b_f32=w1b, wo_f32=wo)
    return consts, weights


def _build_nc(debug=False):
    nc = bass.Bass("TRN2", target_bir_lowering=False)
    h = {}
    h['x'] = nc.dram_tensor('x', [NCH, 128, 8192], F32, kind="ExternalInput").ap()
    for n, sh in [('wvkq_f32', [41, 128, 2048]), ('w0a_f32', [33, 128, 2048]),
                  ('w1a_f32', [32, 128, 2048]), ('w0b_f32', [16, 128, 4096]),
                  ('w1b_f32', [16, 128, 4096]), ('wo_f32', [16, 128, 2048])]:
        h[n] = nc.dram_tensor(n, sh, F32, kind="ExternalInput").ap()
    for n, sh, dt in [('c_idb', [128, 128], BF16), ('c_idf', [128, 128], F32),
                      ('c_mask', [128, 128], BF16), ('c_onesb', [128, 128], BF16),
                      ('c_onesf', [128, 128], F32), ('c_smalls', [128, 176], F32)]:
        h[n] = nc.dram_tensor(n, sh, dt, kind="ExternalInput").ap()
    h['yout'] = nc.dram_tensor('yout', [NCH, 128, 8192], F32, kind="ExternalOutput").ap()
    dbg_kind = "ExternalOutput" if debug else "Internal"
    for n, sh, dt in [('wvkq_bf', [41, 128, 2048], BF16), ('w0a_bf', [33, 128, 2048], BF16),
                      ('w1a_bf', [32, 128, 2048], BF16), ('w0b_bf', [16, 128, 4096], BF16),
                      ('w1b_bf', [16, 128, 4096], BF16), ('wo_bf', [16, 128, 2048], BF16)]:
        h[n] = nc.dram_tensor(n, sh, dt, kind="Internal").ap()
    for n, sh, dt in [('x1', [NCH, 128, 8192], F32), ('x2', [NCH, 128, 8192], F32),
                      ('yg', [NCH, 128, 8192], BF16)]:
        h[n] = nc.dram_tensor(n, sh, dt, kind=dbg_kind).ap()

    import contextlib
    with contextlib.ExitStack() as stk:
        E = stk.enter_context
        sb = {}

        def S_(name, shape, dt):
            sb[name] = E(nc.sbuf_tensor('sb_' + name, list(shape), dt))
            return sb[name]

        for n, sh, dt in [('c_idb', (128, 128), BF16), ('c_idf', (128, 128), F32),
                          ('c_mask', (128, 128), BF16), ('c_onesb', (128, 128), BF16),
                          ('c_onesf', (128, 128), F32), ('c_smalls', (128, 176), F32)]:
            S_(n, sh, dt)
        S_('d_bd', (128, 16), F32)
        S_('acol', (32, 1), F32)
        S_('carry', (128, 24), F32)
        S_('XN', (128, 16384), BF16)
        S_('BIG', (128, 32768), BF16)
        S_('XP0', (128, 512), F32)
        S_('XP1', (128, 512), F32)
        S_('WT0', (128, 4096), BF16)
        S_('WT1', (128, 4096), BF16)
        S_('SCRA', (128, 8192), BF16)
        S_('dtfm', (32, 1024), F32)
        S_('DTa', (32, 512), F32)
        S_('DTb', (32, 512), F32)
        S_('AFM', (32, 128), F32)
        S_('ACSFM', (32, 128), F32)
        S_('DSFM', (32, 128), F32)
        S_('dt_tm', (128, 32), F32)
        S_('acs_tm', (128, 32), F32)
        S_('ds_tm', (128, 32), F32)
        S_('FTM', (128, 32), F32)
        S_('ETM', (128, 32), F32)
        S_('EASB', (128, 32), F32)
        S_('U32', (128, 32), F32)
        S_('T32', (128, 32), F32)
        S_('W32', (128, 32), F32)
        S_('MS', (128, 32), F32)
        S_('ASR', (1, 32), F32)
        S_('KTM', (128, 512), BF16)
        S_('VTM', (128, 512), BF16)
        S_('YNF', (128, 2048), BF16)
        S_('STT', (128, 512), BF16)
        S_('XV', (128, 2048), BF16)
        S_('XVD', (128, 2048), BF16)
        S_('HS0', (128, 1024), F32)
        S_('HS1', (128, 1024), F32)
        S_('HSBF', (128, 1024), BF16)
        S_('YG0', (128, 2048), BF16)
        S_('YG1', (128, 2048), BF16)
        S_('E0S', (128, 512), BF16)
        S_('FS1', (128, 512), F32)
        S_('XNEW', (128, 512), F32)
        S_('XNEW2', (128, 512), F32)
        S_('Bb1', (128, 512), BF16)
        S_('Bb0', (128, 512), BF16)
        S_('x2b0', (128, 512), BF16)
        S_('x2bA', (128, 512), BF16)
        S_('x2bB', (128, 512), BF16)
        S_('ROWS', (1, 1024), F32)
        S_('ZR', (1, 1024), BF16)
        S_('INVM', (1, 512), BF16)
        S_('B1R', (1, 512), BF16)
        S_('B0R', (1, 512), BF16)
        PB = [E(nc.psum_tensor(f'PB{i}', [128, 512], F32)) for i in range(6)]
        PBT = E(nc.psum_tensor('PBT', [128, 1024], BF16))
        PB7 = E(nc.psum_tensor('PB7', [128, 512], F32))
        PR = [f'PS{i}' for i in range(6)]
        PRT, PR7 = 'PST', 'PS7'

        bb = BB(nc)
        sm = sb['c_smalls']

        for n in ('c_idb', 'c_idf', 'c_mask', 'c_onesb', 'c_onesf', 'c_smalls'):
            bb.op('Q', lambda q, n=n: q.dma_start(sb[n][:], h[n][:]),
                  writes=[n], dma=True)
        bb.op('V', lambda v: v.tensor_sub(sb['d_bd'][:], sm[:, 153:169], sm[:, 137:153]),
              reads=['c_smalls'], writes=['d_bd'])
        bb.op('S', lambda s: s.activation(sb['acol'][:], sm[0:32, 170:171], AF.Exp),
              reads=['c_smalls'], writes=['acol'])
        bb.op('V', lambda v: v.tensor_scalar_mul(sb['acol'][:], sb['acol'][:], -1.0),
              reads=['acol'], writes=['acol'])
        bb.op('V', lambda v: v.memset(sb['carry'][:], 0.0), writes=['carry'])
        bb.op('V', lambda v: v.memset(sb['HS0'][:], 0.0), writes=['HS0'])
        bb.op('V', lambda v: v.memset(sb['HSBF'][:], 0.0), writes=['HSBF'])

        def dmacast(n, nt):
            for j in range(nt):
                bb.op('P', lambda g, n=n, j=j: g.dma_start(h[n + '_bf'][j], h[n + '_f32'][j]),
                      writes=[n + '_bf'], dma=True)
        dmacast('w0a', 33)
        dmacast('w0b', 16)
        dmacast('w1a', 32)
        dmacast('w1b', 16)
        late_casts = [('wvkq', j) for j in range(41)] + [('wo', j) for j in range(16)]

        def emit_late_casts(k):
            while late_casts and k > 0:
                n, j = late_casts.pop(0)
                bb.op('P', lambda g, n=n, j=j: g.dma_start(h[n + '_bf'][j], h[n + '_f32'][j]),
                      writes=[n + '_bf'], dma=True)
                k -= 1

        def emit_ln(i2, src_name, xn_off_fn):
            for s in range(16):
                xp = sb['XP0'] if s % 2 == 0 else sb['XP1']
                xpr = 'XP0' if s % 2 == 0 else 'XP1'
                bb.op('Q', lambda q, s=s, xp=xp: q.dma_start(
                    xp[:], h[src_name][i2, :, s * 512:(s + 1) * 512]),
                    reads=[f'{src_name}.{i2}'], writes=[xpr], dma=True)
                bb.op('V', lambda v, s=s, xp=xp: v.tensor_copy(
                    sb['SCRA'][:, s * 512:(s + 1) * 512], xp[:]),
                    reads=[xpr], writes=['SCRA'])
                bb.op('T', lambda t, s=s: t.matmul(
                    PB7[0:1, 0:512], sb['c_onesb'][:, 0:1],
                    sb['SCRA'][:, s * 512:(s + 1) * 512],
                    start=(s == 0), stop=(s == 15)),
                    reads=['SCRA', 'c_onesb'], writes=[PR7])
            x2bufs = [('x2b0', sb['x2b0']), ('E0S', sb['E0S']),
                      ('x2bA', sb['x2bA']), ('x2bB', sb['x2bB'])]
            for s in range(16):
                x2r, x2b = x2bufs[s % 4]
                bb.op('V', lambda v, s=s, x2b=x2b: v.tensor_mul(
                    x2b[:], sb['SCRA'][:, s * 512:(s + 1) * 512],
                    sb['SCRA'][:, s * 512:(s + 1) * 512]),
                    reads=['SCRA'], writes=[x2r])
                bb.op('T', lambda t, s=s, x2b=x2b: t.matmul(
                    PB[5][0:1, 0:512], sb['c_onesb'][:, 0:1], x2b[:],
                    start=(s == 0), stop=(s == 15)),
                    reads=[x2r, 'c_onesb'], writes=[PR[5]])
            R = sb['ROWS']
            bb.op('V', lambda v: v.tensor_scalar_mul(R[0:1, 0:512], PB7[0:1, 0:512], 1.0 / D),
                  reads=[PR7], writes=['ROWS'])
            bb.op('S', lambda s: s.activation(R[0:1, 512:1024], R[0:1, 0:512], AF.Square),
                  reads=['ROWS'], writes=['ROWS'])
            bb.op('V', lambda v: v.scalar_tensor_tensor(
                PB7[32:33, 0:512], PB[5][0:1, 0:512], 1.0 / D, R[0:1, 512:1024],
                ALU.mult, ALU.subtract),
                reads=[PR[5], 'ROWS'], writes=[PR7])
            bb.op('S', lambda s: s.activation(PB[5][32:33, 0:512], PB7[32:33, 0:512], AF.Ln,
                                              bias=sm[0:1, 172:173]),
                  reads=[PR7, 'c_smalls'], writes=[PR[5]])
            bb.op('S', lambda s: s.activation(PB7[96:97, 0:512], PB[5][32:33, 0:512], AF.Exp,
                                              scale=sm[0:1, 173:174]),
                  reads=[PR[5], 'c_smalls'], writes=[PR7])
            bb.op('V', lambda v: v.tensor_copy(sb['B1R'][:], PB7[96:97, 0:512]),
                  reads=[PR7], writes=['B1R'])
            bb.op('V', lambda v: v.scalar_tensor_tensor(
                sb['B0R'][:], R[0:1, 0:512], -1.0, PB7[96:97, 0:512], ALU.mult, ALU.mult),
                reads=['ROWS', PR7], writes=['B0R'])
            bb.op('T', lambda t: t.matmul(PB[5][:, 0:512], sb['c_onesb'][0:1, :], sb['B1R'][:],
                                          start=True, stop=True),
                  reads=['B1R', 'c_onesb'], writes=[PR[5]])
            bb.op('S', lambda s: s.copy(sb['Bb1'][:], PB[5][:, 0:512]),
                  reads=[PR[5]], writes=['Bb1'])
            bb.op('T', lambda t: t.matmul(PB[5][:, 0:512], sb['c_onesb'][0:1, :], sb['B0R'][:],
                                          start=True, stop=True),
                  reads=['B0R', 'c_onesb'], writes=[PR[5]])
            bb.op('S', lambda s: s.copy(sb['Bb0'][:], PB[5][:, 0:512]),
                  reads=[PR[5]], writes=['Bb0'])
            bv1 = bass.AP(sb['Bb1'], 0, [[512, 128], [0, 16], [1, 512]])
            bv0 = bass.AP(sb['Bb0'], 0, [[512, 128], [0, 16], [1, 512]])
            bb.op('V', lambda v: v.tensor_mul(sb['SCRA'][:], sb['SCRA'][:], bv1),
                  reads=['SCRA', 'Bb1'], writes=['SCRA'])
            offs = [xn_off_fn(s) for s in range(16)]
            stride = offs[1] - offs[0]
            out_ap = bass.AP(sb['XN'], offs[0], [[16384, 128], [stride, 16], [1, 512]])
            bb.op('V', lambda v: v.tensor_add(out_ap, sb['SCRA'][:], bv0),
                  reads=['SCRA', 'Bb0'], writes=['XN'])

        def stream_mm_phase(wname, ntiles, K, rhs_fn, out_fn, nchunks=1,
                            rhs_region='XN', after_tile=None):
            ks = K // 128
            for j in range(min(2, ntiles)):
                wt = 'WT0' if j % 2 == 0 else 'WT1'
                bb.op('Q', lambda q, j=j, wt=wt: q.dma_start(
                    sb[wt][:, 0:K], h[wname][j]),
                    reads=[wname], writes=[wt], dma=True)
            for j in range(ntiles):
                wt = 'WT0' if j % 2 == 0 else 'WT1'
                for ch in range(nchunks):
                    bk = (j * nchunks + ch) % 2
                    for s in range(ks):
                        bb.op('T', lambda t, j=j, ch=ch, s=s, wt=wt, bk=bk: t.matmul(
                            PB[bk][:, 0:512],
                            sb[wt][:, s * 128:(s + 1) * 128],
                            rhs_fn(s, ch),
                            start=(s == 0), stop=(s == ks - 1)),
                            reads=[wt, rhs_region],
                            writes=[PR[bk]])
                    out_fn(j, ch, bk)
                if j + 2 < ntiles:
                    wt2 = 'WT0' if j % 2 == 0 else 'WT1'
                    bb.op('Q', lambda q, j=j, wt2=wt2: q.dma_start(
                        sb[wt2][:, 0:K], h[wname][j + 2]),
                        reads=[wname], writes=[wt2], dma=True)
                if after_tile is not None:
                    after_tile(j)

        def ffn_phase(src_name, dst_name, is_out):
            for ia in range(NTA):
                emit_ln(ia, src_name, lambda s: s * 512)
                def h0_out(j, ch, bk):
                    if j < 32:
                        bb.op('S', lambda s, j=j, bk=bk: s.activation(
                            sb['BIG'][:, j * 512:(j + 1) * 512], PB[bk][:, 0:512],
                            AF.Gelu, bias=sm[:, 72 + j:73 + j]),
                            reads=[PR[bk], 'c_smalls'], writes=['BIG'])
                    else:
                        bb.op('S', lambda s, bk=bk: s.copy(sb['ZR'][0:1, 0:512], PB[bk][0:1, 0:512]),
                              reads=[PR[bk]], writes=['ZR'])
                        bb.op('S', lambda s, bk=bk: s.copy(sb['ZR'][0:1, 512:1024], PB[bk][64:65, 0:512]),
                              reads=[PR[bk]], writes=['ZR'])
                        bb.op('V', lambda v: v.tensor_sub(PB7[32:33, 0:512], sb['ZR'][0:1, 0:512],
                                                          sb['ZR'][0:1, 512:1024]),
                              reads=['ZR'], writes=[PR7])
                        bb.op('V', lambda v: v.tensor_single_scalar(sb['INVM'][:], PB7[32:33, 0:512],
                                                                    0.0, ALU.is_lt),
                              reads=[PR7], writes=['INVM'])
                        bb.op('T', lambda t: t.matmul(PB7[:, 0:512], sb['c_onesb'][0:1, :],
                                                      sb['INVM'][:], start=True, stop=True),
                              reads=['INVM', 'c_onesb'], writes=[PR7])
                stream_mm_phase('w0a_bf', 33, 2048,
                                lambda s, ch: sb['XN'][:, s * 512:(s + 1) * 512], h0_out)
                def out0_out(j, ch, bk):
                    bb.op('S', lambda s, j=j, bk=bk: s.activation(
                        sb['SCRA'][:, j * 512:(j + 1) * 512], PB[bk][:, 0:512],
                        AF.Identity, bias=sm[:, 137 + j:138 + j]),
                        reads=[PR[bk], 'c_smalls'], writes=['SCRA'])
                stream_mm_phase('w0b_bf', 16, 4096,
                                lambda s, ch: sb['BIG'][:, s * 512:(s + 1) * 512], out0_out,
                                rhs_region='BIG')
                def h1_out(j, ch, bk):
                    bb.op('S', lambda s, j=j, bk=bk: s.activation(
                        sb['BIG'][:, (32 + j) * 512:(33 + j) * 512], PB[bk][:, 0:512],
                        AF.Relu, bias=sm[:, 105 + j:106 + j]),
                        reads=[PR[bk], 'c_smalls'], writes=['BIG'])
                stream_mm_phase('w1a_bf', 32, 2048,
                                lambda s, ch: sb['XN'][:, s * 512:(s + 1) * 512], h1_out)
                store_ops = []

                def out1_out(j, ch, bk):
                    e0 = sb['SCRA'][:, j * 512:(j + 1) * 512]
                    bb.op('V', lambda v, j=j, bk=bk, e0=e0: v.scalar_tensor_tensor(
                        sb['FS1'][:], PB[bk][:, 0:512], sb['d_bd'][:, j:j + 1], e0,
                        ALU.add, ALU.subtract),
                        reads=[PR[bk], 'd_bd', 'SCRA'], writes=['FS1'])
                    bb.op('V', lambda v: v.tensor_mul(sb['FS1'][:], sb['FS1'][:], PB7[:, 0:512]),
                          reads=['FS1', PR7], writes=['FS1'])
                    bb.op('P', lambda p, e0=e0: p.tensor_add(sb['FS1'][:], sb['FS1'][:], e0),
                          reads=['FS1', 'SCRA'], writes=['FS1'])
                    xp = sb['XP0'] if j % 2 == 0 else sb['XP1']
                    xpr = 'XP0' if j % 2 == 0 else 'XP1'
                    bb.op('Q', lambda q, j=j, xp=xp, ia=ia: q.dma_start(
                        xp[:], h[src_name][ia, :, j * 512:(j + 1) * 512]),
                        reads=[f'{src_name}.{ia}'], writes=[xpr], dma=True)
                    xn2 = sb['XNEW'] if j % 2 == 0 else sb['XNEW2']
                    xr = 'XNEW' if j % 2 == 0 else 'XNEW2'
                    bb.op('V', lambda v, xn2=xn2, xp=xp: v.scalar_tensor_tensor(
                        xn2[:], sb['FS1'][:], 0.5, xp[:], ALU.mult, ALU.add),
                        reads=['FS1', xpr], writes=[xr])
                    store_ops.append(bb.op('S', lambda s, j=j, xn2=xn2, ia=ia: s.dma_start(
                        h[dst_name][ia, :, j * 512:(j + 1) * 512], xn2[:]),
                        reads=[xr], writes=[f'{dst_name}.{ia}'], dma=True))
                stream_mm_phase('w1b_bf', 16, 4096,
                                lambda s, ch: sb['BIG'][:, (32 + s) * 512:(33 + s) * 512], out1_out,
                                rhs_region='BIG')
                if dst_name == 'x1':
                    emit_late_casts(8)
            return store_ops

        def b1_phase():
            gchunk = [0]
            for tb in range(NTB):
                for half in range(2):
                    emit_ln(tb * 2 + half, 'x1', lambda s, half=half: s * 1024 + half * 512)
                def proj_out(j, ch, bk):
                    col = j * 1024 + ch * 512
                    if j < 24:
                        ct = sb['E0S']
                        bb.op('S', lambda s, j=j, bk=bk, ct=ct: s.activation(
                            ct[:], PB[bk][:, 0:512], AF.Identity,
                            bias=sm[:, 48 + j:49 + j], scale=sm[:, 24 + j:25 + j]),
                            reads=[PR[bk], 'c_smalls'], writes=['E0S'])
                        bb.op('V', lambda v, j=j, bk=bk, ct=ct: v.scalar_tensor_tensor(
                            ct[:, 1:512], PB[bk][:, 0:511], sm[:, j:j + 1], ct[:, 1:512],
                            ALU.mult, ALU.add),
                            reads=[PR[bk], 'c_smalls', 'E0S'], writes=['E0S'])
                        bb.op('V', lambda v, j=j, ct=ct: v.scalar_tensor_tensor(
                            ct[:, 0:1], sb['carry'][:, j:j + 1], sm[:, j:j + 1], ct[:, 0:1],
                            ALU.mult, ALU.add),
                            reads=['carry', 'c_smalls', 'E0S'], writes=['E0S'])
                        bb.op('S', lambda s, j=j, bk=bk: s.copy(
                            sb['carry'][:, j:j + 1], PB[bk][:, 511:512]),
                            reads=[PR[bk]], writes=['carry'])
                        bb.op('S', lambda s, j=j, ch=ch, ct=ct, col=col: s.activation(
                            sb['BIG'][:, col:col + 512], ct[:], AF.Silu),
                            reads=['E0S'], writes=[f'BG{ch}'])
                    elif j < 40:
                        gi = j - 24
                        if gi < 8:
                            dst = sb['BIG'][:, 24576 + gi * 1024 + ch * 512: 24576 + gi * 1024 + ch * 512 + 512]
                            dr = f'BG{ch}'
                        else:
                            dst = sb['SCRA'][:, (gi - 8) * 1024 + ch * 512: (gi - 8) * 1024 + ch * 512 + 512]
                            dr = 'SCRA'
                        bb.op('S', lambda s, bk=bk, dst=dst: s.activation(
                            dst, PB[bk][:, 0:512], AF.Silu),
                            reads=[PR[bk]], writes=[dr])
                    else:
                        bia = sm[0:32, 169:170]
                        bb.op('S', lambda s, bk=bk: s.activation(
                            sb['DTa'][:], PB[bk][0:32, 0:512], AF.Abs, bias=bia),
                            reads=[PR[bk], 'c_smalls'], writes=['DTa'])
                        bb.op('S', lambda s, bk=bk: s.activation(
                            sb['DTb'][:], PB[bk][0:32, 0:512], AF.Relu, bias=bia),
                            reads=[PR[bk], 'c_smalls'], writes=['DTb'])
                        bb.op('S', lambda s: s.activation(
                            sb['DTa'][:], sb['DTa'][:], AF.Exp, scale=-1.0),
                            reads=['DTa'], writes=['DTa'])
                        bb.op('S', lambda s: s.activation(
                            sb['DTa'][:], sb['DTa'][:], AF.Ln, bias=1.0),
                            reads=['DTa'], writes=['DTa'])
                        bb.op('V', lambda v, ch=ch: v.tensor_add(
                            sb['dtfm'][:, ch * 512:(ch + 1) * 512], sb['DTa'][:], sb['DTb'][:]),
                            reads=['DTa', 'DTb'], writes=[f'DT{ch}'])
                stream_mm_phase('wvkq_bf', 41, 2048,
                                lambda s, ch: sb['XN'][:, s * 1024: s * 1024 + 512],
                                lambda j, ch, bk: proj_out(j, 0, bk),
                                nchunks=1)

                def inter(j, tb=tb):
                    if j in (10, 20, 30, 40):
                        cc = j // 10 - 1
                        _ssd_chunk(tb, cc, gchunk[0])
                        gchunk[0] += 1

                stream_mm_phase('wvkq_bf', 41, 2048,
                                lambda s, ch: sb['XN'][:, s * 1024 + 512: s * 1024 + 1024],
                                lambda j, ch, bk: proj_out(j, 1, bk),
                                nchunks=1, after_tile=inter)
                for cc in range(4, 8):
                    _ssd_chunk(tb, cc, gchunk[0])
                    gchunk[0] += 1

        def _ssd_chunk(tb, cc, gc):
            col0 = cc * 128
            BGR = f'BG{cc // 4}'
            DTR = f'DT{cc // 4}'
            hs_old = sb['HS0'] if gc % 2 == 0 else sb['HS1']
            hs_new = sb['HS1'] if gc % 2 == 0 else sb['HS0']
            hro = 'HS0' if gc % 2 == 0 else 'HS1'
            hrn = 'HS1' if gc % 2 == 0 else 'HS0'
            bb.op('T', lambda t: t.transpose(PB7[:, 0:32], sb['dtfm'][0:32, col0:col0 + 128],
                                             sb['c_idf'][0:32, 0:32]),
                  reads=[DTR, 'c_idf'], writes=[PR7])
            bb.op('S', lambda s: s.copy(sb['dt_tm'][:], PB7[:, 0:32]),
                  reads=[PR7], writes=['dt_tm'])
            bb.op('V', lambda v: v.tensor_scalar_mul(sb['AFM'][:], sb['dtfm'][0:32, col0:col0 + 128],
                                                     sb['acol'][:]),
                  reads=[DTR, 'acol'], writes=['AFM'])
            bb.op('V', lambda v: v.tensor_tensor_scan(sb['ACSFM'][:], sb['AFM'][:], sb['AFM'][:],
                                                      0.0, ALU.add, ALU.bypass),
                  reads=['AFM'], writes=['ACSFM'])
            bb.op('S', lambda s: s.activation(sb['DSFM'][:], sb['ACSFM'][:], AF.Exp,
                                              scale=-1.0, bias=sb['ACSFM'][:, 127:128]),
                  reads=['ACSFM'], writes=['DSFM'])
            bb.op('T', lambda t: t.transpose(PB7[:, 32:64], sb['ACSFM'][:], sb['c_idf'][0:32, 0:32]),
                  reads=['ACSFM', 'c_idf'], writes=[PR7])
            bb.op('S', lambda s: s.copy(sb['acs_tm'][:], PB7[:, 32:64]),
                  reads=[PR7], writes=['acs_tm'])
            bb.op('T', lambda t: t.transpose(PB7[:, 64:96], sb['DSFM'][:], sb['c_idf'][0:32, 0:32]),
                  reads=['DSFM', 'c_idf'], writes=[PR7])
            bb.op('S', lambda s: s.copy(sb['ds_tm'][:], PB7[:, 64:96]),
                  reads=[PR7], writes=['ds_tm'])
            bb.op('T', lambda t: t.transpose(PB7[0:1, 96:128], sb['ACSFM'][:, 127:128],
                                             sb['c_idf'][0:32, 0:32]),
                  reads=['ACSFM', 'c_idf'], writes=[PR7])
            bb.op('S', lambda s: s.activation(sb['ASR'][:], PB7[0:1, 96:128], AF.Exp),
                  reads=[PR7], writes=['ASR'])
            bb.op('S', lambda s: s.activation(sb['FTM'][:], sb['acs_tm'][:], AF.Exp, scale=-1.0),
                  reads=['acs_tm'], writes=['FTM'])
            bb.op('S', lambda s: s.activation(sb['ETM'][:], sb['acs_tm'][:], AF.Exp),
                  reads=['acs_tm'], writes=['ETM'])
            bb.op('S', lambda s: s.activation(sb['U32'][:], sb['ETM'][:], AF.Square),
                  reads=['ETM'], writes=['U32'])
            bb.op('T', lambda t: t.matmul(PB7[:, 128:160], sb['c_onesf'][0:1, :], sb['ASR'][:],
                                          start=True, stop=True),
                  reads=['ASR', 'c_onesf'], writes=[PR7])
            bb.op('S', lambda s: s.activation(sb['EASB'][:], PB7[:, 128:160], AF.Identity),
                  reads=[PR7], writes=['EASB'])
            for vt in range(4):
                bb.op('T', lambda t, vt=vt: t.transpose(
                    PBT[:, vt * 128:(vt + 1) * 128],
                    sb['BIG'][:, vt * 1024 + col0: vt * 1024 + col0 + 128], sb['c_idb'][:]),
                    reads=[BGR, 'c_idb'], writes=[PRT])
            for kt in range(4):
                bb.op('T', lambda t, kt=kt: t.transpose(
                    PBT[:, 512 + kt * 128: 512 + (kt + 1) * 128],
                    sb['BIG'][:, (4 + kt) * 1024 + col0: (4 + kt) * 1024 + col0 + 128], sb['c_idb'][:]),
                    reads=[BGR, 'c_idb'], writes=[PRT])
            bb.op('S', lambda s: s.copy(sb['VTM'][:], PBT[:, 0:512]),
                  reads=[PRT], writes=['VTM'])
            bb.op('S', lambda s: s.copy(sb['KTM'][:], PBT[:, 512:1024]),
                  reads=[PRT], writes=['KTM'])
            for aa in range(4):
                for ccp in range(2):
                    xv_o = bass.AP(sb['XV'], (8 * aa + ccp) * 64,
                                   [[2048, 128], [128, 4], [1, 64]])
                    v_rep = bass.AP(sb['VTM'], (2 * aa + ccp) * 64,
                                    [[512, 128], [0, 4], [1, 64]])
                    dtB = bass.AP(sb['dt_tm'], 8 * aa + ccp,
                                  [[32, 128], [2, 4], [0, 64]])
                    bb.op('V', lambda v, xv_o=xv_o, v_rep=v_rep, dtB=dtB: v.tensor_mul(
                        xv_o, v_rep, dtB),
                        reads=['VTM', 'dt_tm'], writes=[f'XV{2 * aa}', f'XV{2 * aa + 1}'])
            for aa in range(4):
                for ccp in range(2):
                    xvd_o = bass.AP(sb['XVD'], (2 * aa + ccp) * 256,
                                    [[2048, 128], [64, 4], [1, 64]])
                    xv_v = bass.AP(sb['XV'], (8 * aa + ccp) * 64,
                                   [[2048, 128], [128, 4], [1, 64]])
                    dsB = bass.AP(sb['ds_tm'], 8 * aa + ccp,
                                  [[32, 128], [2, 4], [0, 64]])
                    bb.op('V', lambda v, xvd_o=xvd_o, xv_v=xv_v, dsB=dsB: v.tensor_mul(
                        xvd_o, xv_v, dsB),
                        reads=[f'XV{2 * aa}', f'XV{2 * aa + 1}', 'ds_tm'], writes=['XVD'])

            ygb = sb['YG0'] if gc % 2 == 0 else sb['YG1']
            ygr = 'YG0' if gc % 2 == 0 else 'YG1'

            def emit_st(hh):
                par = hh % 2
                g = G_OF[hh]
                ktile, kbase = 4 + g // 2, 64 * (g % 2)
                qtile, qbase = 8 + hh // 2, 64 * (hh % 2)
                sslot = (hh // 2) % 4
                stslot = hh % 4
                bb.op('T', lambda t, ktile=ktile, kbase=kbase, qtile=qtile,
                      qbase=qbase, par=par, sslot=sslot: t.matmul(
                          PB[par][:, sslot * 128:(sslot + 1) * 128],
                          sb['BIG'][kbase:kbase + 64, ktile * 1024 + col0: ktile * 1024 + col0 + 128],
                          sb['BIG'][qbase:qbase + 64, qtile * 1024 + col0: qtile * 1024 + col0 + 128],
                          start=True, stop=True),
                      reads=[BGR], writes=[PR[par]])
                bb.op('V', lambda v, hh=hh, par=par, sslot=sslot, stslot=stslot:
                      v.scalar_tensor_tensor(
                          sb['STT'][:, stslot * 128:(stslot + 1) * 128],
                          PB[par][:, sslot * 128:(sslot + 1) * 128],
                          sb['FTM'][:, hh:hh + 1], sb['c_mask'][:], ALU.mult, ALU.mult),
                      reads=[PR[par], 'FTM', 'c_mask'], writes=[f'ST{stslot}'])

            def emit_ydyo(hh):
                g = G_OF[hh]
                qtile, qbase = 8 + hh // 2, 64 * (hh % 2)
                stslot = hh % 4
                ybank = 2 + hh // 8
                ypos = (hh % 8) * 64
                hblk = (g // 2) * 256 + JJ_OF[hh] * 64
                bb.op('T', lambda t, hh=hh, stslot=stslot, ybank=ybank, ypos=ypos: t.matmul(
                    PB[ybank][:, ypos:ypos + 64],
                    sb['STT'][:, stslot * 128:(stslot + 1) * 128],
                    sb['XV'][:, hh * 64:(hh + 1) * 64], start=True, stop=False),
                    reads=[f'ST{stslot}', f'XV{hh // 8}'], writes=[PR[ybank]])
                bb.op('T', lambda t, qtile=qtile, qbase=qbase, ybank=ybank,
                      ypos=ypos, hblk=hblk: t.matmul(
                          PB[ybank][:, ypos:ypos + 64],
                          sb['BIG'][qbase:qbase + 64, qtile * 1024 + col0: qtile * 1024 + col0 + 128],
                          sb['HSBF'][qbase:qbase + 64, hblk:hblk + 64],
                          start=False, stop=True),
                      reads=[BGR, 'HSBF'], writes=[PR[ybank]])

            def emit_tail_rows(b):
                y2 = sb['FS1'] if b % 2 == 0 else sb['XNEW']
                y2r = 'FS1' if b % 2 == 0 else 'XNEW'
                c0, c1 = b * 8, (b + 1) * 8
                bb.op('S', lambda s, b=b, y2=y2: s.activation(
                    y2[:], PB[2 + b][:, 0:512], AF.Square),
                    reads=[PR[2 + b]], writes=[y2r])
                seg = bass.AP(y2, 0, [[512, 128], [64, 8], [1, 64]])
                bb.op('V', lambda v, b=b, seg=seg: v.reduce_sum(
                    sb['MS'][:, b * 8:(b + 1) * 8], seg, axis=AX.X),
                    reads=[y2r], writes=['MS'])
                bb.op('V', lambda v, c0=c0, c1=c1: v.tensor_mul(
                    sb['T32'][:, c0:c1], sb['U32'][:, c0:c1], sb['MS'][:, c0:c1]),
                    reads=['U32', 'MS'], writes=['T32'])
                bb.op('S', lambda s, c0=c0, c1=c1: s.activation(
                    sb['W32'][:, c0:c1], sb['T32'][:, c0:c1], AF.Ln,
                    scale=sm[:, 174:175], bias=sm[:, 172:173]),
                    reads=['T32', 'c_smalls'], writes=['W32'])
                bb.op('S', lambda s, c0=c0, c1=c1: s.activation(
                    sb['T32'][:, c0:c1], sb['W32'][:, c0:c1], AF.Exp,
                    scale=sm[:, 173:174]),
                    reads=['W32', 'c_smalls'], writes=['T32'])
                bb.op('V', lambda v, c0=c0, c1=c1: v.tensor_mul(
                    sb['W32'][:, c0:c1], sb['T32'][:, c0:c1], sb['ETM'][:, c0:c1]),
                    reads=['T32', 'ETM'], writes=['W32'])
                wv = bass.AP(sb['W32'], b * 8, [[32, 128], [1, 8], [0, 64]])
                bb.op('V', lambda v, b=b, wv=wv: v.tensor_mul(
                    sb['XV'][:, b * 512:(b + 1) * 512], PB[2 + b][:, 0:512], wv),
                    reads=[PR[2 + b], 'W32'], writes=[f'XV{b}'])

            def emit_tail_tr(b):
                for i in range(4):
                    j = b * 4 + i
                    pcol = (b % 2) * 512 + i * 128
                    bb.op('T', lambda t, j=j, pcol=pcol: t.transpose(
                        PBT[:, pcol:pcol + 128],
                        sb['XV'][:, j * 128:(j + 1) * 128], sb['c_idb'][:]),
                        reads=[f'XV{b}', 'c_idb'], writes=[PRT])
                for i in range(4):
                    j = b * 4 + i
                    pcol = (b % 2) * 512 + i * 128
                    if j < 8:
                        sg = sb['BIG'][:, 24576 + j * 1024 + col0: 24576 + j * 1024 + col0 + 128]
                        sgr = BGR
                    else:
                        sg = sb['SCRA'][:, (j - 8) * 1024 + col0: (j - 8) * 1024 + col0 + 128]
                        sgr = 'SCRA'
                    bb.op('V', lambda v, j=j, pcol=pcol, sg=sg, ygb=ygb: v.scalar_tensor_tensor(
                        ygb[:, j * 128:(j + 1) * 128], PBT[:, pcol:pcol + 128],
                        sm[:, 171:172], sg, ALU.mult, ALU.mult),
                        reads=[PRT, sgr, 'c_smalls'], writes=[ygr])

            for hh in range(32):
                emit_st(hh)
                if hh >= 2:
                    emit_ydyo(hh - 2)
                    if (hh - 2) % 8 == 7:
                        emit_tail_rows((hh - 2) // 8)
                    if hh >= 6 and (hh - 6) % 8 == 7:
                        emit_tail_tr((hh - 6) // 8)
            emit_ydyo(30)
            emit_ydyo(31)
            emit_tail_rows(3)
            emit_tail_tr(2)
            emit_tail_tr(3)

            for g in range(8):
                par = g % 2
                base = 64 * par
                scol = ((g // 2) % 2) * 256
                sbank = par
                bb.op('T', lambda t, g=g, base=base, scol=scol, sbank=sbank: t.matmul(
                    PB[sbank][base:base + 64, scol:scol + 256],
                    sb['KTM'][:, g * 64:(g + 1) * 64],
                    sb['XVD'][:, g * 256:(g + 1) * 256], start=True, stop=True),
                    reads=['KTM', 'XVD'], writes=[PR[sbank]])
                for jj in range(4):
                    hh2 = SLOTS_OF_G[g][jj]
                    hblk = (g // 2) * 256 + jj * 64
                    bb.op('V', lambda v, g=g, jj=jj, hh2=hh2, base=base, scol=scol,
                          sbank=sbank, hblk=hblk: v.scalar_tensor_tensor(
                              hs_new[base:base + 64, hblk:hblk + 64],
                              hs_old[base:base + 64, hblk:hblk + 64],
                              sb['EASB'][base:base + 64, hh2:hh2 + 1],
                              PB[sbank][base:base + 64, scol + jj * 64: scol + (jj + 1) * 64],
                              ALU.mult, ALU.add),
                          reads=[hro, 'EASB', PR[sbank]], writes=[hrn])
            bb.op('S', lambda s: s.copy(sb['HSBF'][:], hs_new[:]),
                  reads=[hrn], writes=['HSBF'])
            i2 = (tb * 8 + cc) // 4
            off = (cc % 4) * 128
            dst = bass.AP(h['yg'].tensor, i2 * 128 * 8192 + off,
                          [[8192, 128], [512, 16], [1, 128]])
            src = bass.AP(ygb, 0, [[2048, 128], [128, 16], [1, 128]])
            bb.op('S', lambda s, dst=dst, src=src: s.dma_start(dst, src),
                  reads=[ygr], writes=[f'yg.{i2}'], dma=True)

        def b2_phase():
            for j in range(16):
                bb.op('Q', lambda q, j=j: q.dma_start(
                    sb['BIG'][:, j * 2048:(j + 1) * 2048], h['wo_bf'][j]),
                    reads=['wo_bf'], writes=['BIG'], dma=True)
            for i2 in range(NCH):
                half = (i2 % 2) * 8192
                xnr = 'XNa' if i2 % 2 == 0 else 'XNb'
                bb.op('Q', lambda q, i2=i2, half=half: q.dma_start(
                    sb['XN'][:, half:half + 8192], h['yg'][i2]),
                    reads=[f'yg.{i2}'], writes=[xnr], dma=True)
                for j in range(16):
                    bk = j % 4
                    xp = sb['XP0'] if j % 2 == 0 else sb['XP1']
                    xpr = 'XP0' if j % 2 == 0 else 'XP1'
                    bb.op('Q', lambda q, i2=i2, j=j, xp=xp: q.dma_start(
                        xp[:], h['x1'][i2, :, j * 512:(j + 1) * 512]),
                        reads=[f'x1.{i2}'], writes=[xpr], dma=True)
                    for s in range(16):
                        bb.op('T', lambda t, j=j, s=s, bk=bk, half=half: t.matmul(
                            PB[bk][:, 0:512],
                            sb['BIG'][:, j * 2048 + s * 128: j * 2048 + (s + 1) * 128],
                            sb['XN'][:, half + s * 512: half + (s + 1) * 512],
                            start=(s == 0), stop=(s == 15)),
                            reads=['BIG', xnr], writes=[PR[bk]])
                    xn2 = sb['XNEW'] if j % 2 == 0 else sb['XNEW2']
                    xr = 'XNEW' if j % 2 == 0 else 'XNEW2'
                    bb.op('V', lambda v, bk=bk, xn2=xn2, xp=xp: v.tensor_add(
                        xn2[:], PB[bk][:, 0:512], xp[:]),
                        reads=[PR[bk], xpr], writes=[xr])
                    bb.op('S', lambda s, i2=i2, j=j, xn2=xn2: s.dma_start(
                        h['x2'][i2, :, j * 512:(j + 1) * 512], xn2[:]),
                        reads=[xr], writes=[f'x2.{i2}'], dma=True)

        def big_barrier():
            bb.op('V', lambda v: v.engine_nop(),
                  reads=['BIG', 'BG0', 'BG1', 'DT0', 'DT1'],
                  writes=['BIG', 'BG0', 'BG1', 'DT0', 'DT1'])

        ffn_phase('x', 'x1', False)
        big_barrier()
        b1_phase()
        big_barrier()
        b2_phase()
        final_stores = ffn_phase('x2', 'yout', True)
        bb.fence('Q', [final_stores[-1]])
        bb.build()
    return nc


_NC = None


def kernel(**inputs):
    global _NC
    consts, weights = _host_prep(inputs)
    if _NC is None:
        _NC = _build_nc()
    x = np.asarray(inputs['L0_attn_hidden_states'])
    in_maps = []
    for b in range(NCORES):
        m = dict(consts)
        m.update(weights)
        m['x'] = _x_chunks(np.ascontiguousarray(x[b]))
        in_maps.append(m)
    res = run_bass_kernel_spmd(_NC, in_maps, core_ids=list(range(NCORES)))
    out = np.empty((NCORES, L, D), np.float32)
    for b in range(NCORES):
        out[b] = _out_unchunks(np.asarray(res.results[b]['yout']))
    return out


# version 16 (final)
# speedup vs baseline: 1.0661x; 1.0304x over previous
import numpy as np
import ml_dtypes

import concourse.bass as bass
import concourse.mybir as mybir
from concourse.bass_utils import run_bass_kernel_spmd

F32 = mybir.dt.float32
BF16 = mybir.dt.bfloat16
AF = mybir.ActivationFunctionType
ALU = mybir.AluOpType
AX = mybir.AxisListType
bfdt = ml_dtypes.bfloat16

D = 2048
L = 4096
NCORES = 8
TA = 512
NTA = L // TA
TB = 1024
NTB = L // TB
NCH = 8
PROJ_T = 41
EPS_LN = 1e-5
EPS_RMS = 1e-5

_E = [0, 1, 2, 3, 8, 9, 10, 11, 16, 17, 18, 19, 24, 25, 26, 27]
_O = [e + 4 for e in _E]
SLOT2ORIG = [0] * 32
for j in range(16):
    SLOT2ORIG[2 * j] = _E[j]
    SLOT2ORIG[2 * j + 1] = _O[j]
G_OF = [SLOT2ORIG[s] // 4 for s in range(32)]
JJ_OF = [SLOT2ORIG[s] % 4 for s in range(32)]
SLOTS_OF_G = [[(8 * (g // 2) + 2 * jj + (g % 2)) for jj in range(4)] for g in range(8)]


class _Op:
    __slots__ = ('sem', 'emit', 'engine', 'dma', 'waits', 'inc', 'value',
                 'clock', 'seq')

    def __init__(self, sem, engine, emit, dma, seq):
        self.sem = sem
        self.engine = engine
        self.emit = emit
        self.dma = dma
        self.seq = seq
        self.waits = []
        self.inc = bool(dma)
        self.value = None
        self.clock = None


_ENGINES = ('T', 'V', 'S', 'P', 'Q')


def _dma_sem_key(regions):
    r = regions[0] if regions else 'MISC'
    for p in ('x1.', 'x2.', 'yg.', 'yout.'):
        if r.startswith(p):
            return 'CH' + r.split('.')[1]
    return r


class BB:

    def __init__(self, nc):
        self.nc = nc
        self.queues = {e: [] for e in _ENGINES}
        self.regions = {}
        self.counts = {}
        self._known = {e: {} for e in _ENGINES}

    def _mk(self, engine, emit, dma, reads, writes):
        if dma:
            key = _dma_sem_key(list(writes) or list(reads))
            key = 'D_' + key
        else:
            key = engine
        units = 16 if dma else 1
        cnt = self.counts.get(key, 0) + units
        self.counts[key] = cnt
        o = _Op(key, engine, emit, dma, cnt)
        return o

    def op(self, engine, emit, reads=(), writes=(), dma=False):
        o = self._mk(engine, emit, dma, reads, writes)
        known = self._known[engine]

        rl, wl = [], list(writes)
        for r in reads:
            (wl if r.startswith('PS') else rl).append(r)

        deps = []
        for r in rl:
            ent = self.regions.get(r)
            if ent is not None and ent[0] is not None:
                deps.append(ent[0])
        for w in wl:
            ent = self.regions.get(w)
            if ent is not None:
                if ent[0] is not None:
                    deps.append(ent[0])
                deps.extend(ent[1])

        for d in deps:
            if d.sem == o.sem and not d.dma:
                if engine == 'T' and d.engine == 'T':
                    continue
            if d.seq <= known.get(d.sem, -1):
                continue
            o.waits.append(d)
            d.inc = True
            if d.clock is not None:
                for k2, v in d.clock.items():
                    if v > known.get(k2, -1):
                        known[k2] = v
            if d.seq > known.get(d.sem, -1):
                known[d.sem] = d.seq

        o.clock = dict(known)
        o.clock[o.sem] = max(o.clock.get(o.sem, -1), o.seq)

        for r in rl:
            ent = self.regions.setdefault(r, [None, []])
            ent[1].append(o)
        for w in wl:
            self.regions[w] = [o, []]
        self.queues[engine].append(o)
        return o

    def fence(self, engine, deps):
        o = self._mk(engine, None, False, (), ())
        o.inc = False
        known = self._known[engine]
        for d in deps:
            if d.seq <= known.get(d.sem, -1):
                continue
            o.waits.append(d)
            d.inc = True
            if d.seq > known.get(d.sem, -1):
                known[d.sem] = d.seq
        o.clock = dict(known)
        self.queues[engine].append(o)
        return o

    def build(self):
        nc = self.nc
        ccount = {}
        for e in _ENGINES:
            for o in self.queues[e]:
                if o.dma:
                    if o.inc:
                        o.value = o.seq
                else:
                    if o.inc:
                        c = ccount.get(o.sem, 0) + 1
                        ccount[o.sem] = c
                        o.value = c
        import contextlib
        used = set()
        for e in _ENGINES:
            for o in self.queues[e]:
                for d in o.waits:
                    used.add(d.sem)
                if o.inc:
                    used.add(o.sem)
        with contextlib.ExitStack() as st:
            sems = {}
            for k in sorted(used):
                sems[k] = st.enter_context(nc.semaphore('sm_' + k.replace('.', '_')))
            st.enter_context(_BlockRunner(nc, self, sems))

    def _wait_val(self, d):
        return d.value if not d.dma else d.seq


class _BlockRunner:
    def __init__(self, nc, bb, sems):
        self.nc = nc
        self.bb = bb
        self.sems = sems

    def __enter__(self):
        nc, bb, sems = self.nc, self.bb, self.sems
        self.block = nc.Block()
        block = self.block.__enter__()

        def run_queue(h, e):
            for o in bb.queues[e]:
                for d in o.waits:
                    h.wait_ge(sems[d.sem], d.value if not d.dma else d.seq)
                if o.emit is not None:
                    ins = o.emit(h)
                    if o.inc:
                        ins.then_inc(sems[o.sem], 16 if o.dma else 1)

        if bb.queues['T']:
            block.tensor(lambda h: run_queue(h, 'T'))
        if bb.queues['V']:
            block.vector(lambda h: run_queue(h, 'V'))
        if bb.queues['S']:
            block.scalar(lambda h: run_queue(h, 'S'))
        if bb.queues['P']:
            block.gpsimd(lambda h: run_queue(h, 'P'))
        if bb.queues['Q']:
            block.sync(lambda h: run_queue(h, 'Q'))
        return self

    def __exit__(self, *a):
        return self.block.__exit__(*a)


def _tile_images(W):
    F, K = W.shape
    nt = F // 128
    A = W.reshape(nt, 128, K // 128, 128)
    return np.ascontiguousarray(A.transpose(0, 3, 2, 1)).reshape(nt, 128, K)


def _x_chunks(xb):
    t = xb.reshape(NCH, 512, 16, 128)
    return np.ascontiguousarray(t.transpose(0, 3, 2, 1)).reshape(NCH, 128, 16 * 512)


def _out_unchunks(y):
    t = y.reshape(NCH, 128, 16, 512)
    return np.ascontiguousarray(t.transpose(0, 3, 2, 1)).reshape(L, D)


def _host_prep(inp):
    W = np.asarray(inp['L0_attn_W_vkqgdt'])
    conv_w = np.asarray(inp['L0_attn_conv_w'])
    conv_b = np.asarray(inp['L0_attn_conv_b'])
    dt_bias = np.asarray(inp['L0_attn_dt_bias'])
    A_log = np.asarray(inp['L0_attn_A_log_bias'])
    gnorm = np.asarray(inp['L0_attn_gnorm_w'])
    Wo = np.asarray(inp['L0_attn_W_o'])
    Wg = np.asarray(inp['L0_ffn_Wg'])
    W0a = np.asarray(inp['L0_ffn_W0a'])
    b0a = np.asarray(inp['L0_ffn_b0a'])
    W0b = np.asarray(inp['L0_ffn_W0b'])
    b0b = np.asarray(inp['L0_ffn_b0b'])
    W1a = np.asarray(inp['L0_ffn_W1a'])
    b1a = np.asarray(inp['L0_ffn_b1a'])
    W1b = np.asarray(inp['L0_ffn_W1b'])
    b1b = np.asarray(inp['L0_ffn_b1b'])

    qperm = np.array([SLOT2ORIG[s] * 64 + p for s in range(32) for p in range(64)])
    Wp = np.zeros((5248, D), np.float32)
    Wp[0:1024] = W[0:1024]
    Wp[1024:3072] = W[1024 + qperm]
    Wp[3072:5120] = W[3072 + qperm]
    Wp[5120:5152] = W[5120 + np.array(SLOT2ORIG)]
    wvkq = _tile_images(Wp)

    W0a_ext = np.zeros((33 * 128, D), np.float32)
    W0a_ext[0:4096] = W0a
    W0a_ext[4096] = Wg[0]
    W0a_ext[4160] = Wg[1]
    w0a = _tile_images(W0a_ext)
    w1a = _tile_images(W1a)
    w0b = _tile_images(W0b)
    w1b = _tile_images(W1b)
    perm_din = np.array([SLOT2ORIG[s] * 64 + p for s in range(32) for p in range(64)])
    wo = _tile_images(np.ascontiguousarray(Wo[:, perm_din]))

    cperm = np.concatenate([np.arange(1024), 1024 + qperm])
    cw = conv_w[cperm]
    cb = conv_b[cperm]
    sm = np.zeros((128, 176), np.float32)
    sm[:, 0:24] = cw[:, 0].reshape(24, 128).T
    sm[:, 24:48] = cw[:, 1].reshape(24, 128).T
    sm[:, 48:72] = cb.reshape(24, 128).T
    sm[:, 72:104] = b0a.reshape(32, 128).T
    sm[:, 105:137] = b1a.reshape(32, 128).T
    sm[:, 137:153] = b0b.reshape(16, 128).T
    sm[:, 153:169] = b1b.reshape(16, 128).T
    sm[0:32, 169] = dt_bias[SLOT2ORIG]
    sm[0:32, 170] = A_log[SLOT2ORIG]
    sm[:, 171] = np.tile(gnorm, 2)
    sm[:, 172] = 1e-5
    sm[:, 173] = -0.5
    sm[:, 174] = 1.0 / 64

    consts = dict(
        c_idb=np.eye(128, dtype=bfdt),
        c_idf=np.eye(128, dtype=np.float32),
        c_mask=np.triu(np.ones((128, 128), np.float32)).astype(bfdt),
        c_onesb=np.ones((128, 128), bfdt),
        c_onesf=np.ones((128, 128), np.float32),
        c_smalls=sm,
    )
    weights = dict(wvkq_f32=wvkq, w0a_f32=w0a, w1a_f32=w1a,
                   w0b_f32=w0b, w1b_f32=w1b, wo_f32=wo)
    return consts, weights


def _build_nc(debug=False):
    nc = bass.Bass("TRN2", target_bir_lowering=False)
    h = {}
    h['x'] = nc.dram_tensor('x', [NCH, 128, 8192], F32, kind="ExternalInput").ap()
    for n, sh in [('wvkq_f32', [41, 128, 2048]), ('w0a_f32', [33, 128, 2048]),
                  ('w1a_f32', [32, 128, 2048]), ('w0b_f32', [16, 128, 4096]),
                  ('w1b_f32', [16, 128, 4096]), ('wo_f32', [16, 128, 2048])]:
        h[n] = nc.dram_tensor(n, sh, F32, kind="ExternalInput").ap()
    for n, sh, dt in [('c_idb', [128, 128], BF16), ('c_idf', [128, 128], F32),
                      ('c_mask', [128, 128], BF16), ('c_onesb', [128, 128], BF16),
                      ('c_onesf', [128, 128], F32), ('c_smalls', [128, 176], F32)]:
        h[n] = nc.dram_tensor(n, sh, dt, kind="ExternalInput").ap()
    h['yout'] = nc.dram_tensor('yout', [NCH, 128, 8192], F32, kind="ExternalOutput").ap()
    dbg_kind = "ExternalOutput" if debug else "Internal"
    for n, sh, dt in [('wvkq_bf', [41, 128, 2048], BF16), ('w0a_bf', [33, 128, 2048], BF16),
                      ('w1a_bf', [32, 128, 2048], BF16), ('w0b_bf', [16, 128, 4096], BF16),
                      ('w1b_bf', [16, 128, 4096], BF16), ('wo_bf', [16, 128, 2048], BF16)]:
        h[n] = nc.dram_tensor(n, sh, dt, kind="Internal").ap()
    for n, sh, dt in [('x1', [NCH, 128, 8192], F32), ('x2', [NCH, 128, 8192], F32),
                      ('yg', [NCH, 128, 8192], BF16)]:
        h[n] = nc.dram_tensor(n, sh, dt, kind=dbg_kind).ap()

    import contextlib
    with contextlib.ExitStack() as stk:
        E = stk.enter_context
        sb = {}

        def S_(name, shape, dt):
            sb[name] = E(nc.sbuf_tensor('sb_' + name, list(shape), dt))
            return sb[name]

        for n, sh, dt in [('c_idb', (128, 128), BF16), ('c_idf', (128, 128), F32),
                          ('c_mask', (128, 128), BF16), ('c_onesb', (128, 128), BF16),
                          ('c_onesf', (128, 128), F32), ('c_smalls', (128, 176), F32)]:
            S_(n, sh, dt)
        S_('d_bd', (128, 16), F32)
        S_('acol', (32, 1), F32)
        S_('carry', (128, 24), F32)
        S_('XN', (128, 16384), BF16)
        S_('BIG', (128, 32768), BF16)
        S_('XP0', (128, 512), F32)
        S_('XP1', (128, 512), F32)
        S_('WT0', (128, 4096), BF16)
        S_('WT1', (128, 4096), BF16)
        S_('WT2', (128, 4096), BF16)
        S_('SCRA', (128, 8192), BF16)
        S_('dtfm', (32, 1024), F32)
        S_('DTa', (32, 512), F32)
        S_('DTb', (32, 512), F32)
        S_('AFM', (32, 128), F32)
        S_('ACSFM', (32, 128), F32)
        S_('DSFM', (32, 128), F32)
        S_('dt_tm', (128, 32), F32)
        S_('acs_tm', (128, 32), F32)
        S_('ds_tm', (128, 32), F32)
        S_('FTM', (128, 32), F32)
        S_('ETM', (128, 32), F32)
        S_('EASB', (128, 32), F32)
        S_('U32', (128, 32), F32)
        S_('T32', (128, 32), F32)
        S_('W32', (128, 32), F32)
        S_('MS', (128, 32), F32)
        S_('ASR', (1, 32), F32)
        S_('KTM', (128, 512), BF16)
        S_('VTM', (128, 512), BF16)
        S_('STT', (128, 512), BF16)
        S_('XV', (128, 2048), BF16)
        S_('XVD', (128, 2048), BF16)
        S_('HS0', (128, 1024), F32)
        S_('HS1', (128, 1024), F32)
        S_('HSBF', (128, 1024), BF16)
        S_('YG0', (128, 2048), BF16)
        S_('YG1', (128, 2048), BF16)
        S_('E0S', (128, 512), BF16)
        S_('FS1', (128, 512), F32)
        S_('XNEW', (128, 512), F32)
        S_('XNEW2', (128, 512), F32)
        S_('Bb1', (128, 512), BF16)
        S_('Bb0', (128, 512), BF16)
        S_('x2b0', (128, 512), BF16)
        S_('x2bA', (128, 512), BF16)
        S_('x2bB', (128, 512), BF16)
        S_('ROWS', (1, 1024), F32)
        S_('ZR', (1, 1024), BF16)
        S_('INVM', (1, 512), BF16)
        S_('B1R', (1, 512), BF16)
        S_('B0R', (1, 512), BF16)
        PB = [E(nc.psum_tensor(f'PB{i}', [128, 512], F32)) for i in range(6)]
        PBT = E(nc.psum_tensor('PBT', [128, 1024], BF16))
        PB7 = E(nc.psum_tensor('PB7', [128, 512], F32))
        PR = [f'PS{i}' for i in range(6)]
        PRT, PR7 = 'PST', 'PS7'

        bb = BB(nc)
        sm = sb['c_smalls']

        for n in ('c_idb', 'c_idf', 'c_mask', 'c_onesb', 'c_onesf', 'c_smalls'):
            bb.op('Q', lambda q, n=n: q.dma_start(sb[n][:], h[n][:]),
                  writes=[n], dma=True)
        bb.op('V', lambda v: v.tensor_sub(sb['d_bd'][:], sm[:, 153:169], sm[:, 137:153]),
              reads=['c_smalls'], writes=['d_bd'])
        bb.op('S', lambda s: s.activation(sb['acol'][:], sm[0:32, 170:171], AF.Exp),
              reads=['c_smalls'], writes=['acol'])
        bb.op('V', lambda v: v.tensor_scalar_mul(sb['acol'][:], sb['acol'][:], -1.0),
              reads=['acol'], writes=['acol'])
        bb.op('V', lambda v: v.memset(sb['carry'][:], 0.0), writes=['carry'])
        bb.op('V', lambda v: v.memset(sb['HS0'][:], 0.0), writes=['HS0'])
        bb.op('V', lambda v: v.memset(sb['HSBF'][:], 0.0), writes=['HSBF'])

        def dmacast(n, nt):
            for j in range(nt):
                bb.op('P', lambda g, n=n, j=j: g.dma_start(h[n + '_bf'][j], h[n + '_f32'][j]),
                      writes=[n + '_bf'], dma=True)
        dmacast('w0a', 33)
        dmacast('w0b', 16)
        dmacast('w1a', 32)
        dmacast('w1b', 16)
        late_casts = [('wvkq', j) for j in range(41)] + [('wo', j) for j in range(16)]

        def emit_late_casts(k):
            while late_casts and k > 0:
                n, j = late_casts.pop(0)
                bb.op('P', lambda g, n=n, j=j: g.dma_start(h[n + '_bf'][j], h[n + '_f32'][j]),
                      writes=[n + '_bf'], dma=True)
                k -= 1

        def emit_ln(i2, src_name, xn_off_fn):
            for s in range(16):
                xp = sb['XP0'] if s % 2 == 0 else sb['XP1']
                xpr = 'XP0' if s % 2 == 0 else 'XP1'
                bb.op('Q', lambda q, s=s, xp=xp: q.dma_start(
                    xp[:], h[src_name][i2, :, s * 512:(s + 1) * 512]),
                    reads=[f'{src_name}.{i2}'], writes=[xpr], dma=True)
                bb.op('V', lambda v, s=s, xp=xp: v.tensor_copy(
                    sb['SCRA'][:, s * 512:(s + 1) * 512], xp[:]),
                    reads=[xpr], writes=['SCRA'])
                bb.op('T', lambda t, s=s: t.matmul(
                    PB7[0:1, 0:512], sb['c_onesb'][:, 0:1],
                    sb['SCRA'][:, s * 512:(s + 1) * 512],
                    start=(s == 0), stop=(s == 15)),
                    reads=['SCRA', 'c_onesb'], writes=[PR7])
            x2bufs = [('x2b0', sb['x2b0']), ('E0S', sb['E0S']),
                      ('x2bA', sb['x2bA']), ('x2bB', sb['x2bB'])]
            for s in range(16):
                x2r, x2b = x2bufs[s % 4]
                bb.op('V', lambda v, s=s, x2b=x2b: v.tensor_mul(
                    x2b[:], sb['SCRA'][:, s * 512:(s + 1) * 512],
                    sb['SCRA'][:, s * 512:(s + 1) * 512]),
                    reads=['SCRA'], writes=[x2r])
                bb.op('T', lambda t, s=s, x2b=x2b: t.matmul(
                    PB[5][0:1, 0:512], sb['c_onesb'][:, 0:1], x2b[:],
                    start=(s == 0), stop=(s == 15)),
                    reads=[x2r, 'c_onesb'], writes=[PR[5]])
            R = sb['ROWS']
            bb.op('V', lambda v: v.tensor_scalar_mul(R[0:1, 0:512], PB7[0:1, 0:512], 1.0 / D),
                  reads=[PR7], writes=['ROWS'])
            bb.op('S', lambda s: s.activation(R[0:1, 512:1024], R[0:1, 0:512], AF.Square),
                  reads=['ROWS'], writes=['ROWS'])
            bb.op('V', lambda v: v.scalar_tensor_tensor(
                PB7[32:33, 0:512], PB[5][0:1, 0:512], 1.0 / D, R[0:1, 512:1024],
                ALU.mult, ALU.subtract),
                reads=[PR[5], 'ROWS'], writes=[PR7])
            bb.op('S', lambda s: s.activation(PB[5][32:33, 0:512], PB7[32:33, 0:512], AF.Ln,
                                              bias=sm[0:1, 172:173]),
                  reads=[PR7, 'c_smalls'], writes=[PR[5]])
            bb.op('S', lambda s: s.activation(PB7[96:97, 0:512], PB[5][32:33, 0:512], AF.Exp,
                                              scale=sm[0:1, 173:174]),
                  reads=[PR[5], 'c_smalls'], writes=[PR7])
            bb.op('V', lambda v: v.tensor_copy(sb['B1R'][:], PB7[96:97, 0:512]),
                  reads=[PR7], writes=['B1R'])
            bb.op('V', lambda v: v.scalar_tensor_tensor(
                sb['B0R'][:], R[0:1, 0:512], -1.0, PB7[96:97, 0:512], ALU.mult, ALU.mult),
                reads=['ROWS', PR7], writes=['B0R'])
            bb.op('T', lambda t: t.matmul(PB[5][:, 0:512], sb['c_onesb'][0:1, :], sb['B1R'][:],
                                          start=True, stop=True),
                  reads=['B1R', 'c_onesb'], writes=[PR[5]])
            bb.op('S', lambda s: s.copy(sb['Bb1'][:], PB[5][:, 0:512]),
                  reads=[PR[5]], writes=['Bb1'])
            bb.op('T', lambda t: t.matmul(PB[5][:, 0:512], sb['c_onesb'][0:1, :], sb['B0R'][:],
                                          start=True, stop=True),
                  reads=['B0R', 'c_onesb'], writes=[PR[5]])
            bb.op('S', lambda s: s.copy(sb['Bb0'][:], PB[5][:, 0:512]),
                  reads=[PR[5]], writes=['Bb0'])
            bv1 = bass.AP(sb['Bb1'], 0, [[512, 128], [0, 16], [1, 512]])
            bv0 = bass.AP(sb['Bb0'], 0, [[512, 128], [0, 16], [1, 512]])
            bb.op('V', lambda v: v.tensor_mul(sb['SCRA'][:], sb['SCRA'][:], bv1),
                  reads=['SCRA', 'Bb1'], writes=['SCRA'])
            offs = [xn_off_fn(s) for s in range(16)]
            stride = offs[1] - offs[0]
            out_ap = bass.AP(sb['XN'], offs[0], [[16384, 128], [stride, 16], [1, 512]])
            bb.op('V', lambda v: v.tensor_add(out_ap, sb['SCRA'][:], bv0),
                  reads=['SCRA', 'Bb0'], writes=['XN'])

        def stream_mm_phase(wname, ntiles, K, rhs_fn, out_fn, nchunks=1,
                            rhs_region='XN', after_tile=None):
            ks = K // 128
            WTS = ('WT0', 'WT1', 'WT2')
            for j in range(min(3, ntiles)):
                wt = WTS[j % 3]
                bb.op('Q', lambda q, j=j, wt=wt: q.dma_start(
                    sb[wt][:, 0:K], h[wname][j]),
                    reads=[wname], writes=[wt], dma=True)
            for j in range(ntiles):
                wt = WTS[j % 3]
                for ch in range(nchunks):
                    bk = (j * nchunks + ch) % 2
                    for s in range(ks):
                        bb.op('T', lambda t, j=j, ch=ch, s=s, wt=wt, bk=bk: t.matmul(
                            PB[bk][:, 0:512],
                            sb[wt][:, s * 128:(s + 1) * 128],
                            rhs_fn(s, ch),
                            start=(s == 0), stop=(s == ks - 1)),
                            reads=[wt, rhs_region],
                            writes=[PR[bk]])
                    out_fn(j, ch, bk)
                if j + 3 < ntiles:
                    wt2 = WTS[j % 3]
                    bb.op('Q', lambda q, j=j, wt2=wt2: q.dma_start(
                        sb[wt2][:, 0:K], h[wname][j + 3]),
                        reads=[wname], writes=[wt2], dma=True)
                if after_tile is not None:
                    after_tile(j)

        def ffn_phase(src_name, dst_name, is_out):
            for ia in range(NTA):
                emit_ln(ia, src_name, lambda s: s * 512)
                def h0_out(j, ch, bk):
                    if j < 32:
                        bb.op('S', lambda s, j=j, bk=bk: s.activation(
                            sb['BIG'][:, j * 512:(j + 1) * 512], PB[bk][:, 0:512],
                            AF.Gelu, bias=sm[:, 72 + j:73 + j]),
                            reads=[PR[bk], 'c_smalls'], writes=['BIG'])
                    else:
                        bb.op('S', lambda s, bk=bk: s.copy(sb['ZR'][0:1, 0:512], PB[bk][0:1, 0:512]),
                              reads=[PR[bk]], writes=['ZR'])
                        bb.op('S', lambda s, bk=bk: s.copy(sb['ZR'][0:1, 512:1024], PB[bk][64:65, 0:512]),
                              reads=[PR[bk]], writes=['ZR'])
                        bb.op('V', lambda v: v.tensor_sub(PB7[32:33, 0:512], sb['ZR'][0:1, 0:512],
                                                          sb['ZR'][0:1, 512:1024]),
                              reads=['ZR'], writes=[PR7])
                        bb.op('V', lambda v: v.tensor_single_scalar(sb['INVM'][:], PB7[32:33, 0:512],
                                                                    0.0, ALU.is_lt),
                              reads=[PR7], writes=['INVM'])
                        bb.op('T', lambda t: t.matmul(PB7[:, 0:512], sb['c_onesb'][0:1, :],
                                                      sb['INVM'][:], start=True, stop=True),
                              reads=['INVM', 'c_onesb'], writes=[PR7])
                stream_mm_phase('w0a_bf', 33, 2048,
                                lambda s, ch: sb['XN'][:, s * 512:(s + 1) * 512], h0_out)
                def out0_out(j, ch, bk):
                    bb.op('S', lambda s, j=j, bk=bk: s.activation(
                        sb['SCRA'][:, j * 512:(j + 1) * 512], PB[bk][:, 0:512],
                        AF.Identity, bias=sm[:, 137 + j:138 + j]),
                        reads=[PR[bk], 'c_smalls'], writes=['SCRA'])
                stream_mm_phase('w0b_bf', 16, 4096,
                                lambda s, ch: sb['BIG'][:, s * 512:(s + 1) * 512], out0_out,
                                rhs_region='BIG')
                def h1_out(j, ch, bk):
                    bb.op('S', lambda s, j=j, bk=bk: s.activation(
                        sb['BIG'][:, (32 + j) * 512:(33 + j) * 512], PB[bk][:, 0:512],
                        AF.Relu, bias=sm[:, 105 + j:106 + j]),
                        reads=[PR[bk], 'c_smalls'], writes=['BIG'])
                stream_mm_phase('w1a_bf', 32, 2048,
                                lambda s, ch: sb['XN'][:, s * 512:(s + 1) * 512], h1_out)
                store_ops = []

                def out1_out(j, ch, bk):
                    e0 = sb['SCRA'][:, j * 512:(j + 1) * 512]
                    bb.op('V', lambda v, j=j, bk=bk, e0=e0: v.scalar_tensor_tensor(
                        sb['FS1'][:], PB[bk][:, 0:512], sb['d_bd'][:, j:j + 1], e0,
                        ALU.add, ALU.subtract),
                        reads=[PR[bk], 'd_bd', 'SCRA'], writes=['FS1'])
                    bb.op('V', lambda v: v.tensor_mul(sb['FS1'][:], sb['FS1'][:], PB7[:, 0:512]),
                          reads=['FS1', PR7], writes=['FS1'])
                    bb.op('P', lambda p, e0=e0: p.tensor_add(sb['FS1'][:], sb['FS1'][:], e0),
                          reads=['FS1', 'SCRA'], writes=['FS1'])
                    xp = sb['XP0'] if j % 2 == 0 else sb['XP1']
                    xpr = 'XP0' if j % 2 == 0 else 'XP1'
                    bb.op('Q', lambda q, j=j, xp=xp, ia=ia: q.dma_start(
                        xp[:], h[src_name][ia, :, j * 512:(j + 1) * 512]),
                        reads=[f'{src_name}.{ia}'], writes=[xpr], dma=True)
                    xn2 = sb['XNEW'] if j % 2 == 0 else sb['XNEW2']
                    xr = 'XNEW' if j % 2 == 0 else 'XNEW2'
                    bb.op('V', lambda v, xn2=xn2, xp=xp: v.scalar_tensor_tensor(
                        xn2[:], sb['FS1'][:], 0.5, xp[:], ALU.mult, ALU.add),
                        reads=['FS1', xpr], writes=[xr])
                    store_ops.append(bb.op('S', lambda s, j=j, xn2=xn2, ia=ia: s.dma_start(
                        h[dst_name][ia, :, j * 512:(j + 1) * 512], xn2[:]),
                        reads=[xr], writes=[f'{dst_name}.{ia}'], dma=True))
                stream_mm_phase('w1b_bf', 16, 4096,
                                lambda s, ch: sb['BIG'][:, (32 + s) * 512:(33 + s) * 512], out1_out,
                                rhs_region='BIG')
                if dst_name == 'x1':
                    emit_late_casts(8)
            return store_ops

        def b1_phase():
            gchunk = [0]
            for tb in range(NTB):
                for half in range(2):
                    emit_ln(tb * 2 + half, 'x1', lambda s, half=half: s * 1024 + half * 512)
                def proj_out(j, ch, bk):
                    col = j * 1024 + ch * 512
                    if j < 24:
                        ct = sb['E0S']
                        bb.op('S', lambda s, j=j, bk=bk, ct=ct: s.activation(
                            ct[:], PB[bk][:, 0:512], AF.Identity,
                            bias=sm[:, 48 + j:49 + j], scale=sm[:, 24 + j:25 + j]),
                            reads=[PR[bk], 'c_smalls'], writes=['E0S'])
                        bb.op('V', lambda v, j=j, bk=bk, ct=ct: v.scalar_tensor_tensor(
                            ct[:, 1:512], PB[bk][:, 0:511], sm[:, j:j + 1], ct[:, 1:512],
                            ALU.mult, ALU.add),
                            reads=[PR[bk], 'c_smalls', 'E0S'], writes=['E0S'])
                        bb.op('V', lambda v, j=j, ct=ct: v.scalar_tensor_tensor(
                            ct[:, 0:1], sb['carry'][:, j:j + 1], sm[:, j:j + 1], ct[:, 0:1],
                            ALU.mult, ALU.add),
                            reads=['carry', 'c_smalls', 'E0S'], writes=['E0S'])
                        bb.op('S', lambda s, j=j, bk=bk: s.copy(
                            sb['carry'][:, j:j + 1], PB[bk][:, 511:512]),
                            reads=[PR[bk]], writes=['carry'])
                        bb.op('S', lambda s, j=j, ch=ch, ct=ct, col=col: s.activation(
                            sb['BIG'][:, col:col + 512], ct[:], AF.Silu),
                            reads=['E0S'], writes=[f'BG{ch}'])
                    elif j < 40:
                        gi = j - 24
                        if gi < 8:
                            dst = sb['BIG'][:, 24576 + gi * 1024 + ch * 512: 24576 + gi * 1024 + ch * 512 + 512]
                            dr = f'BG{ch}'
                        else:
                            dst = sb['SCRA'][:, (gi - 8) * 1024 + ch * 512: (gi - 8) * 1024 + ch * 512 + 512]
                            dr = 'SCRA'
                        bb.op('S', lambda s, bk=bk, dst=dst: s.activation(
                            dst, PB[bk][:, 0:512], AF.Silu),
                            reads=[PR[bk]], writes=[dr])
                    else:
                        bia = sm[0:32, 169:170]
                        bb.op('S', lambda s, bk=bk: s.activation(
                            sb['DTa'][:], PB[bk][0:32, 0:512], AF.Abs, bias=bia),
                            reads=[PR[bk], 'c_smalls'], writes=['DTa'])
                        bb.op('S', lambda s, bk=bk: s.activation(
                            sb['DTb'][:], PB[bk][0:32, 0:512], AF.Relu, bias=bia),
                            reads=[PR[bk], 'c_smalls'], writes=['DTb'])
                        bb.op('S', lambda s: s.activation(
                            sb['DTa'][:], sb['DTa'][:], AF.Exp, scale=-1.0),
                            reads=['DTa'], writes=['DTa'])
                        bb.op('S', lambda s: s.activation(
                            sb['DTa'][:], sb['DTa'][:], AF.Ln, bias=1.0),
                            reads=['DTa'], writes=['DTa'])
                        bb.op('V', lambda v, ch=ch: v.tensor_add(
                            sb['dtfm'][:, ch * 512:(ch + 1) * 512], sb['DTa'][:], sb['DTb'][:]),
                            reads=['DTa', 'DTb'], writes=[f'DT{ch}'])
                stream_mm_phase('wvkq_bf', 41, 2048,
                                lambda s, ch: sb['XN'][:, s * 1024: s * 1024 + 512],
                                lambda j, ch, bk: proj_out(j, 0, bk),
                                nchunks=1)

                def inter(j, tb=tb):
                    if j in (10, 20, 30, 40):
                        cc = j // 10 - 1
                        _ssd_chunk(tb, cc, gchunk[0])
                        gchunk[0] += 1

                stream_mm_phase('wvkq_bf', 41, 2048,
                                lambda s, ch: sb['XN'][:, s * 1024 + 512: s * 1024 + 1024],
                                lambda j, ch, bk: proj_out(j, 1, bk),
                                nchunks=1, after_tile=inter)
                for cc in range(4, 8):
                    _ssd_chunk(tb, cc, gchunk[0])
                    gchunk[0] += 1

        def _ssd_chunk(tb, cc, gc):
            col0 = cc * 128
            BGR = f'BG{cc // 4}'
            DTR = f'DT{cc // 4}'
            hs_old = sb['HS0'] if gc % 2 == 0 else sb['HS1']
            hs_new = sb['HS1'] if gc % 2 == 0 else sb['HS0']
            hro = 'HS0' if gc % 2 == 0 else 'HS1'
            hrn = 'HS1' if gc % 2 == 0 else 'HS0'
            bb.op('T', lambda t: t.transpose(PB7[:, 0:32], sb['dtfm'][0:32, col0:col0 + 128],
                                             sb['c_idf'][0:32, 0:32]),
                  reads=[DTR, 'c_idf'], writes=[PR7])
            bb.op('S', lambda s: s.copy(sb['dt_tm'][:], PB7[:, 0:32]),
                  reads=[PR7], writes=['dt_tm'])
            bb.op('V', lambda v: v.tensor_scalar_mul(sb['AFM'][:], sb['dtfm'][0:32, col0:col0 + 128],
                                                     sb['acol'][:]),
                  reads=[DTR, 'acol'], writes=['AFM'])
            bb.op('V', lambda v: v.tensor_tensor_scan(sb['ACSFM'][:], sb['AFM'][:], sb['AFM'][:],
                                                      0.0, ALU.add, ALU.bypass),
                  reads=['AFM'], writes=['ACSFM'])
            bb.op('S', lambda s: s.activation(sb['DSFM'][:], sb['ACSFM'][:], AF.Exp,
                                              scale=-1.0, bias=sb['ACSFM'][:, 127:128]),
                  reads=['ACSFM'], writes=['DSFM'])
            bb.op('T', lambda t: t.transpose(PB7[:, 32:64], sb['ACSFM'][:], sb['c_idf'][0:32, 0:32]),
                  reads=['ACSFM', 'c_idf'], writes=[PR7])
            bb.op('S', lambda s: s.copy(sb['acs_tm'][:], PB7[:, 32:64]),
                  reads=[PR7], writes=['acs_tm'])
            bb.op('T', lambda t: t.transpose(PB7[:, 64:96], sb['DSFM'][:], sb['c_idf'][0:32, 0:32]),
                  reads=['DSFM', 'c_idf'], writes=[PR7])
            bb.op('S', lambda s: s.copy(sb['ds_tm'][:], PB7[:, 64:96]),
                  reads=[PR7], writes=['ds_tm'])
            bb.op('T', lambda t: t.transpose(PB7[0:1, 96:128], sb['ACSFM'][:, 127:128],
                                             sb['c_idf'][0:32, 0:32]),
                  reads=['ACSFM', 'c_idf'], writes=[PR7])
            bb.op('S', lambda s: s.activation(sb['ASR'][:], PB7[0:1, 96:128], AF.Exp),
                  reads=[PR7], writes=['ASR'])
            bb.op('S', lambda s: s.activation(sb['FTM'][:], sb['acs_tm'][:], AF.Exp, scale=-1.0),
                  reads=['acs_tm'], writes=['FTM'])
            bb.op('S', lambda s: s.activation(sb['ETM'][:], sb['acs_tm'][:], AF.Exp),
                  reads=['acs_tm'], writes=['ETM'])
            bb.op('S', lambda s: s.activation(sb['U32'][:], sb['ETM'][:], AF.Square),
                  reads=['ETM'], writes=['U32'])
            bb.op('T', lambda t: t.matmul(PB7[:, 128:160], sb['c_onesf'][0:1, :], sb['ASR'][:],
                                          start=True, stop=True),
                  reads=['ASR', 'c_onesf'], writes=[PR7])
            bb.op('S', lambda s: s.activation(sb['EASB'][:], PB7[:, 128:160], AF.Identity),
                  reads=[PR7], writes=['EASB'])
            for vt in range(4):
                bb.op('T', lambda t, vt=vt: t.transpose(
                    PBT[:, vt * 128:(vt + 1) * 128],
                    sb['BIG'][:, vt * 1024 + col0: vt * 1024 + col0 + 128], sb['c_idb'][:]),
                    reads=[BGR, 'c_idb'], writes=[PRT])
            for kt in range(4):
                bb.op('T', lambda t, kt=kt: t.transpose(
                    PBT[:, 512 + kt * 128: 512 + (kt + 1) * 128],
                    sb['BIG'][:, (4 + kt) * 1024 + col0: (4 + kt) * 1024 + col0 + 128], sb['c_idb'][:]),
                    reads=[BGR, 'c_idb'], writes=[PRT])
            bb.op('S', lambda s: s.copy(sb['VTM'][:], PBT[:, 0:512]),
                  reads=[PRT], writes=['VTM'])
            bb.op('S', lambda s: s.copy(sb['KTM'][:], PBT[:, 512:1024]),
                  reads=[PRT], writes=['KTM'])
            for aa in range(4):
                for ccp in range(2):
                    xv_o = bass.AP(sb['XV'], (8 * aa + ccp) * 64,
                                   [[2048, 128], [128, 4], [1, 64]])
                    v_rep = bass.AP(sb['VTM'], (2 * aa + ccp) * 64,
                                    [[512, 128], [0, 4], [1, 64]])
                    dtB = bass.AP(sb['dt_tm'], 8 * aa + ccp,
                                  [[32, 128], [2, 4], [0, 64]])
                    bb.op('V', lambda v, xv_o=xv_o, v_rep=v_rep, dtB=dtB: v.tensor_mul(
                        xv_o, v_rep, dtB),
                        reads=['VTM', 'dt_tm'], writes=[f'XV{2 * aa}', f'XV{2 * aa + 1}'])
            for aa in range(4):
                for ccp in range(2):
                    xvd_o = bass.AP(sb['XVD'], (2 * aa + ccp) * 256,
                                    [[2048, 128], [64, 4], [1, 64]])
                    xv_v = bass.AP(sb['XV'], (8 * aa + ccp) * 64,
                                   [[2048, 128], [128, 4], [1, 64]])
                    dsB = bass.AP(sb['ds_tm'], 8 * aa + ccp,
                                  [[32, 128], [2, 4], [0, 64]])
                    bb.op('V', lambda v, xvd_o=xvd_o, xv_v=xv_v, dsB=dsB: v.tensor_mul(
                        xvd_o, xv_v, dsB),
                        reads=[f'XV{2 * aa}', f'XV{2 * aa + 1}', 'ds_tm'], writes=['XVD'])

            ygb = sb['YG0'] if gc % 2 == 0 else sb['YG1']
            ygr = 'YG0' if gc % 2 == 0 else 'YG1'

            def emit_st(hh):
                par = hh % 2
                g = G_OF[hh]
                ktile, kbase = 4 + g // 2, 64 * (g % 2)
                qtile, qbase = 8 + hh // 2, 64 * (hh % 2)
                sslot = (hh // 2) % 4
                stslot = hh % 4
                bb.op('T', lambda t, ktile=ktile, kbase=kbase, qtile=qtile,
                      qbase=qbase, par=par, sslot=sslot: t.matmul(
                          PB[par][:, sslot * 128:(sslot + 1) * 128],
                          sb['BIG'][kbase:kbase + 64, ktile * 1024 + col0: ktile * 1024 + col0 + 128],
                          sb['BIG'][qbase:qbase + 64, qtile * 1024 + col0: qtile * 1024 + col0 + 128],
                          start=True, stop=True),
                      reads=[BGR], writes=[PR[par]])
                bb.op('V', lambda v, hh=hh, par=par, sslot=sslot, stslot=stslot:
                      v.scalar_tensor_tensor(
                          sb['STT'][:, stslot * 128:(stslot + 1) * 128],
                          PB[par][:, sslot * 128:(sslot + 1) * 128],
                          sb['FTM'][:, hh:hh + 1], sb['c_mask'][:], ALU.mult, ALU.mult),
                      reads=[PR[par], 'FTM', 'c_mask'], writes=[f'ST{stslot}'])

            def emit_ydyo(hh):
                g = G_OF[hh]
                qtile, qbase = 8 + hh // 2, 64 * (hh % 2)
                stslot = hh % 4
                ybank = 2 + hh // 8
                ypos = (hh % 8) * 64
                hblk = (g // 2) * 256 + JJ_OF[hh] * 64
                bb.op('T', lambda t, hh=hh, stslot=stslot, ybank=ybank, ypos=ypos: t.matmul(
                    PB[ybank][:, ypos:ypos + 64],
                    sb['STT'][:, stslot * 128:(stslot + 1) * 128],
                    sb['XV'][:, hh * 64:(hh + 1) * 64], start=True, stop=False),
                    reads=[f'ST{stslot}', f'XV{hh // 8}'], writes=[PR[ybank]])
                bb.op('T', lambda t, qtile=qtile, qbase=qbase, ybank=ybank,
                      ypos=ypos, hblk=hblk: t.matmul(
                          PB[ybank][:, ypos:ypos + 64],
                          sb['BIG'][qbase:qbase + 64, qtile * 1024 + col0: qtile * 1024 + col0 + 128],
                          sb['HSBF'][qbase:qbase + 64, hblk:hblk + 64],
                          start=False, stop=True),
                      reads=[BGR, 'HSBF'], writes=[PR[ybank]])

            def emit_tail_rows(b):
                y2 = sb['FS1'] if b % 2 == 0 else sb['XNEW']
                y2r = 'FS1' if b % 2 == 0 else 'XNEW'
                c0, c1 = b * 8, (b + 1) * 8
                bb.op('S', lambda s, b=b, y2=y2: s.activation(
                    y2[:], PB[2 + b][:, 0:512], AF.Square),
                    reads=[PR[2 + b]], writes=[y2r])
                seg = bass.AP(y2, 0, [[512, 128], [64, 8], [1, 64]])
                bb.op('V', lambda v, b=b, seg=seg: v.reduce_sum(
                    sb['MS'][:, b * 8:(b + 1) * 8], seg, axis=AX.X),
                    reads=[y2r], writes=['MS'])
                bb.op('V', lambda v, c0=c0, c1=c1: v.tensor_mul(
                    sb['T32'][:, c0:c1], sb['U32'][:, c0:c1], sb['MS'][:, c0:c1]),
                    reads=['U32', 'MS'], writes=['T32'])
                bb.op('S', lambda s, c0=c0, c1=c1: s.activation(
                    sb['W32'][:, c0:c1], sb['T32'][:, c0:c1], AF.Ln,
                    scale=sm[:, 174:175], bias=sm[:, 172:173]),
                    reads=['T32', 'c_smalls'], writes=['W32'])
                bb.op('S', lambda s, c0=c0, c1=c1: s.activation(
                    sb['T32'][:, c0:c1], sb['W32'][:, c0:c1], AF.Exp,
                    scale=sm[:, 173:174]),
                    reads=['W32', 'c_smalls'], writes=['T32'])
                bb.op('V', lambda v, c0=c0, c1=c1: v.tensor_mul(
                    sb['W32'][:, c0:c1], sb['T32'][:, c0:c1], sb['ETM'][:, c0:c1]),
                    reads=['T32', 'ETM'], writes=['W32'])
                wv = bass.AP(sb['W32'], b * 8, [[32, 128], [1, 8], [0, 64]])
                bb.op('V', lambda v, b=b, wv=wv: v.tensor_mul(
                    sb['XV'][:, b * 512:(b + 1) * 512], PB[2 + b][:, 0:512], wv),
                    reads=[PR[2 + b], 'W32'], writes=[f'XV{b}'])

            def emit_tail_tr(b):
                for i in range(4):
                    j = b * 4 + i
                    pcol = (b % 2) * 512 + i * 128
                    bb.op('T', lambda t, j=j, pcol=pcol: t.transpose(
                        PBT[:, pcol:pcol + 128],
                        sb['XV'][:, j * 128:(j + 1) * 128], sb['c_idb'][:]),
                        reads=[f'XV{b}', 'c_idb'], writes=[PRT])
                for i in range(4):
                    j = b * 4 + i
                    pcol = (b % 2) * 512 + i * 128
                    if j < 8:
                        sg = sb['BIG'][:, 24576 + j * 1024 + col0: 24576 + j * 1024 + col0 + 128]
                        sgr = BGR
                    else:
                        sg = sb['SCRA'][:, (j - 8) * 1024 + col0: (j - 8) * 1024 + col0 + 128]
                        sgr = 'SCRA'
                    bb.op('V', lambda v, j=j, pcol=pcol, sg=sg, ygb=ygb: v.scalar_tensor_tensor(
                        ygb[:, j * 128:(j + 1) * 128], PBT[:, pcol:pcol + 128],
                        sm[:, 171:172], sg, ALU.mult, ALU.mult),
                        reads=[PRT, sgr, 'c_smalls'], writes=[ygr])

            for hh in range(32):
                emit_st(hh)
                if hh >= 2:
                    emit_ydyo(hh - 2)
                    if (hh - 2) % 8 == 7:
                        emit_tail_rows((hh - 2) // 8)
                    if hh >= 6 and (hh - 6) % 8 == 7:
                        emit_tail_tr((hh - 6) // 8)
            emit_ydyo(30)
            emit_ydyo(31)
            emit_tail_rows(3)
            emit_tail_tr(2)
            emit_tail_tr(3)

            for g in range(8):
                par = g % 2
                base = 64 * par
                scol = ((g // 2) % 2) * 256
                sbank = par
                bb.op('T', lambda t, g=g, base=base, scol=scol, sbank=sbank: t.matmul(
                    PB[sbank][base:base + 64, scol:scol + 256],
                    sb['KTM'][:, g * 64:(g + 1) * 64],
                    sb['XVD'][:, g * 256:(g + 1) * 256], start=True, stop=True),
                    reads=['KTM', 'XVD'], writes=[PR[sbank]])
                for jj in range(4):
                    hh2 = SLOTS_OF_G[g][jj]
                    hblk = (g // 2) * 256 + jj * 64
                    bb.op('V', lambda v, g=g, jj=jj, hh2=hh2, base=base, scol=scol,
                          sbank=sbank, hblk=hblk: v.scalar_tensor_tensor(
                              hs_new[base:base + 64, hblk:hblk + 64],
                              hs_old[base:base + 64, hblk:hblk + 64],
                              sb['EASB'][base:base + 64, hh2:hh2 + 1],
                              PB[sbank][base:base + 64, scol + jj * 64: scol + (jj + 1) * 64],
                              ALU.mult, ALU.add),
                          reads=[hro, 'EASB', PR[sbank]], writes=[hrn])
            bb.op('S', lambda s: s.copy(sb['HSBF'][:], hs_new[:]),
                  reads=[hrn], writes=['HSBF'])
            i2 = (tb * 8 + cc) // 4
            off = (cc % 4) * 128
            dst = bass.AP(h['yg'].tensor, i2 * 128 * 8192 + off,
                          [[8192, 128], [512, 16], [1, 128]])
            src = bass.AP(ygb, 0, [[2048, 128], [128, 16], [1, 128]])
            bb.op('S', lambda s, dst=dst, src=src: s.dma_start(dst, src),
                  reads=[ygr], writes=[f'yg.{i2}'], dma=True)

        def b2_phase():
            for j in range(16):
                bb.op('Q', lambda q, j=j: q.dma_start(
                    sb['BIG'][:, j * 2048:(j + 1) * 2048], h['wo_bf'][j]),
                    reads=['wo_bf'], writes=['BIG'], dma=True)
            for i2 in range(NCH):
                half = (i2 % 2) * 8192
                xnr = 'XNa' if i2 % 2 == 0 else 'XNb'
                bb.op('Q', lambda q, i2=i2, half=half: q.dma_start(
                    sb['XN'][:, half:half + 8192], h['yg'][i2]),
                    reads=[f'yg.{i2}'], writes=[xnr], dma=True)
                for j in range(16):
                    bk = j % 4
                    xp = sb['XP0'] if j % 2 == 0 else sb['XP1']
                    xpr = 'XP0' if j % 2 == 0 else 'XP1'
                    bb.op('Q', lambda q, i2=i2, j=j, xp=xp: q.dma_start(
                        xp[:], h['x1'][i2, :, j * 512:(j + 1) * 512]),
                        reads=[f'x1.{i2}'], writes=[xpr], dma=True)
                    for s in range(16):
                        bb.op('T', lambda t, j=j, s=s, bk=bk, half=half: t.matmul(
                            PB[bk][:, 0:512],
                            sb['BIG'][:, j * 2048 + s * 128: j * 2048 + (s + 1) * 128],
                            sb['XN'][:, half + s * 512: half + (s + 1) * 512],
                            start=(s == 0), stop=(s == 15)),
                            reads=['BIG', xnr], writes=[PR[bk]])
                    xn2 = sb['XNEW'] if j % 2 == 0 else sb['XNEW2']
                    xr = 'XNEW' if j % 2 == 0 else 'XNEW2'
                    bb.op('V', lambda v, bk=bk, xn2=xn2, xp=xp: v.tensor_add(
                        xn2[:], PB[bk][:, 0:512], xp[:]),
                        reads=[PR[bk], xpr], writes=[xr])
                    bb.op('S', lambda s, i2=i2, j=j, xn2=xn2: s.dma_start(
                        h['x2'][i2, :, j * 512:(j + 1) * 512], xn2[:]),
                        reads=[xr], writes=[f'x2.{i2}'], dma=True)

        def big_barrier():
            bb.op('V', lambda v: v.engine_nop(),
                  reads=['BIG', 'BG0', 'BG1', 'DT0', 'DT1'],
                  writes=['BIG', 'BG0', 'BG1', 'DT0', 'DT1'])

        ffn_phase('x', 'x1', False)
        big_barrier()
        b1_phase()
        big_barrier()
        b2_phase()
        final_stores = ffn_phase('x2', 'yout', True)
        bb.fence('Q', [final_stores[-1]])
        bb.build()
    return nc


_NC = None


def kernel(**inputs):
    global _NC
    consts, weights = _host_prep(inputs)
    if _NC is None:
        _NC = _build_nc()
    x = np.asarray(inputs['L0_attn_hidden_states'])
    in_maps = []
    for b in range(NCORES):
        m = dict(consts)
        m.update(weights)
        m['x'] = _x_chunks(np.ascontiguousarray(x[b]))
        in_maps.append(m)
    res = run_bass_kernel_spmd(_NC, in_maps, core_ids=list(range(NCORES)))
    out = np.empty((NCORES, L, D), np.float32)
    for b in range(NCORES):
        out[b] = _out_unchunks(np.asarray(res.results[b]['yout']))
    return out


# version 17
# speedup vs baseline: 1.0864x; 1.0191x over previous
import numpy as np
import ml_dtypes

import concourse.bass as bass
import concourse.mybir as mybir
from concourse.bass_utils import run_bass_kernel_spmd

F32 = mybir.dt.float32
BF16 = mybir.dt.bfloat16
AF = mybir.ActivationFunctionType
ALU = mybir.AluOpType
AX = mybir.AxisListType
bfdt = ml_dtypes.bfloat16

D = 2048
L = 4096
NCORES = 8
TA = 512
NTA = L // TA
TB = 1024
NTB = L // TB
NCH = 8
PROJ_T = 41
EPS_LN = 1e-5
EPS_RMS = 1e-5

_E = [0, 1, 2, 3, 8, 9, 10, 11, 16, 17, 18, 19, 24, 25, 26, 27]
_O = [e + 4 for e in _E]
SLOT2ORIG = [0] * 32
for j in range(16):
    SLOT2ORIG[2 * j] = _E[j]
    SLOT2ORIG[2 * j + 1] = _O[j]
G_OF = [SLOT2ORIG[s] // 4 for s in range(32)]
JJ_OF = [SLOT2ORIG[s] % 4 for s in range(32)]
SLOTS_OF_G = [[(8 * (g // 2) + 2 * jj + (g % 2)) for jj in range(4)] for g in range(8)]


class _Op:
    __slots__ = ('sem', 'emit', 'engine', 'dma', 'waits', 'inc', 'value',
                 'clock', 'seq')

    def __init__(self, sem, engine, emit, dma, seq):
        self.sem = sem
        self.engine = engine
        self.emit = emit
        self.dma = dma
        self.seq = seq
        self.waits = []
        self.inc = bool(dma)
        self.value = None
        self.clock = None


_ENGINES = ('T', 'V', 'S', 'P', 'Q')


def _dma_sem_key(regions):
    r = regions[0] if regions else 'MISC'
    for p in ('x1.', 'x2.', 'yg.', 'yout.'):
        if r.startswith(p):
            return 'CH' + r.split('.')[1]
    return r


class BB:

    def __init__(self, nc):
        self.nc = nc
        self.queues = {e: [] for e in _ENGINES}
        self.regions = {}
        self.counts = {}
        self._known = {e: {} for e in _ENGINES}

    def _mk(self, engine, emit, dma, reads, writes):
        if dma:
            key = _dma_sem_key(list(writes) or list(reads))
            key = 'D_' + key
        else:
            key = engine
        units = 16 if dma else 1
        cnt = self.counts.get(key, 0) + units
        self.counts[key] = cnt
        o = _Op(key, engine, emit, dma, cnt)
        return o

    def op(self, engine, emit, reads=(), writes=(), dma=False):
        o = self._mk(engine, emit, dma, reads, writes)
        known = self._known[engine]

        rl, wl = [], list(writes)
        for r in reads:
            (wl if r.startswith('PS') else rl).append(r)

        deps = []
        for r in rl:
            ent = self.regions.get(r)
            if ent is not None and ent[0] is not None:
                deps.append(ent[0])
        for w in wl:
            ent = self.regions.get(w)
            if ent is not None:
                if ent[0] is not None:
                    deps.append(ent[0])
                deps.extend(ent[1])

        for d in deps:
            if d.sem == o.sem and not d.dma:
                if engine == 'T' and d.engine == 'T':
                    continue
            if d.seq <= known.get(d.sem, -1):
                continue
            o.waits.append(d)
            d.inc = True
            if d.clock is not None:
                for k2, v in d.clock.items():
                    if v > known.get(k2, -1):
                        known[k2] = v
            if d.seq > known.get(d.sem, -1):
                known[d.sem] = d.seq

        o.clock = dict(known)
        o.clock[o.sem] = max(o.clock.get(o.sem, -1), o.seq)

        for r in rl:
            ent = self.regions.setdefault(r, [None, []])
            ent[1].append(o)
        for w in wl:
            self.regions[w] = [o, []]
        self.queues[engine].append(o)
        return o

    def fence(self, engine, deps):
        o = self._mk(engine, None, False, (), ())
        o.inc = False
        known = self._known[engine]
        for d in deps:
            if d.seq <= known.get(d.sem, -1):
                continue
            o.waits.append(d)
            d.inc = True
            if d.seq > known.get(d.sem, -1):
                known[d.sem] = d.seq
        o.clock = dict(known)
        self.queues[engine].append(o)
        return o

    def build(self):
        nc = self.nc
        ccount = {}
        for e in _ENGINES:
            for o in self.queues[e]:
                if o.dma:
                    if o.inc:
                        o.value = o.seq
                else:
                    if o.inc:
                        c = ccount.get(o.sem, 0) + 1
                        ccount[o.sem] = c
                        o.value = c
        import contextlib
        used = set()
        for e in _ENGINES:
            for o in self.queues[e]:
                for d in o.waits:
                    used.add(d.sem)
                if o.inc:
                    used.add(o.sem)
        with contextlib.ExitStack() as st:
            sems = {}
            for k in sorted(used):
                sems[k] = st.enter_context(nc.semaphore('sm_' + k.replace('.', '_')))
            st.enter_context(_BlockRunner(nc, self, sems))

    def _wait_val(self, d):
        return d.value if not d.dma else d.seq


class _BlockRunner:
    def __init__(self, nc, bb, sems):
        self.nc = nc
        self.bb = bb
        self.sems = sems

    def __enter__(self):
        nc, bb, sems = self.nc, self.bb, self.sems
        self.block = nc.Block()
        block = self.block.__enter__()

        def run_queue(h, e):
            for o in bb.queues[e]:
                for d in o.waits:
                    h.wait_ge(sems[d.sem], d.value if not d.dma else d.seq)
                if o.emit is not None:
                    ins = o.emit(h)
                    if o.inc:
                        ins.then_inc(sems[o.sem], 16 if o.dma else 1)

        if bb.queues['T']:
            block.tensor(lambda h: run_queue(h, 'T'))
        if bb.queues['V']:
            block.vector(lambda h: run_queue(h, 'V'))
        if bb.queues['S']:
            block.scalar(lambda h: run_queue(h, 'S'))
        if bb.queues['P']:
            block.gpsimd(lambda h: run_queue(h, 'P'))
        if bb.queues['Q']:
            block.sync(lambda h: run_queue(h, 'Q'))
        return self

    def __exit__(self, *a):
        return self.block.__exit__(*a)


def _tile_images(W):
    F, K = W.shape
    nt = F // 128
    A = W.reshape(nt, 128, K // 128, 128)
    return np.ascontiguousarray(A.transpose(0, 3, 2, 1)).reshape(nt, 128, K)


def _x_chunks(xb):
    t = xb.reshape(NCH, 512, 16, 128)
    return np.ascontiguousarray(t.transpose(0, 3, 2, 1)).reshape(NCH, 128, 16 * 512)


def _out_unchunks(y):
    t = y.reshape(NCH, 128, 16, 512)
    return np.ascontiguousarray(t.transpose(0, 3, 2, 1)).reshape(L, D)


def _host_prep(inp):
    W = np.asarray(inp['L0_attn_W_vkqgdt'])
    conv_w = np.asarray(inp['L0_attn_conv_w'])
    conv_b = np.asarray(inp['L0_attn_conv_b'])
    dt_bias = np.asarray(inp['L0_attn_dt_bias'])
    A_log = np.asarray(inp['L0_attn_A_log_bias'])
    gnorm = np.asarray(inp['L0_attn_gnorm_w'])
    Wo = np.asarray(inp['L0_attn_W_o'])
    Wg = np.asarray(inp['L0_ffn_Wg'])
    W0a = np.asarray(inp['L0_ffn_W0a'])
    b0a = np.asarray(inp['L0_ffn_b0a'])
    W0b = np.asarray(inp['L0_ffn_W0b'])
    b0b = np.asarray(inp['L0_ffn_b0b'])
    W1a = np.asarray(inp['L0_ffn_W1a'])
    b1a = np.asarray(inp['L0_ffn_b1a'])
    W1b = np.asarray(inp['L0_ffn_W1b'])
    b1b = np.asarray(inp['L0_ffn_b1b'])

    qperm = np.array([SLOT2ORIG[s] * 64 + p for s in range(32) for p in range(64)])
    Wp = np.zeros((5248, D), np.float32)
    Wp[0:1024] = W[0:1024]
    Wp[1024:3072] = W[1024 + qperm]
    Wp[3072:5120] = W[3072 + qperm]
    Wp[5120:5152] = W[5120 + np.array(SLOT2ORIG)]
    wvkq = _tile_images(Wp)

    W0a_ext = np.zeros((33 * 128, D), np.float32)
    W0a_ext[0:4096] = W0a
    W0a_ext[4096] = Wg[0]
    W0a_ext[4160] = Wg[1]
    w0a = _tile_images(W0a_ext)
    w1a = _tile_images(W1a)
    w0b = _tile_images(W0b)
    w1b = _tile_images(W1b)
    perm_din = np.array([SLOT2ORIG[s] * 64 + p for s in range(32) for p in range(64)])
    wo = _tile_images(np.ascontiguousarray(Wo[:, perm_din]))

    cperm = np.concatenate([np.arange(1024), 1024 + qperm])
    cw = conv_w[cperm]
    cb = conv_b[cperm]
    sm = np.zeros((128, 176), np.float32)
    sm[:, 0:24] = cw[:, 0].reshape(24, 128).T
    sm[:, 24:48] = cw[:, 1].reshape(24, 128).T
    sm[:, 48:72] = cb.reshape(24, 128).T
    sm[:, 72:104] = b0a.reshape(32, 128).T
    sm[:, 105:137] = b1a.reshape(32, 128).T
    sm[:, 137:153] = b0b.reshape(16, 128).T
    sm[:, 153:169] = b1b.reshape(16, 128).T
    sm[0:32, 169] = dt_bias[SLOT2ORIG]
    sm[0:32, 170] = A_log[SLOT2ORIG]
    sm[:, 171] = np.tile(gnorm, 2)
    sm[:, 172] = 1e-5
    sm[:, 173] = -0.5
    sm[:, 174] = 1.0 / 64

    consts = dict(
        c_idb=np.eye(128, dtype=bfdt),
        c_idf=np.eye(128, dtype=np.float32),
        c_mask=np.triu(np.ones((128, 128), np.float32)).astype(bfdt),
        c_onesb=np.ones((128, 128), bfdt),
        c_onesf=np.ones((128, 128), np.float32),
        c_smalls=sm,
    )
    weights = dict(wvkq_f32=wvkq, w0a_f32=w0a, w1a_f32=w1a,
                   w0b_f32=w0b, w1b_f32=w1b, wo_f32=wo)
    return consts, weights


def _build_nc(debug=False):
    nc = bass.Bass("TRN2", target_bir_lowering=False)
    h = {}
    h['x'] = nc.dram_tensor('x', [NCH, 128, 8192], F32, kind="ExternalInput").ap()
    for n, sh in [('wvkq_f32', [41, 128, 2048]), ('w0a_f32', [33, 128, 2048]),
                  ('w1a_f32', [32, 128, 2048]), ('w0b_f32', [16, 128, 4096]),
                  ('w1b_f32', [16, 128, 4096]), ('wo_f32', [16, 128, 2048])]:
        h[n] = nc.dram_tensor(n, sh, F32, kind="ExternalInput").ap()
    for n, sh, dt in [('c_idb', [128, 128], BF16), ('c_idf', [128, 128], F32),
                      ('c_mask', [128, 128], BF16), ('c_onesb', [128, 128], BF16),
                      ('c_onesf', [128, 128], F32), ('c_smalls', [128, 176], F32)]:
        h[n] = nc.dram_tensor(n, sh, dt, kind="ExternalInput").ap()
    h['yout'] = nc.dram_tensor('yout', [NCH, 128, 8192], F32, kind="ExternalOutput").ap()
    dbg_kind = "ExternalOutput" if debug else "Internal"
    for n, sh, dt in [('wvkq_bf', [41, 128, 2048], BF16), ('w0a_bf', [33, 128, 2048], BF16),
                      ('w1a_bf', [32, 128, 2048], BF16), ('w0b_bf', [16, 128, 4096], BF16),
                      ('w1b_bf', [16, 128, 4096], BF16), ('wo_bf', [16, 128, 2048], BF16)]:
        h[n] = nc.dram_tensor(n, sh, dt, kind="Internal").ap()
    for n, sh, dt in [('x1', [NCH, 128, 8192], F32), ('x2', [NCH, 128, 8192], F32),
                      ('yg', [NCH, 128, 8192], BF16)]:
        h[n] = nc.dram_tensor(n, sh, dt, kind=dbg_kind).ap()

    import contextlib
    with contextlib.ExitStack() as stk:
        E = stk.enter_context
        sb = {}

        def S_(name, shape, dt):
            sb[name] = E(nc.sbuf_tensor('sb_' + name, list(shape), dt))
            return sb[name]

        for n, sh, dt in [('c_idb', (128, 128), BF16), ('c_idf', (128, 128), F32),
                          ('c_mask', (128, 128), BF16), ('c_onesb', (128, 128), BF16),
                          ('c_onesf', (128, 128), F32), ('c_smalls', (128, 176), F32)]:
            S_(n, sh, dt)
        S_('d_bd', (128, 16), F32)
        S_('acol', (32, 1), F32)
        S_('carry', (128, 24), F32)
        S_('XN', (128, 16384), BF16)
        S_('BIG', (128, 32768), BF16)
        S_('XP0', (128, 512), F32)
        S_('XP1', (128, 512), F32)
        S_('WT0', (128, 4096), BF16)
        S_('WT1', (128, 4096), BF16)
        S_('WT2', (128, 4096), BF16)
        S_('SCRA', (128, 8192), BF16)
        S_('dtfm', (32, 1024), F32)
        S_('DTa', (32, 512), F32)
        S_('DTb', (32, 512), F32)
        S_('AFM', (32, 128), F32)
        S_('ACSFM', (32, 128), F32)
        S_('DSFM', (32, 128), F32)
        S_('dt_tm', (128, 32), F32)
        S_('acs_tm', (128, 32), F32)
        S_('ds_tm', (128, 32), F32)
        S_('FTM', (128, 32), F32)
        S_('ETM', (128, 32), F32)
        S_('EASB', (128, 32), F32)
        S_('U32', (128, 32), F32)
        S_('T32', (128, 32), F32)
        S_('W32', (128, 32), F32)
        S_('MS', (128, 32), F32)
        S_('ASR', (1, 32), F32)
        S_('KTM', (128, 512), BF16)
        S_('VTM', (128, 512), BF16)
        S_('STT', (128, 512), BF16)
        S_('XV', (128, 2048), BF16)
        S_('XVD', (128, 2048), BF16)
        S_('HS0', (128, 1024), F32)
        S_('HS1', (128, 1024), F32)
        S_('HSBF', (128, 1024), BF16)
        S_('YG0', (128, 2048), BF16)
        S_('YG1', (128, 2048), BF16)
        S_('E0S', (128, 512), BF16)
        S_('FS1', (128, 512), F32)
        S_('XNEW', (128, 512), F32)
        S_('XNEW2', (128, 512), F32)
        S_('Bb1', (128, 512), BF16)
        S_('Bb0', (128, 512), BF16)
        S_('x2b0', (128, 512), BF16)
        S_('x2bA', (128, 512), BF16)
        S_('x2bB', (128, 512), BF16)
        S_('ROWS', (1, 1024), F32)
        S_('ZR', (1, 1024), BF16)
        S_('INVM', (1, 512), BF16)
        S_('B1R', (1, 512), BF16)
        S_('B0R', (1, 512), BF16)
        PB = [E(nc.psum_tensor(f'PB{i}', [128, 512], F32)) for i in range(6)]
        PBT = E(nc.psum_tensor('PBT', [128, 1024], BF16))
        PB7 = E(nc.psum_tensor('PB7', [128, 512], F32))
        PR = [f'PS{i}' for i in range(6)]
        PRT, PR7 = 'PST', 'PS7'

        bb = BB(nc)
        sm = sb['c_smalls']

        for n in ('c_idb', 'c_idf', 'c_mask', 'c_onesb', 'c_onesf', 'c_smalls'):
            bb.op('Q', lambda q, n=n: q.dma_start(sb[n][:], h[n][:]),
                  writes=[n], dma=True)
        bb.op('V', lambda v: v.tensor_sub(sb['d_bd'][:], sm[:, 153:169], sm[:, 137:153]),
              reads=['c_smalls'], writes=['d_bd'])
        bb.op('S', lambda s: s.activation(sb['acol'][:], sm[0:32, 170:171], AF.Exp),
              reads=['c_smalls'], writes=['acol'])
        bb.op('V', lambda v: v.tensor_scalar_mul(sb['acol'][:], sb['acol'][:], -1.0),
              reads=['acol'], writes=['acol'])
        bb.op('V', lambda v: v.memset(sb['carry'][:], 0.0), writes=['carry'])
        bb.op('V', lambda v: v.memset(sb['HS0'][:], 0.0), writes=['HS0'])
        bb.op('V', lambda v: v.memset(sb['HSBF'][:], 0.0), writes=['HSBF'])

        def dmacast(n, nt):
            for j in range(nt):
                bb.op('P', lambda g, n=n, j=j: g.dma_start(h[n + '_bf'][j], h[n + '_f32'][j]),
                      writes=[n + '_bf'], dma=True)
        dmacast('w0a', 33)
        dmacast('w0b', 16)
        dmacast('w1a', 32)
        dmacast('w1b', 16)
        late_casts = [('wvkq', j) for j in range(41)] + [('wo', j) for j in range(16)]

        def emit_late_casts(k):
            while late_casts and k > 0:
                n, j = late_casts.pop(0)
                bb.op('P', lambda g, n=n, j=j: g.dma_start(h[n + '_bf'][j], h[n + '_f32'][j]),
                      writes=[n + '_bf'], dma=True)
                k -= 1

        def emit_ln(i2, src_name, xn_off_fn):
            for s in range(16):
                xp = sb['XP0'] if s % 2 == 0 else sb['XP1']
                xpr = 'XP0' if s % 2 == 0 else 'XP1'
                bb.op('Q', lambda q, s=s, xp=xp: q.dma_start(
                    xp[:], h[src_name][i2, :, s * 512:(s + 1) * 512]),
                    reads=[f'{src_name}.{i2}'], writes=[xpr], dma=True)
                bb.op('V', lambda v, s=s, xp=xp: v.tensor_copy(
                    sb['SCRA'][:, s * 512:(s + 1) * 512], xp[:]),
                    reads=[xpr], writes=['SCRA'])
                bb.op('T', lambda t, s=s: t.matmul(
                    PB7[0:1, 0:512], sb['c_onesb'][:, 0:1],
                    sb['SCRA'][:, s * 512:(s + 1) * 512],
                    start=(s == 0), stop=(s == 15)),
                    reads=['SCRA', 'c_onesb'], writes=[PR7])
            x2bufs = [('x2b0', sb['x2b0']), ('E0S', sb['E0S']),
                      ('x2bA', sb['x2bA']), ('x2bB', sb['x2bB'])]
            for s in range(16):
                x2r, x2b = x2bufs[s % 4]
                bb.op('V', lambda v, s=s, x2b=x2b: v.tensor_mul(
                    x2b[:], sb['SCRA'][:, s * 512:(s + 1) * 512],
                    sb['SCRA'][:, s * 512:(s + 1) * 512]),
                    reads=['SCRA'], writes=[x2r])
                bb.op('T', lambda t, s=s, x2b=x2b: t.matmul(
                    PB[5][0:1, 0:512], sb['c_onesb'][:, 0:1], x2b[:],
                    start=(s == 0), stop=(s == 15)),
                    reads=[x2r, 'c_onesb'], writes=[PR[5]])
            R = sb['ROWS']
            bb.op('V', lambda v: v.tensor_scalar_mul(R[0:1, 0:512], PB7[0:1, 0:512], 1.0 / D),
                  reads=[PR7], writes=['ROWS'])
            bb.op('S', lambda s: s.activation(R[0:1, 512:1024], R[0:1, 0:512], AF.Square),
                  reads=['ROWS'], writes=['ROWS'])
            bb.op('V', lambda v: v.scalar_tensor_tensor(
                PB7[32:33, 0:512], PB[5][0:1, 0:512], 1.0 / D, R[0:1, 512:1024],
                ALU.mult, ALU.subtract),
                reads=[PR[5], 'ROWS'], writes=[PR7])
            bb.op('S', lambda s: s.activation(PB[5][32:33, 0:512], PB7[32:33, 0:512], AF.Ln,
                                              bias=sm[0:1, 172:173]),
                  reads=[PR7, 'c_smalls'], writes=[PR[5]])
            bb.op('S', lambda s: s.activation(PB7[96:97, 0:512], PB[5][32:33, 0:512], AF.Exp,
                                              scale=sm[0:1, 173:174]),
                  reads=[PR[5], 'c_smalls'], writes=[PR7])
            bb.op('V', lambda v: v.tensor_copy(sb['B1R'][:], PB7[96:97, 0:512]),
                  reads=[PR7], writes=['B1R'])
            bb.op('V', lambda v: v.scalar_tensor_tensor(
                sb['B0R'][:], R[0:1, 0:512], -1.0, PB7[96:97, 0:512], ALU.mult, ALU.mult),
                reads=['ROWS', PR7], writes=['B0R'])
            bb.op('T', lambda t: t.matmul(PB[5][:, 0:512], sb['c_onesb'][0:1, :], sb['B1R'][:],
                                          start=True, stop=True),
                  reads=['B1R', 'c_onesb'], writes=[PR[5]])
            bb.op('S', lambda s: s.copy(sb['Bb1'][:], PB[5][:, 0:512]),
                  reads=[PR[5]], writes=['Bb1'])
            bb.op('T', lambda t: t.matmul(PB[5][:, 0:512], sb['c_onesb'][0:1, :], sb['B0R'][:],
                                          start=True, stop=True),
                  reads=['B0R', 'c_onesb'], writes=[PR[5]])
            bb.op('S', lambda s: s.copy(sb['Bb0'][:], PB[5][:, 0:512]),
                  reads=[PR[5]], writes=['Bb0'])
            bv1 = bass.AP(sb['Bb1'], 0, [[512, 128], [0, 16], [1, 512]])
            bv0 = bass.AP(sb['Bb0'], 0, [[512, 128], [0, 16], [1, 512]])
            bb.op('V', lambda v: v.tensor_mul(sb['SCRA'][:], sb['SCRA'][:], bv1),
                  reads=['SCRA', 'Bb1'], writes=['SCRA'])
            offs = [xn_off_fn(s) for s in range(16)]
            stride = offs[1] - offs[0]
            out_ap = bass.AP(sb['XN'], offs[0], [[16384, 128], [stride, 16], [1, 512]])
            bb.op('V', lambda v: v.tensor_add(out_ap, sb['SCRA'][:], bv0),
                  reads=['SCRA', 'Bb0'], writes=['XN'])

        def stream_mm_phase(wname, ntiles, K, rhs_fn, out_fn, nchunks=1,
                            rhs_region='XN', after_tile=None):
            ks = K // 128
            WTS = ('WT0', 'WT1', 'WT2')
            for j in range(min(3, ntiles)):
                wt = WTS[j % 3]
                bb.op('Q', lambda q, j=j, wt=wt: q.dma_start(
                    sb[wt][:, 0:K], h[wname][j]),
                    reads=[wname], writes=[wt], dma=True)
            for j in range(ntiles):
                wt = WTS[j % 3]
                for ch in range(nchunks):
                    bk = (j * nchunks + ch) % 2
                    for s in range(ks):
                        bb.op('T', lambda t, j=j, ch=ch, s=s, wt=wt, bk=bk: t.matmul(
                            PB[bk][:, 0:512],
                            sb[wt][:, s * 128:(s + 1) * 128],
                            rhs_fn(s, ch),
                            start=(s == 0), stop=(s == ks - 1)),
                            reads=[wt, rhs_region],
                            writes=[PR[bk]])
                    out_fn(j, ch, bk)
                if j + 3 < ntiles:
                    wt2 = WTS[j % 3]
                    bb.op('Q', lambda q, j=j, wt2=wt2: q.dma_start(
                        sb[wt2][:, 0:K], h[wname][j + 3]),
                        reads=[wname], writes=[wt2], dma=True)
                if after_tile is not None:
                    after_tile(j)

        def ffn_phase(src_name, dst_name, is_out):
            for ia in range(NTA):
                emit_ln(ia, src_name, lambda s: s * 512)
                def h0_out(j, ch, bk):
                    if j < 32:
                        bb.op('S', lambda s, j=j, bk=bk: s.activation(
                            sb['BIG'][:, j * 512:(j + 1) * 512], PB[bk][:, 0:512],
                            AF.Gelu, bias=sm[:, 72 + j:73 + j]),
                            reads=[PR[bk], 'c_smalls'], writes=['BIG'])
                    else:
                        bb.op('S', lambda s, bk=bk: s.copy(sb['ZR'][0:1, 0:512], PB[bk][0:1, 0:512]),
                              reads=[PR[bk]], writes=['ZR'])
                        bb.op('S', lambda s, bk=bk: s.copy(sb['ZR'][0:1, 512:1024], PB[bk][64:65, 0:512]),
                              reads=[PR[bk]], writes=['ZR'])
                        bb.op('V', lambda v: v.tensor_sub(PB7[32:33, 0:512], sb['ZR'][0:1, 0:512],
                                                          sb['ZR'][0:1, 512:1024]),
                              reads=['ZR'], writes=[PR7])
                        bb.op('V', lambda v: v.tensor_single_scalar(sb['INVM'][:], PB7[32:33, 0:512],
                                                                    0.0, ALU.is_lt),
                              reads=[PR7], writes=['INVM'])
                        bb.op('T', lambda t: t.matmul(PB7[:, 0:512], sb['c_onesb'][0:1, :],
                                                      sb['INVM'][:], start=True, stop=True),
                              reads=['INVM', 'c_onesb'], writes=[PR7])
                stream_mm_phase('w0a_bf', 33, 2048,
                                lambda s, ch: sb['XN'][:, s * 512:(s + 1) * 512], h0_out)
                def out0_out(j, ch, bk):
                    bb.op('S', lambda s, j=j, bk=bk: s.activation(
                        sb['SCRA'][:, j * 512:(j + 1) * 512], PB[bk][:, 0:512],
                        AF.Identity, bias=sm[:, 137 + j:138 + j]),
                        reads=[PR[bk], 'c_smalls'], writes=['SCRA'])
                stream_mm_phase('w0b_bf', 16, 4096,
                                lambda s, ch: sb['BIG'][:, s * 512:(s + 1) * 512], out0_out,
                                rhs_region='BIG')
                def h1_out(j, ch, bk):
                    bb.op('S', lambda s, j=j, bk=bk: s.activation(
                        sb['BIG'][:, (32 + j) * 512:(33 + j) * 512], PB[bk][:, 0:512],
                        AF.Relu, bias=sm[:, 105 + j:106 + j]),
                        reads=[PR[bk], 'c_smalls'], writes=['BIG'])
                stream_mm_phase('w1a_bf', 32, 2048,
                                lambda s, ch: sb['XN'][:, s * 512:(s + 1) * 512], h1_out)
                store_ops = []

                def out1_out(j, ch, bk):
                    e0 = sb['SCRA'][:, j * 512:(j + 1) * 512]
                    bb.op('V', lambda v, j=j, bk=bk, e0=e0: v.scalar_tensor_tensor(
                        sb['FS1'][:], PB[bk][:, 0:512], sb['d_bd'][:, j:j + 1], e0,
                        ALU.add, ALU.subtract),
                        reads=[PR[bk], 'd_bd', 'SCRA'], writes=['FS1'])
                    bb.op('V', lambda v: v.tensor_mul(sb['FS1'][:], sb['FS1'][:], PB7[:, 0:512]),
                          reads=['FS1', PR7], writes=['FS1'])
                    bb.op('P', lambda p, e0=e0: p.tensor_add(sb['FS1'][:], sb['FS1'][:], e0),
                          reads=['FS1', 'SCRA'], writes=['FS1'])
                    xp = sb['XP0'] if j % 2 == 0 else sb['XP1']
                    xpr = 'XP0' if j % 2 == 0 else 'XP1'
                    bb.op('Q', lambda q, j=j, xp=xp, ia=ia: q.dma_start(
                        xp[:], h[src_name][ia, :, j * 512:(j + 1) * 512]),
                        reads=[f'{src_name}.{ia}'], writes=[xpr], dma=True)
                    xn2 = sb['XNEW'] if j % 2 == 0 else sb['XNEW2']
                    xr = 'XNEW' if j % 2 == 0 else 'XNEW2'
                    bb.op('V', lambda v, xn2=xn2, xp=xp: v.scalar_tensor_tensor(
                        xn2[:], sb['FS1'][:], 0.5, xp[:], ALU.mult, ALU.add),
                        reads=['FS1', xpr], writes=[xr])
                    store_ops.append(bb.op('S', lambda s, j=j, xn2=xn2, ia=ia: s.dma_start(
                        h[dst_name][ia, :, j * 512:(j + 1) * 512], xn2[:]),
                        reads=[xr], writes=[f'{dst_name}.{ia}'], dma=True))
                stream_mm_phase('w1b_bf', 16, 4096,
                                lambda s, ch: sb['BIG'][:, (32 + s) * 512:(33 + s) * 512], out1_out,
                                rhs_region='BIG')
                if dst_name == 'x1':
                    emit_late_casts(8)
            return store_ops

        def b1_phase():
            gchunk = [0]
            for tb in range(NTB):
                for half in range(2):
                    emit_ln(tb * 2 + half, 'x1', lambda s, half=half: s * 1024 + half * 512)
                def proj_out(j, ch, bk):
                    col = j * 1024 + ch * 512
                    if j < 24:
                        ct = sb['E0S']
                        bb.op('S', lambda s, j=j, bk=bk, ct=ct: s.activation(
                            ct[:], PB[bk][:, 0:512], AF.Identity,
                            bias=sm[:, 48 + j:49 + j], scale=sm[:, 24 + j:25 + j]),
                            reads=[PR[bk], 'c_smalls'], writes=['E0S'])
                        bb.op('V', lambda v, j=j, bk=bk, ct=ct: v.scalar_tensor_tensor(
                            ct[:, 1:512], PB[bk][:, 0:511], sm[:, j:j + 1], ct[:, 1:512],
                            ALU.mult, ALU.add),
                            reads=[PR[bk], 'c_smalls', 'E0S'], writes=['E0S'])
                        bb.op('V', lambda v, j=j, ct=ct: v.scalar_tensor_tensor(
                            ct[:, 0:1], sb['carry'][:, j:j + 1], sm[:, j:j + 1], ct[:, 0:1],
                            ALU.mult, ALU.add),
                            reads=['carry', 'c_smalls', 'E0S'], writes=['E0S'])
                        bb.op('S', lambda s, j=j, bk=bk: s.copy(
                            sb['carry'][:, j:j + 1], PB[bk][:, 511:512]),
                            reads=[PR[bk]], writes=['carry'])
                        bb.op('S', lambda s, j=j, ch=ch, ct=ct, col=col: s.activation(
                            sb['BIG'][:, col:col + 512], ct[:], AF.Silu),
                            reads=['E0S'], writes=[f'BG{ch}'])
                    elif j < 40:
                        gi = j - 24
                        if gi < 8:
                            dst = sb['BIG'][:, 24576 + gi * 1024 + ch * 512: 24576 + gi * 1024 + ch * 512 + 512]
                            dr = f'BG{ch}'
                        else:
                            dst = sb['SCRA'][:, (gi - 8) * 1024 + ch * 512: (gi - 8) * 1024 + ch * 512 + 512]
                            dr = 'SCRA'
                        bb.op('S', lambda s, bk=bk, dst=dst: s.activation(
                            dst, PB[bk][:, 0:512], AF.Silu),
                            reads=[PR[bk]], writes=[dr])
                    else:
                        bia = sm[0:32, 169:170]
                        bb.op('S', lambda s, bk=bk: s.activation(
                            sb['DTa'][:], PB[bk][0:32, 0:512], AF.Abs, bias=bia),
                            reads=[PR[bk], 'c_smalls'], writes=['DTa'])
                        bb.op('S', lambda s, bk=bk: s.activation(
                            sb['DTb'][:], PB[bk][0:32, 0:512], AF.Relu, bias=bia),
                            reads=[PR[bk], 'c_smalls'], writes=['DTb'])
                        bb.op('S', lambda s: s.activation(
                            sb['DTa'][:], sb['DTa'][:], AF.Exp, scale=-1.0),
                            reads=['DTa'], writes=['DTa'])
                        bb.op('S', lambda s: s.activation(
                            sb['DTa'][:], sb['DTa'][:], AF.Ln, bias=1.0),
                            reads=['DTa'], writes=['DTa'])
                        bb.op('V', lambda v, ch=ch: v.tensor_add(
                            sb['dtfm'][:, ch * 512:(ch + 1) * 512], sb['DTa'][:], sb['DTb'][:]),
                            reads=['DTa', 'DTb'], writes=[f'DT{ch}'])
                stream_mm_phase('wvkq_bf', 41, 2048,
                                lambda s, ch: sb['XN'][:, s * 1024: s * 1024 + 512],
                                lambda j, ch, bk: proj_out(j, 0, bk),
                                nchunks=1)

                def inter(j, tb=tb):
                    if j in (10, 20, 30, 40):
                        cc = j // 10 - 1
                        _ssd_chunk(tb, cc, gchunk[0])
                        gchunk[0] += 1

                stream_mm_phase('wvkq_bf', 41, 2048,
                                lambda s, ch: sb['XN'][:, s * 1024 + 512: s * 1024 + 1024],
                                lambda j, ch, bk: proj_out(j, 1, bk),
                                nchunks=1, after_tile=inter)
                for cc in range(4, 8):
                    _ssd_chunk(tb, cc, gchunk[0])
                    gchunk[0] += 1

        def _ssd_chunk(tb, cc, gc):
            col0 = cc * 128
            BGR = f'BG{cc // 4}'
            DTR = f'DT{cc // 4}'
            hs_old = sb['HS0'] if gc % 2 == 0 else sb['HS1']
            hs_new = sb['HS1'] if gc % 2 == 0 else sb['HS0']
            hro = 'HS0' if gc % 2 == 0 else 'HS1'
            hrn = 'HS1' if gc % 2 == 0 else 'HS0'
            bb.op('T', lambda t: t.transpose(PB7[:, 0:32], sb['dtfm'][0:32, col0:col0 + 128],
                                             sb['c_idf'][0:32, 0:32]),
                  reads=[DTR, 'c_idf'], writes=[PR7])
            bb.op('S', lambda s: s.copy(sb['dt_tm'][:], PB7[:, 0:32]),
                  reads=[PR7], writes=['dt_tm'])
            bb.op('V', lambda v: v.tensor_scalar_mul(sb['AFM'][:], sb['dtfm'][0:32, col0:col0 + 128],
                                                     sb['acol'][:]),
                  reads=[DTR, 'acol'], writes=['AFM'])
            bb.op('V', lambda v: v.tensor_tensor_scan(sb['ACSFM'][:], sb['AFM'][:], sb['AFM'][:],
                                                      0.0, ALU.add, ALU.bypass),
                  reads=['AFM'], writes=['ACSFM'])
            bb.op('S', lambda s: s.activation(sb['DSFM'][:], sb['ACSFM'][:], AF.Exp,
                                              scale=-1.0, bias=sb['ACSFM'][:, 127:128]),
                  reads=['ACSFM'], writes=['DSFM'])
            bb.op('T', lambda t: t.transpose(PB7[:, 32:64], sb['ACSFM'][:], sb['c_idf'][0:32, 0:32]),
                  reads=['ACSFM', 'c_idf'], writes=[PR7])
            bb.op('S', lambda s: s.copy(sb['acs_tm'][:], PB7[:, 32:64]),
                  reads=[PR7], writes=['acs_tm'])
            bb.op('T', lambda t: t.transpose(PB7[:, 64:96], sb['DSFM'][:], sb['c_idf'][0:32, 0:32]),
                  reads=['DSFM', 'c_idf'], writes=[PR7])
            bb.op('S', lambda s: s.copy(sb['ds_tm'][:], PB7[:, 64:96]),
                  reads=[PR7], writes=['ds_tm'])
            bb.op('T', lambda t: t.transpose(PB7[0:1, 96:128], sb['ACSFM'][:, 127:128],
                                             sb['c_idf'][0:32, 0:32]),
                  reads=['ACSFM', 'c_idf'], writes=[PR7])
            bb.op('S', lambda s: s.activation(sb['ASR'][:], PB7[0:1, 96:128], AF.Exp),
                  reads=[PR7], writes=['ASR'])
            bb.op('S', lambda s: s.activation(sb['FTM'][:], sb['acs_tm'][:], AF.Exp, scale=-1.0),
                  reads=['acs_tm'], writes=['FTM'])
            bb.op('S', lambda s: s.activation(sb['ETM'][:], sb['acs_tm'][:], AF.Exp),
                  reads=['acs_tm'], writes=['ETM'])
            bb.op('S', lambda s: s.activation(sb['U32'][:], sb['ETM'][:], AF.Square),
                  reads=['ETM'], writes=['U32'])
            bb.op('T', lambda t: t.matmul(PB7[:, 128:160], sb['c_onesf'][0:1, :], sb['ASR'][:],
                                          start=True, stop=True),
                  reads=['ASR', 'c_onesf'], writes=[PR7])
            bb.op('S', lambda s: s.activation(sb['EASB'][:], PB7[:, 128:160], AF.Identity),
                  reads=[PR7], writes=['EASB'])
            for vt in range(4):
                bb.op('T', lambda t, vt=vt: t.transpose(
                    PBT[:, vt * 128:(vt + 1) * 128],
                    sb['BIG'][:, vt * 1024 + col0: vt * 1024 + col0 + 128], sb['c_idb'][:]),
                    reads=[BGR, 'c_idb'], writes=[PRT])
            for kt in range(4):
                bb.op('T', lambda t, kt=kt: t.transpose(
                    PBT[:, 512 + kt * 128: 512 + (kt + 1) * 128],
                    sb['BIG'][:, (4 + kt) * 1024 + col0: (4 + kt) * 1024 + col0 + 128], sb['c_idb'][:]),
                    reads=[BGR, 'c_idb'], writes=[PRT])
            bb.op('S', lambda s: s.copy(sb['VTM'][:], PBT[:, 0:512]),
                  reads=[PRT], writes=['VTM'])
            bb.op('S', lambda s: s.copy(sb['KTM'][:], PBT[:, 512:1024]),
                  reads=[PRT], writes=['KTM'])
            for aa in range(4):
                for ccp in range(2):
                    xv_o = bass.AP(sb['XV'], (8 * aa + ccp) * 64,
                                   [[2048, 128], [128, 4], [1, 64]])
                    v_rep = bass.AP(sb['VTM'], (2 * aa + ccp) * 64,
                                    [[512, 128], [0, 4], [1, 64]])
                    dtB = bass.AP(sb['dt_tm'], 8 * aa + ccp,
                                  [[32, 128], [2, 4], [0, 64]])
                    bb.op('V', lambda v, xv_o=xv_o, v_rep=v_rep, dtB=dtB: v.tensor_mul(
                        xv_o, v_rep, dtB),
                        reads=['VTM', 'dt_tm'], writes=[f'XV{2 * aa}', f'XV{2 * aa + 1}'])
            for aa in range(4):
                for ccp in range(2):
                    xvd_o = bass.AP(sb['XVD'], (2 * aa + ccp) * 256,
                                    [[2048, 128], [64, 4], [1, 64]])
                    xv_v = bass.AP(sb['XV'], (8 * aa + ccp) * 64,
                                   [[2048, 128], [128, 4], [1, 64]])
                    dsB = bass.AP(sb['ds_tm'], 8 * aa + ccp,
                                  [[32, 128], [2, 4], [0, 64]])
                    bb.op('V', lambda v, xvd_o=xvd_o, xv_v=xv_v, dsB=dsB: v.tensor_mul(
                        xvd_o, xv_v, dsB),
                        reads=[f'XV{2 * aa}', f'XV{2 * aa + 1}', 'ds_tm'], writes=['XVD'])

            ygb = sb['YG0'] if gc % 2 == 0 else sb['YG1']
            ygr = 'YG0' if gc % 2 == 0 else 'YG1'

            def emit_st(hh):
                par = hh % 2
                g = G_OF[hh]
                ktile, kbase = 4 + g // 2, 64 * (g % 2)
                qtile, qbase = 8 + hh // 2, 64 * (hh % 2)
                sslot = (hh // 2) % 4
                stslot = hh % 4
                if JJ_OF[hh] == 0:
                    qoff = qbase * 32768 + qtile * 1024 + col0
                    rhs3 = bass.AP(sb['BIG'], qoff, [[32768, 64], [1024, 4], [1, 128]])
                    bb.op('T', lambda t, ktile=ktile, kbase=kbase, par=par,
                          rhs3=rhs3: t.matmul(
                              PB[par][:, 0:512],
                              sb['BIG'][kbase:kbase + 64,
                                        ktile * 1024 + col0: ktile * 1024 + col0 + 128],
                              rhs3, start=True, stop=True),
                          reads=[BGR], writes=[PR[par]])
                bb.op('V', lambda v, hh=hh, par=par, sslot=sslot, stslot=stslot:
                      v.scalar_tensor_tensor(
                          sb['STT'][:, stslot * 128:(stslot + 1) * 128],
                          PB[par][:, sslot * 128:(sslot + 1) * 128],
                          sb['FTM'][:, hh:hh + 1], sb['c_mask'][:], ALU.mult, ALU.mult),
                      reads=[PR[par], 'FTM', 'c_mask'], writes=[f'ST{stslot}'])

            def emit_ydyo(hh):
                g = G_OF[hh]
                qtile, qbase = 8 + hh // 2, 64 * (hh % 2)
                stslot = hh % 4
                ybank = 2 + hh // 8
                ypos = (hh % 8) * 64
                hblk = (g // 2) * 256 + JJ_OF[hh] * 64
                bb.op('T', lambda t, hh=hh, stslot=stslot, ybank=ybank, ypos=ypos: t.matmul(
                    PB[ybank][:, ypos:ypos + 64],
                    sb['STT'][:, stslot * 128:(stslot + 1) * 128],
                    sb['XV'][:, hh * 64:(hh + 1) * 64], start=True, stop=False),
                    reads=[f'ST{stslot}', f'XV{hh // 8}'], writes=[PR[ybank]])
                bb.op('T', lambda t, qtile=qtile, qbase=qbase, ybank=ybank,
                      ypos=ypos, hblk=hblk: t.matmul(
                          PB[ybank][:, ypos:ypos + 64],
                          sb['BIG'][qbase:qbase + 64, qtile * 1024 + col0: qtile * 1024 + col0 + 128],
                          sb['HSBF'][qbase:qbase + 64, hblk:hblk + 64],
                          start=False, stop=True),
                      reads=[BGR, 'HSBF'], writes=[PR[ybank]])

            def emit_tail_rows(b):
                y2 = sb['FS1'] if b % 2 == 0 else sb['XNEW']
                y2r = 'FS1' if b % 2 == 0 else 'XNEW'
                c0, c1 = b * 8, (b + 1) * 8
                bb.op('S', lambda s, b=b, y2=y2: s.activation(
                    y2[:], PB[2 + b][:, 0:512], AF.Square),
                    reads=[PR[2 + b]], writes=[y2r])
                seg = bass.AP(y2, 0, [[512, 128], [64, 8], [1, 64]])
                bb.op('V', lambda v, b=b, seg=seg: v.reduce_sum(
                    sb['MS'][:, b * 8:(b + 1) * 8], seg, axis=AX.X),
                    reads=[y2r], writes=['MS'])
                bb.op('V', lambda v, c0=c0, c1=c1: v.tensor_mul(
                    sb['T32'][:, c0:c1], sb['U32'][:, c0:c1], sb['MS'][:, c0:c1]),
                    reads=['U32', 'MS'], writes=['T32'])
                bb.op('S', lambda s, c0=c0, c1=c1: s.activation(
                    sb['W32'][:, c0:c1], sb['T32'][:, c0:c1], AF.Ln,
                    scale=sm[:, 174:175], bias=sm[:, 172:173]),
                    reads=['T32', 'c_smalls'], writes=['W32'])
                bb.op('S', lambda s, c0=c0, c1=c1: s.activation(
                    sb['T32'][:, c0:c1], sb['W32'][:, c0:c1], AF.Exp,
                    scale=sm[:, 173:174]),
                    reads=['W32', 'c_smalls'], writes=['T32'])
                bb.op('V', lambda v, c0=c0, c1=c1: v.tensor_mul(
                    sb['W32'][:, c0:c1], sb['T32'][:, c0:c1], sb['ETM'][:, c0:c1]),
                    reads=['T32', 'ETM'], writes=['W32'])
                wv = bass.AP(sb['W32'], b * 8, [[32, 128], [1, 8], [0, 64]])
                bb.op('V', lambda v, b=b, wv=wv: v.tensor_mul(
                    sb['XV'][:, b * 512:(b + 1) * 512], PB[2 + b][:, 0:512], wv),
                    reads=[PR[2 + b], 'W32'], writes=[f'XV{b}'])

            def emit_tail_tr(b):
                for i in range(4):
                    j = b * 4 + i
                    pcol = (b % 2) * 512 + i * 128
                    bb.op('T', lambda t, j=j, pcol=pcol: t.transpose(
                        PBT[:, pcol:pcol + 128],
                        sb['XV'][:, j * 128:(j + 1) * 128], sb['c_idb'][:]),
                        reads=[f'XV{b}', 'c_idb'], writes=[PRT])
                for i in range(4):
                    j = b * 4 + i
                    pcol = (b % 2) * 512 + i * 128
                    if j < 8:
                        sg = sb['BIG'][:, 24576 + j * 1024 + col0: 24576 + j * 1024 + col0 + 128]
                        sgr = BGR
                    else:
                        sg = sb['SCRA'][:, (j - 8) * 1024 + col0: (j - 8) * 1024 + col0 + 128]
                        sgr = 'SCRA'
                    bb.op('V', lambda v, j=j, pcol=pcol, sg=sg, ygb=ygb: v.scalar_tensor_tensor(
                        ygb[:, j * 128:(j + 1) * 128], PBT[:, pcol:pcol + 128],
                        sm[:, 171:172], sg, ALU.mult, ALU.mult),
                        reads=[PRT, sgr, 'c_smalls'], writes=[ygr])

            for hh in range(32):
                emit_st(hh)
                if hh >= 2:
                    emit_ydyo(hh - 2)
                    if (hh - 2) % 8 == 7:
                        emit_tail_rows((hh - 2) // 8)
                    if hh >= 6 and (hh - 6) % 8 == 7:
                        emit_tail_tr((hh - 6) // 8)
            emit_ydyo(30)
            emit_ydyo(31)
            emit_tail_rows(3)
            emit_tail_tr(2)
            emit_tail_tr(3)

            for g in range(8):
                par = g % 2
                base = 64 * par
                scol = ((g // 2) % 2) * 256
                sbank = par
                bb.op('T', lambda t, g=g, base=base, scol=scol, sbank=sbank: t.matmul(
                    PB[sbank][base:base + 64, scol:scol + 256],
                    sb['KTM'][:, g * 64:(g + 1) * 64],
                    sb['XVD'][:, g * 256:(g + 1) * 256], start=True, stop=True),
                    reads=['KTM', 'XVD'], writes=[PR[sbank]])
                for jj in range(4):
                    hh2 = SLOTS_OF_G[g][jj]
                    hblk = (g // 2) * 256 + jj * 64
                    bb.op('V', lambda v, g=g, jj=jj, hh2=hh2, base=base, scol=scol,
                          sbank=sbank, hblk=hblk: v.scalar_tensor_tensor(
                              hs_new[base:base + 64, hblk:hblk + 64],
                              hs_old[base:base + 64, hblk:hblk + 64],
                              sb['EASB'][base:base + 64, hh2:hh2 + 1],
                              PB[sbank][base:base + 64, scol + jj * 64: scol + (jj + 1) * 64],
                              ALU.mult, ALU.add),
                          reads=[hro, 'EASB', PR[sbank]], writes=[hrn])
            bb.op('S', lambda s: s.copy(sb['HSBF'][:], hs_new[:]),
                  reads=[hrn], writes=['HSBF'])
            i2 = (tb * 8 + cc) // 4
            off = (cc % 4) * 128
            dst = bass.AP(h['yg'].tensor, i2 * 128 * 8192 + off,
                          [[8192, 128], [512, 16], [1, 128]])
            src = bass.AP(ygb, 0, [[2048, 128], [128, 16], [1, 128]])
            bb.op('S', lambda s, dst=dst, src=src: s.dma_start(dst, src),
                  reads=[ygr], writes=[f'yg.{i2}'], dma=True)

        def b2_phase():
            for j in range(16):
                bb.op('Q', lambda q, j=j: q.dma_start(
                    sb['BIG'][:, j * 2048:(j + 1) * 2048], h['wo_bf'][j]),
                    reads=['wo_bf'], writes=['BIG'], dma=True)
            for i2 in range(NCH):
                half = (i2 % 2) * 8192
                xnr = 'XNa' if i2 % 2 == 0 else 'XNb'
                bb.op('Q', lambda q, i2=i2, half=half: q.dma_start(
                    sb['XN'][:, half:half + 8192], h['yg'][i2]),
                    reads=[f'yg.{i2}'], writes=[xnr], dma=True)
                for j in range(16):
                    bk = j % 4
                    xp = sb['XP0'] if j % 2 == 0 else sb['XP1']
                    xpr = 'XP0' if j % 2 == 0 else 'XP1'
                    bb.op('Q', lambda q, i2=i2, j=j, xp=xp: q.dma_start(
                        xp[:], h['x1'][i2, :, j * 512:(j + 1) * 512]),
                        reads=[f'x1.{i2}'], writes=[xpr], dma=True)
                    for s in range(16):
                        bb.op('T', lambda t, j=j, s=s, bk=bk, half=half: t.matmul(
                            PB[bk][:, 0:512],
                            sb['BIG'][:, j * 2048 + s * 128: j * 2048 + (s + 1) * 128],
                            sb['XN'][:, half + s * 512: half + (s + 1) * 512],
                            start=(s == 0), stop=(s == 15)),
                            reads=['BIG', xnr], writes=[PR[bk]])
                    xn2 = sb['XNEW'] if j % 2 == 0 else sb['XNEW2']
                    xr = 'XNEW' if j % 2 == 0 else 'XNEW2'
                    bb.op('V', lambda v, bk=bk, xn2=xn2, xp=xp: v.tensor_add(
                        xn2[:], PB[bk][:, 0:512], xp[:]),
                        reads=[PR[bk], xpr], writes=[xr])
                    bb.op('S', lambda s, i2=i2, j=j, xn2=xn2: s.dma_start(
                        h['x2'][i2, :, j * 512:(j + 1) * 512], xn2[:]),
                        reads=[xr], writes=[f'x2.{i2}'], dma=True)

        def big_barrier():
            bb.op('V', lambda v: v.engine_nop(),
                  reads=['BIG', 'BG0', 'BG1', 'DT0', 'DT1'],
                  writes=['BIG', 'BG0', 'BG1', 'DT0', 'DT1'])

        ffn_phase('x', 'x1', False)
        big_barrier()
        b1_phase()
        big_barrier()
        b2_phase()
        final_stores = ffn_phase('x2', 'yout', True)
        bb.fence('Q', [final_stores[-1]])
        bb.build()
    return nc


_NC = None


def kernel(**inputs):
    global _NC
    consts, weights = _host_prep(inputs)
    if _NC is None:
        _NC = _build_nc()
    x = np.asarray(inputs['L0_attn_hidden_states'])
    in_maps = []
    for b in range(NCORES):
        m = dict(consts)
        m.update(weights)
        m['x'] = _x_chunks(np.ascontiguousarray(x[b]))
        in_maps.append(m)
    res = run_bass_kernel_spmd(_NC, in_maps, core_ids=list(range(NCORES)))
    out = np.empty((NCORES, L, D), np.float32)
    for b in range(NCORES):
        out[b] = _out_unchunks(np.asarray(res.results[b]['yout']))
    return out
